# version 20
# baseline (speedup 1.0000x reference)
import numpy as np
import ml_dtypes
import concourse.bass as bass
import concourse.mybir as mybir
from concourse.bass_utils import run_bass_kernel_spmd

F32 = mybir.dt.float32
BF16 = mybir.dt.bfloat16
AF = mybir.ActivationFunctionType
ALU = mybir.AluOpType

D = 1024
NPT = 8
TL = 2048
TG = 4096
HD = 64
NHL = 8
KVH = 4
QC = 512
NQC = TG // QC
NST = TG // 128
CSHIFT = 15.0
EPS = 1e-5

BF = lambda a: np.asarray(a, np.float32).astype(ml_dtypes.bfloat16).astype(np.float32)


def _chunk_class(st, qc):
    d0 = st * 128 - qc * QC
    if d0 <= -128:
        return "A"
    if d0 >= QC:
        return "B"
    return "D"



def _finish_early(nc, sems, V, sem_cms):
    b = nc.Block()
    blk = b.__enter__()

    @blk.sync
    def _(sy):
        sy.wait_ge(sems["ld"], V["ld"])

    @blk.gpsimd
    def _(g):
        g.wait_ge(sems["lg"], V["lg"])
        g.wait_ge(sems["cc1"], V["cc1"])
        g.wait_ge(sems["cc2"], V["cc2"])

    @blk.vector
    def _(v):
        v.wait_ge(sems["s_dv"], V["s_dv"])

    @blk.scalar
    def _(s):
        s.wait_ge(sems["s_ac"], V["s_ac"])

    @blk.tensor
    def _(t):
        t.wait_ge(sems["s_pe"], V["s_pe"])
    b.__exit__(None, None, None)
    for cm in reversed(sem_cms):
        cm.__exit__(None, None, None)
    return nc


def build_program(nblocks=99, b3_parts=9, b4_parts=9):
    nc = bass.Bass("TRN2", target_bir_lowering=False, detect_race_conditions=False)

    x_in = nc.dram_tensor("x", [D, TL], F32, kind="ExternalInput")
    w_ffn = [nc.dram_tensor(n, [D, D], F32, kind="ExternalInput")
             for n in ("wffn0", "wffn1", "wffn2")]
    wq_in = nc.dram_tensor("wq", [D, NHL * HD], F32, kind="ExternalInput")
    wkv_in = nc.dram_tensor("wkv", [D, 512], F32, kind="ExternalInput")
    wproj_in = nc.dram_tensor("wproj", [D, D], F32, kind="ExternalInput")
    bproj_in = nc.dram_tensor("bproj", [128, NPT], F32, kind="ExternalInput")
    fbias_in = nc.dram_tensor("fbias", [128, NPT], F32, kind="ExternalInput")
    augw_in = nc.dram_tensor("augw", [5, NHL * 128], F32, kind="ExternalInput")
    augq_in = nc.dram_tensor("augq", [5, TG], F32, kind="ExternalInput")
    biasc_in = nc.dram_tensor("biasc", [128, NHL * NST], F32, kind="ExternalInput")
    dtab_in = nc.dram_tensor("dtab", [128, 4 * QC], F32, kind="ExternalInput")
    slop_in = nc.dram_tensor("slop", [128, NHL], F32, kind="ExternalInput")
    onek_in = nc.dram_tensor("onek", [128, 1], F32, kind="ExternalInput")
    ones1_in = nc.dram_tensor("ones1", [1, 128], F32, kind="ExternalInput")
    id64_in = nc.dram_tensor("id64", [128, 128], F32, kind="ExternalInput")
    negc_in = nc.dram_tensor("negc", [128, 1], F32, kind="ExternalInput")
    sel_in = nc.dram_tensor("sel", [128, 2], F32, kind="ExternalInput")

    y_out = nc.dram_tensor("y", [D, TL], F32, kind="ExternalOutput")

    rs_spill = nc.dram_tensor("rs_spill", [128, NPT * TL], F32)
    cin1a = nc.dram_tensor("cin1a", [D, TL // 2], BF16)
    cin1b = nc.dram_tensor("cin1b", [D, TL // 2], BF16)
    cout1a = nc.dram_tensor("cout1a", [2 * D, TL // 2], BF16)
    cout1b = nc.dram_tensor("cout1b", [2 * D, TL // 2], BF16)
    snd2a = nc.dram_tensor("snd2a", [512, TL], BF16)
    snd2b = nc.dram_tensor("snd2b", [512, TL], BF16)
    rcv2a = nc.dram_tensor("rcv2a", [1024, TL], BF16)
    rcv2b = nc.dram_tensor("rcv2b", [1024, TL], BF16)

    SB = {}

    SBUF_BASE = 17 * 1024

    def sb(name, off, shape, dtype):
        off = off + SBUF_BASE
        t = nc.alloc_sbuf_tensor_at(name, list(shape), dtype, offset=off)
        SB[name] = (off, shape, dtype)
        return t

    K1 = 1024
    resid = sb("resid", 0, [128, NPT * TL], F32)
    xlnfull = sb("xlnfull", 0, [128, NPT * TG], BF16)

    wsb = [sb(f"wsb{i}", 64 * K1 + i * 16 * K1, [128, NPT * K1], BF16)
           for i in range(3)]
    q_sb = sb("q_sb", 64 * K1, [128, 4 * TG], BF16)
    kf_sb = sb("kf_sb", 96 * K1, [128, 2 * TG], BF16)
    ao_sb = sb("ao_sb", 64 * K1, [128, NPT * TL], BF16)

    h1c = sb("h1c", 112 * K1, [128, NPT * QC], BF16)
    h2c = sb("h2c", 120 * K1, [128, NPT * QC], BF16)
    xb = sb("xb", 128 * K1, [128, 2 * QC], BF16)
    xsq = sb("xsq", 130 * K1, [128, 2 * QC], BF16)
    VPS = 68
    vp_sb = sb("vp_sb", 112 * K1, [128, KVH * NST * VPS], BF16)
    wpj_sb = sb("wpj_sb", 112 * K1, [128, NPT * K1], BF16)

    wq_sb = sb("wq_sb", 132 * K1, [128, NPT * 512], BF16)
    wkv_sb = sb("wkv_sb", 140 * K1, [128, NPT * 512], BF16)
    xlnc = sb("xlnc", 132 * K1, [128, 2 * NPT * QC], BF16)
    pt_sb = sb("pt_sb", 132 * K1, [128, 4 * 1024], BF16)
    dsc = sb("dsc", 140 * K1, [128, 2 * QC], F32)
    rr_sb = sb("rr_sb", 144 * K1, [1, 2 * QC], F32)
    aost = sb("aost", 150 * K1, [64, 4 * QC], BF16)
    lnl_sb = sb("lnl_sb", 160 * K1, [1, 2 * QC], F32)
    lnt_att = sb("lnt_att", 168 * K1, [1, 2 * QC], F32)
    rbs = sb("rbs", 154 * K1, [64, QC], F32)

    lnstage = sb("lnstage", 148 * K1, [128, NPT * QC], BF16)
    vstage = sb("vstage", 148 * K1, [128, QC], BF16)
    tmpf = sb("tmpf", 156 * K1, [128, 2 * QC], F32)
    mrbs = sb("mrbs", 128 * K1, [128, QC], F32)
    th_sb = sb("th_sb", 148 * K1, [128, QC], BF16)
    tab_off = 185 * K1

    stA = sb("stA", 160 * K1, [1, 4 * QC], F32)
    stB = sb("stB", 168 * K1, [1, 4 * QC], F32)
    stC = sb("stC", 176 * K1, [1, 4 * QC], F32)
    lnstage2 = sb("lnstage2", 132 * K1, [128, NPT * QC], BF16)

    augw_sb = sb("augw_sb", tab_off, [69, NHL * 128], BF16)
    augq_sb = sb("augq_sb", tab_off + 2 * K1, [69, TG], BF16)
    biasc_sb = sb("biasc_sb", tab_off + 10 * K1, [128, NHL * NST], F32)
    dtab_sb = sb("dtab_sb", tab_off + 11 * K1, [128, 4 * QC], F32)
    slop_sb = sb("slop_sb", tab_off + 19 * K1, [128, NHL], F32)
    onek_sb = sb("onek_sb", tab_off + 19 * K1 + 64, [128, 1], BF16)
    ones1_sb = sb("ones1_sb", tab_off + 19 * K1 + 128, [1, 128], F32)
    id64_sb = sb("id64_sb", tab_off + 20 * K1, [128, 128], BF16)
    bproj_sb = sb("bproj_sb", tab_off + 20 * K1 + 256, [128, NPT], F32)
    fbias_sb = sb("fbias_sb", tab_off + 20 * K1 + 512, [128, NPT], F32)
    negc_sb = sb("negc_sb", tab_off + 20 * K1 + 768, [128, 1], F32)
    sel_sb = sb("sel_sb", tab_off + 20 * K1 + 832, [128, 2], F32)
    aoc0 = sb("aoc0", 132 * K1, [128, TL], BF16)
    aoc1 = sb("aoc1", 136 * K1, [128, TL], BF16)
    aot2 = sb("aot2", 140 * K1, [128, TL], BF16)
    assert tab_off + 21 * K1 + 17 * K1 <= 229344 // 1024 * K1

    P01 = nc.alloc_psum_tensor("P01", [128, 2048], F32)
    P0 = P01[:, 0:1024]
    P1 = P01[:, 1024:2048]
    P2 = nc.alloc_psum_tensor("P2", [65, QC], F32)
    P3 = nc.alloc_psum_tensor("P3", [65, QC], F32)
    PT_guard = nc.psum_tensor("PT", [128, 512], BF16)
    PT = PT_guard.__enter__()

    semnames = ["ld", "lg", "cc1", "cc2", "s_pe", "s_ac", "s_dv", "s_gp",
                "s_qk", "s_ex", "s_pv", "s_st", "s_r", "s_ao", "s_lnl", "s_fin"]
    sems = {}
    sem_cms = []
    for n in semnames:
        cm = nc.semaphore(n)
        sems[n] = cm.__enter__()
        sem_cms.append(cm)
    V = {n: 0 for n in semnames}

    def inc(inst, name, n=1):
        inst.then_inc(sems[name], n)
        V[name] += n

    blocks = []

    def open_block():
        b = nc.Block()
        bb = b.__enter__()
        blocks.append(b)
        return bb

    def close_block(b=None):
        blocks.pop().__exit__(None, None, None)

    blk = open_block()

    x_ld = {}

    @blk.sync
    def _(sy):
        for t_in, t_sb in ((onek_in, None), (ones1_in, ones1_sb)):
            pass
        for nch in range(TL // QC):
            sy.dma_start(
                resid[:, :].rearrange("p (t c) -> p t c", c=TL)[:, :, nch * QC:nch * QC + QC],
                x_in.ap().rearrange("(t p) c -> p t c", p=128)[:, :, nch * QC:nch * QC + QC],
            ).then_inc(sems["ld"], 16)
            V["ld"] += 16
            x_ld[nch] = V["ld"]
        for t_in, t_sb in ((bproj_in, bproj_sb), (fbias_in, fbias_sb),
                           (biasc_in, biasc_sb), (dtab_in, dtab_sb),
                           (slop_in, slop_sb), (ones1_in, ones1_sb),
                           (negc_in, negc_sb), (sel_in, sel_sb)):
            sy.dma_start(t_sb[:, :], t_in.ap()).then_inc(sems["ld"], 16)
            V["ld"] += 16

    @blk.gpsimd
    def _(g):
        for i in range(3):
            g.dma_start(wsb[i][:, :].rearrange("p (t c) -> p t c", c=K1),
                        w_ffn[i].ap().rearrange("(t p) c -> p t c", p=128)).then_inc(sems["lg"], 16)
            V["lg"] += 16
        for base in (0, 64):
            g.dma_start(augw_sb[base:base + 5, :], augw_in.ap()).then_inc(sems["lg"], 16)
            V["lg"] += 16
            g.dma_start(augq_sb[base:base + 5, :], augq_in.ap()).then_inc(sems["lg"], 16)
            V["lg"] += 16
        g.dma_start(onek_sb[:, :], onek_in.ap()).then_inc(sems["lg"], 16)
        V["lg"] += 16
        g.dma_start(id64_sb[:, :], id64_in.ap()).then_inc(sems["lg"], 16)
        V["lg"] += 16
    b0_ld = V["ld"]
    b0_lg = V["lg"]

    close_block()
    if nblocks <= 1:
        return _finish_early(nc, sems, V, sem_cms)

    def emit_ln_stats_all(blk):
        for nch in range(TL // QC):
            emit_ln_stats_chunk(blk, nch)
        emit_ln_stats_fin(blk)

    def emit_ln_stats_chunk(blk, nch):
        if True:
            @blk.vector
            def _(v, nch=nch):
                if nch in x_ld:
                    v.wait_ge(sems["ld"], x_ld[nch])
                for pt in range(NPT):
                    sl = xb[:, (pt % 2) * QC:(pt % 2) * QC + QC]
                    sq = xsq[:, (pt % 2) * QC:(pt % 2) * QC + QC]
                    k = nch * NPT + pt
                    if k >= 2:
                        v.wait_ge(sems["s_pe"], V["s_pe"] + (k - 2) * 2 + 2 - nch * 2 * NPT)
                    v.tensor_copy(sl, resid[:, pt * TL + nch * QC:
                                            pt * TL + nch * QC + QC])
                    i1 = v.tensor_mul(sq, sl, sl)
                    inc(i1, "s_dv")

            @blk.tensor
            def _(t, nch=nch):
                if nch == 0:
                    t.wait_ge(sems["lg"], b0_lg)
                for pt in range(NPT):
                    t.wait_ge(sems["s_dv"], V["s_dv"] - NPT + pt + 1)
                    if pt == 0 and nch > 0:
                        t.wait_ge(sems["s_dv"], V["s_dv"] - NPT)
                    sl = xb[:, (pt % 2) * QC:(pt % 2) * QC + QC]
                    sq = xsq[:, (pt % 2) * QC:(pt % 2) * QC + QC]
                    i1 = t.matmul(P2[0:1, :], onek_sb[:, :], sl,
                                  start=(pt == 0), stop=(pt == NPT - 1))
                    inc(i1, "s_pe")
                    i2 = t.matmul(P3[0:1, :], onek_sb[:, :], sq,
                                  start=(pt == 0), stop=(pt == NPT - 1))
                    inc(i2, "s_pe")

            @blk.vector
            def _(v, nch=nch):
                v.wait_ge(sems["s_pe"], V["s_pe"])
                c = nch * QC
                v.tensor_copy(stA[:, c:c + QC], P2[0:1, :])
                v.tensor_copy(stB[:, c:c + QC], P3[0:1, :])
                v.tensor_mul(stC[:, c:c + QC], stA[:, c:c + QC], stA[:, c:c + QC])
                v.scalar_tensor_tensor(stB[:, c:c + QC], stB[:, c:c + QC], 1.0,
                                       stC[:, c:c + QC], ALU.mult, ALU.subtract)
                i1 = v.tensor_scalar_add(stB[:, c:c + QC], stB[:, c:c + QC], EPS)
                inc(i1, "s_dv")

    def emit_ln_stats_fin(blk):
        @blk.scalar
        def _(s):
            s.wait_ge(sems["s_dv"], V["s_dv"])
            s.activation(stC[:, :], stB[:, :], AF.Ln, bias=0.0, scale=1.0)
            i1 = s.activation(stB[:, :], stC[:, :], AF.Exp, bias=0.0, scale=-0.5)
            inc(i1, "s_ac")

        @blk.vector
        def _(v):
            v.wait_ge(sems["s_ac"], V["s_ac"])
            i1 = v.tensor_mul(stC[:, :], stA[:, :], stB[:, :])
            inc(i1, "s_dv")

    def emit_ln_bcast(blk, nch, extra_dv_wait=0):
        @blk.tensor
        def _(t):
            t.wait_ge(sems["s_dv"], V["s_dv"] + extra_dv_wait)
            c = nch * QC
            t.matmul(P1[:, 0:QC], ones1_sb[0:1, :], stB[:, c:c + QC],
                     start=True, stop=True)
            i1 = t.matmul(P1[:, QC:2 * QC], ones1_sb[0:1, :], stC[:, c:c + QC],
                          start=True, stop=True)
            inc(i1, "s_pe")

    def emit_ln_apply(blk, nch, dst, dst_col, ld_wait=None):
        pe_target = V["s_pe"]
        gp_base = V["s_gp"]

        @blk.vector
        def _(v):
            v.wait_ge(sems["s_pe"], pe_target)
            v.tensor_copy(mrbs[:, :], P1[:, QC:2 * QC])
            for pt in range(NPT):
                if pt >= 2:
                    v.wait_ge(sems["s_gp"], gp_base + pt - 1)
                src = resid[:, pt * TL + nch * QC:pt * TL + nch * QC + QC]
                i1 = v.tensor_mul(tmpf[:, (pt % 2) * QC:(pt % 2) * QC + QC],
                                  src, P1[:, 0:QC])
                inc(i1, "s_dv")

        @blk.gpsimd
        def _(g):
            g.wait_ge(sems["s_pe"], pe_target)
            if ld_wait is not None:
                g.wait_ge(sems["ld"], ld_wait)
            for pt in range(NPT):
                g.wait_ge(sems["s_dv"], V["s_dv"] - NPT + pt + 1)
                i1 = g.tensor_sub(dst[:, dst_col(pt):dst_col(pt) + QC],
                                  tmpf[:, (pt % 2) * QC:(pt % 2) * QC + QC],
                                  mrbs[:, :])
                inc(i1, "s_gp")

    def emit_ffn_chunk(blk, nch, src_t, src_c0):
        layers = [(wsb[0], src_t, src_c0, h1c), (wsb[1], h1c, 0, h2c),
                  (wsb[2], h2c, 0, None)]
        for li, (w, src, sc0, dst) in enumerate(layers):
            pe_start = V["s_pe"]

            @blk.tensor
            def _(t, w=w, src=src, sc0=sc0, li=li):
                for mt in range(NPT):
                    slot = P0[:, (mt % 2) * QC:(mt % 2) * QC + QC]
                    if mt == 0:
                        t.wait_ge(sems["s_dv"], V["s_dv"])
                        if li == 0:
                            t.wait_ge(sems["s_gp"], V["s_gp"])
                    if mt >= 2:
                        if li == 2:
                            t.wait_ge(sems["s_ac"], V["s_ac"] + mt - 1)
                        else:
                            t.wait_ge(sems["s_dv"], V["s_dv"] + mt - 1)
                    for kc in range(NPT):
                        i1 = t.matmul(slot, w[:, kc * K1 + mt * 128:kc * K1 + mt * 128 + 128],
                                      src[:, sc0 + kc * QC:sc0 + kc * QC + QC],
                                      start=(kc == 0), stop=(kc == NPT - 1))
                    inc(i1, "s_pe")

            if li < 2:
                @blk.vector
                def _(v, dst=dst, pe_start=pe_start):
                    for mt in range(NPT):
                        v.wait_ge(sems["s_pe"], pe_start + mt + 1)
                        slot = P0[:, (mt % 2) * QC:(mt % 2) * QC + QC]
                        i1 = v.tensor_copy(dst[:, mt * QC:mt * QC + QC], slot)
                        inc(i1, "s_dv")
            else:
                @blk.scalar
                def _(s, pe_start=pe_start):
                    for mt in range(NPT):
                        s.wait_ge(sems["s_pe"], pe_start + mt + 1)
                        s.wait_ge(sems["s_dv"], V["s_dv"] + mt)
                        slot = P0[:, (mt % 2) * QC:(mt % 2) * QC + QC]
                        i1 = s.activation(th_sb[:, :], slot, AF.Tanh,
                                          bias=fbias_sb[:, mt:mt + 1], scale=1.0)
                        inc(i1, "s_ac")

                @blk.vector
                def _(v, nch=nch):
                    for mt in range(NPT):
                        v.wait_ge(sems["s_ac"], V["s_ac"] - NPT + mt + 1)
                        dstr = resid[:, mt * TL + nch * QC:mt * TL + nch * QC + QC]
                        i1 = v.scalar_tensor_tensor(dstr, th_sb[:, :], 0.5, dstr,
                                                    ALU.mult, ALU.add)
                        inc(i1, "s_dv")

    blk = open_block()
    emit_ln_stats_all(blk)
    emit_ln_bcast(blk, 0)
    emit_ln_apply(blk, 0, xlnc, lambda pt: pt * QC)
    for nch in range(TL // QC):
        if nch + 1 < TL // QC:
            emit_ln_bcast(blk, nch + 1)
        emit_ffn_chunk(blk, nch, xlnc, (nch % 2) * NPT * QC)
        if nch + 1 < TL // QC:
            p = (nch + 1) % 2
            emit_ln_apply(blk, nch + 1, xlnc, lambda pt, p=p: p * NPT * QC + pt * QC)
    close_block()
    if nblocks <= 2:
        return _finish_early(nc, sems, V, sem_cms)
    blk = open_block()
    emit_ln_stats_all(blk)
    emit_ln_bcast(blk, 0)
    dma_ld_after = {}
    for nch in range(TL // QC):
        buf = lnstage if nch % 2 == 0 else lnstage2
        emit_ln_apply(blk, nch, buf, lambda pt: pt * QC,
                      ld_wait=(dma_ld_after[nch - 2] if nch >= 2 else None))
        if nch + 1 < TL // QC:
            emit_ln_bcast(blk, nch + 1)

        @blk.sync
        def _(sy, nch=nch, buf=buf):
            sy.wait_ge(sems["s_gp"], V["s_gp"])
            ct = cin1a if nch < 2 else cin1b
            i1 = sy.dma_start(
                ct.ap().rearrange("(t p) c -> p t c", p=128)[
                    :, :, (nch % 2) * QC:(nch % 2) * QC + QC],
                buf[:, :].rearrange("p (t c) -> p t c", c=QC))
            inc(i1, "ld", 16)
        dma_ld_after[nch] = V["ld"]

        if nch == 1:
            cc1a_ld = V["ld"]

            @blk.gpsimd
            def _(g, cc1a_ld=cc1a_ld):
                g.wait_ge(sems["ld"], cc1a_ld)
                i1 = g.collective_compute(
                    "AllGather", ALU.bypass,
                    replica_groups=[[0, 1], [2, 3], [4, 5], [6, 7]],
                    ins=[cin1a.ap().opt()], outs=[cout1a.ap().opt()])
                inc(i1, "cc1")
    cc1_ld = V["ld"]

    @blk.sync
    def _(sy):
        sy.wait_ge(sems["s_dv"], V["s_dv"])
        i1 = sy.dma_start(rs_spill.ap(), resid[:, :])
        inc(i1, "ld", 16)

    @blk.gpsimd
    def _(g):
        g.wait_ge(sems["ld"], cc1_ld)
        i1 = g.collective_compute(
            "AllGather", ALU.bypass,
            replica_groups=[[0, 1], [2, 3], [4, 5], [6, 7]],
            ins=[cin1b.ap().opt()], outs=[cout1b.ap().opt()])
        inc(i1, "cc1")

    close_block()
    if nblocks <= 3:
        return _finish_early(nc, sems, V, sem_cms)
    blk = open_block()

    @blk.gpsimd
    def _(g):
        g.dma_start(wq_sb[:, :].rearrange("p (t c) -> p t c", c=512),
                    wq_in.ap().rearrange("(t p) c -> p t c", p=128)).then_inc(sems["lg"], 16)
        V["lg"] += 16
        g.dma_start(wkv_sb[:, :].rearrange("p (t c) -> p t c", c=512),
                    wkv_in.ap().rearrange("(t p) c -> p t c", p=128)).then_inc(sems["lg"], 16)
        V["lg"] += 16

    @blk.sync
    def _(sy):
        sy.wait_ge(sems["cc1"], V["cc1"] - 1)
        for r in range(2):
            i1 = sy.dma_start(
                xlnfull[:, :].rearrange("p (t c) -> p t c", c=TG)[
                    :, :, r * TL:r * TL + TL // 2],
                cout1a.ap()[r * D:(r + 1) * D, :].rearrange("(t p) c -> p t c", p=128))
            inc(i1, "ld", 16)
    xlna_ready = V["ld"]

    @blk.sync
    def _(sy):
        sy.wait_ge(sems["cc1"], V["cc1"])
        for r in range(2):
            i1 = sy.dma_start(
                xlnfull[:, :].rearrange("p (t c) -> p t c", c=TG)[
                    :, :, r * TL + TL // 2:r * TL + TL],
                cout1b.ap()[r * D:(r + 1) * D, :].rearrange("(t p) c -> p t c", p=128))
            inc(i1, "ld", 16)

    @blk.vector
    def _(v):
        i1 = v.memset(vp_sb[:, :], 1.0)
        inc(i1, "s_dv")

    xln_ready = V["ld"]
    vpones = V["s_dv"]

    _kv_mts = [] if b3_parts < 2 else ([0, 1] if b3_parts < 3 else [0, 1, 2, 3])
    evac_dv = {}
    _nch_order = [0, 1, 4, 5, 2, 3, 6, 7]
    for mt in _kv_mts:
        for ni in range(NQC):
            nch = _nch_order[ni]
            pe_t = V["s_pe"]
            step = mt * NQC + ni

            @blk.tensor
            def _(t, mt=mt, nch=nch, ni=ni, step=step):
                if mt == 0 and ni == 0:
                    t.wait_ge(sems["ld"], xlna_ready)
                    t.wait_ge(sems["lg"], V["lg"])
                if mt == 0 and ni == 4:
                    t.wait_ge(sems["ld"], xln_ready)
                slot = P0[:, (nch % 2) * QC:(nch % 2) * QC + QC]
                if step >= 2:
                    t.wait_ge(sems["s_dv"], evac_dv[step - 2])
                for kc in range(NPT):
                    i1 = t.matmul(slot,
                                  wkv_sb[:, kc * 512 + mt * 128:kc * 512 + mt * 128 + 128],
                                  xlnfull[:, kc * TG + nch * QC:kc * TG + nch * QC + QC],
                                  start=(kc == 0), stop=(kc == NPT - 1))
                inc(i1, "s_pe")

            if mt < 2:
                @blk.vector
                def _(v, mt=mt, nch=nch, pe_t=pe_t):
                    v.wait_ge(sems["s_pe"], pe_t + 1)
                    slot = P0[:, (nch % 2) * QC:(nch % 2) * QC + QC]
                    i1 = v.tensor_copy(kf_sb[:, mt * TG + nch * QC:mt * TG + nch * QC + QC],
                                       slot)
                    inc(i1, "s_dv")
                evac_dv[step] = V["s_dv"]
            else:
                @blk.vector
                def _(v, mt=mt, nch=nch, pe_t=pe_t):
                    v.wait_ge(sems["s_pe"], pe_t + 1)
                    slot = P0[:, (nch % 2) * QC:(nch % 2) * QC + QC]
                    i1 = v.tensor_copy(vstage[:, :], slot)
                    inc(i1, "s_dv")
                evac_dv[step] = V["s_dv"]
                dv_t = V["s_dv"]

                @blk.tensor
                def _(t, mt=mt, nch=nch, dv_t=dv_t):
                    t.wait_ge(sems["s_dv"], dv_t)
                    if mt == 2 and nch == 0:
                        t.wait_ge(sems["s_dv"], vpones)
                    for sst in range(4):
                        i1 = t.transpose(PT[:, sst * 128:sst * 128 + 128],
                                         vstage[:, sst * 128:sst * 128 + 128],
                                         id64_sb[:, :])
                    inc(i1, "s_pe")

                @blk.vector
                def _(v, mt=mt, nch=nch):
                    v.wait_ge(sems["s_pe"], V["s_pe"])
                    for half in range(2):
                        g = (mt - 2) * 2 + half
                        st0 = nch * 4
                        i1 = v.tensor_copy(
                            vp_sb[:, :].rearrange("p (a b) -> p a b", b=VPS)[
                                :, g * NST + st0:g * NST + st0 + 4, 0:64],
                            PT[:, :].rearrange("p (a b) -> p a b", b=128)[
                                :, :, half * 64:half * 64 + 64])
                        inc(i1, "s_dv")

    qevac_dv = {}
    for mt in (range(4) if b3_parts >= 4 else []):
        for ni in range(NQC):
            nch = _nch_order[ni]
            pe_t = V["s_pe"]
            step = mt * NQC + ni

            @blk.tensor
            def _(t, mt=mt, nch=nch, step=step):
                slot = P1[:, (nch % 2) * QC:(nch % 2) * QC + QC]
                if step >= 2:
                    t.wait_ge(sems["s_dv"], qevac_dv[step - 2])
                for kc in range(NPT):
                    i1 = t.matmul(slot,
                                  wq_sb[:, kc * 512 + mt * 128:kc * 512 + mt * 128 + 128],
                                  xlnfull[:, kc * TG + nch * QC:kc * TG + nch * QC + QC],
                                  start=(kc == 0), stop=(kc == NPT - 1))
                inc(i1, "s_pe")

            @blk.vector
            def _(v, mt=mt, nch=nch, pe_t=pe_t):
                v.wait_ge(sems["s_pe"], pe_t + 1)
                slot = P1[:, (nch % 2) * QC:(nch % 2) * QC + QC]
                i1 = v.tensor_scalar_mul(q_sb[:, mt * TG + nch * QC:mt * TG + nch * QC + QC],
                                         slot, 0.125)
                inc(i1, "s_dv")
            qevac_dv[step] = V["s_dv"]

    close_block()
    PT_guard.__exit__(None, None, None)
    P4 = nc.alloc_psum_tensor("P4", [65, QC], F32)
    P5 = nc.alloc_psum_tensor("P5", [65, QC], F32)
    if nblocks <= 4:
        return _finish_early(nc, sems, V, sem_cms)
    blk = open_block()

    @blk.sync
    def _(sy):
        i1 = sy.dma_start(resid[:, :], rs_spill.ap())
        inc(i1, "ld", 16)
    resid_back = V["ld"]

    NGRP = NST

    def lin(pr, qcg, st):
        return (pr * 4 + qcg) * NST + st

    devents = []
    dix = {}
    if b4_parts <= 1:
        _pq = []
    elif b4_parts == 2:
        _pq = [(0, 0)]
    elif b4_parts == 3:
        _pq = [(0, 0), (0, 1)]
    elif b4_parts == 4:
        _pq = [(0, q) for q in range(4)]
    else:
        _pq = [(pr, q) for pr in range(4) for q in range(4)]
    for pr, qcg in _pq:
        for st in range(NST):
            for hx in range(2):
                for j in range(2):
                    if _chunk_class(st, qcg * 2 + j) == "D":
                        dix[(pr, qcg, st, hx)] = dix.get((pr, qcg, st, hx))
                        dix[(pr, qcg, st, hx, j)] = len(devents)
                        devents.append((pr, qcg, st, hx, j))

    qk0 = V["s_qk"]
    ex0 = V["s_ex"]
    pv0 = V["s_pv"]
    st0 = V["s_st"]
    r0 = V["s_r"]
    ao0 = V["s_ao"]
    ld0 = V["ld"]
    pe0 = V["s_pe"]
    lnl0 = V["s_lnl"]

    def _exp_of_event(e):
        pe_, pq_, ps_, ph_, pj_ = devents[e]
        return ex0 + 2 * lin(pe_, pq_, ps_) + 1 + ph_

    for pr, qcg in _pq:
        G = pr * 4 + qcg
        L0 = G * NST
        heads = (2 * pr, 2 * pr + 1)
        QKP = (P0, P1)
        PVP = ((P2, P3), (P4, P5))

        @blk.tensor
        def _(t, pr=pr, qcg=qcg, G=G, L0=L0, heads=heads):
            for st in range(NST):
                L = L0 + st
                acls = [_chunk_class(st, qcg * 2 + j) for j in range(2)]
                if L >= 1:
                    t.wait_ge(sems["s_ex"], ex0 + 2 * (L - 1) + 2)
                if st == 0 and G >= 1:
                    t.wait_ge(sems["s_ao"], ao0 + 4 * (G - 1) + 4)
                last = [None, None]
                for j in range(2):
                    for hx in range(2):
                        l = heads[hx]
                        band = (l % 2) * 64
                        g = l % 4
                        lhs = kf_sb[(g % 2) * 64:(g % 2) * 64 + 64,
                                    (g // 2) * TG + st * 128:(g // 2) * TG + st * 128 + 128]
                        rhs = q_sb[band:band + 64,
                                   (l // 2) * TG + (qcg * 2 + j) * QC:
                                   (l // 2) * TG + (qcg * 2 + j) * QC + QC]
                        last[hx] = t.matmul(QKP[hx][:, j * QC:j * QC + QC], lhs, rhs,
                                            start=True, stop=(acls[j] != "A"),
                                            tile_position=(band, 0))
                for j in range(2):
                    if acls[j] != "A":
                        continue
                    qc = qcg * 2 + j
                    for hx in range(2):
                        l = heads[hx]
                        aband = 64 - (l % 2) * 64
                        last[hx] = t.matmul(
                            QKP[hx][:, j * QC:j * QC + QC],
                            augw_sb[aband:aband + 5, l * 128:l * 128 + 128],
                            augq_sb[aband:aband + 5, qc * QC:qc * QC + QC],
                            start=False, stop=True, tile_position=(aband, 0))
                last[0].then_inc(sems["s_qk"], 1)
                last[1].then_inc(sems["s_qk"], 1)
                if st >= 1:
                    for hx in range(2):
                        l = heads[hx]
                        g = l % 4
                        vcol = (g * NST + (st - 1)) * VPS
                        for j in range(2):
                            pvp = PVP[hx][j]
                            i1 = t.matmul(
                                pvp[:, :], vp_sb[:, vcol:vcol + 65],
                                pt_sb[:, ((st - 1) % 2) * 2048 + hx * 1024 + j * QC:
                                      ((st - 1) % 2) * 2048 + hx * 1024 + j * QC + QC],
                                start=(st - 1 == 0), stop=False)
                        i1.then_inc(sems["s_pv"], 1)
            for hx in range(2):
                l = heads[hx]
                g = l % 4
                t.wait_ge(sems["s_ex"], ex0 + 2 * (L0 + NST - 1) + 1 + hx)
                vcol = (g * NST + NST - 1) * VPS
                for j in range(2):
                    pvp = PVP[hx][j]
                    i1 = t.matmul(
                        pvp[:, :], vp_sb[:, vcol:vcol + 65],
                        pt_sb[:, ((NST - 1) % 2) * 2048 + hx * 1024 + j * QC:
                              ((NST - 1) % 2) * 2048 + hx * 1024 + j * QC + QC],
                        start=False, stop=True)
                i1.then_inc(sems["s_pv"], 1)
            for hx in range(2):
                t.wait_ge(sems["s_r"], r0 + 2 * G + 1 + hx)
                qk = QKP[hx]
                t.matmul(qk[0:64, 0:QC], ones1_sb[0:1, 0:64], rr_sb[0:1, 0:QC],
                         start=True, stop=True)
                i1 = t.matmul(qk[0:64, QC:2 * QC], ones1_sb[0:1, 0:64],
                              rr_sb[0:1, QC:2 * QC], start=True, stop=True)
                i1.then_inc(sems["s_pe"], 1)
        V["s_qk"] += 2 * NST
        V["s_pv"] += 2 * NST
        V["s_pe"] += 2

        @blk.vector
        def _(v, pr=pr, qcg=qcg, G=G, L0=L0, heads=heads):
            for st in range(NST):
                L = L0 + st
                for hx in range(2):
                    l = heads[hx]
                    qk = QKP[hx]
                    for j in range(2):
                        qc = qcg * 2 + j
                        if _chunk_class(st, qc) != "D":
                            continue
                        e = dix[(pr, qcg, st, hx, j)]
                        v.wait_ge(sems["s_qk"], qk0 + 2 * L + 1 + hx)
                        if e >= 2:
                            v.wait_ge(sems["s_ex"], _exp_of_event(e - 2))
                        d0 = st * 128 - qc * QC
                        i1 = v.scalar_tensor_tensor(
                            dsc[:, (e % 2) * QC:(e % 2) * QC + QC],
                            dtab_sb[:, (d0 // 128) * QC:(d0 // 128) * QC + QC],
                            slop_sb[:, l:l + 1],
                            qk[:, j * QC:j * QC + QC],
                            ALU.mult, ALU.add)
                        inc(i1, "s_st")
            for hx in range(2):
                l = heads[hx]
                qk = QKP[hx]
                pva, pvb = PVP[hx]
                v.wait_ge(sems["s_pv"], pv0 + 2 * (L0 + NST - 1) + 1 + hx)
                if G >= 1 or hx >= 1:
                    v.wait_ge(sems["s_r"], r0 + 2 * G + hx)
                v.tensor_copy(lnl_sb[:, 0:QC], pva[64:65, :])
                i1 = v.tensor_copy(lnl_sb[:, QC:2 * QC], pvb[64:65, :])
                inc(i1, "s_lnl")
                v.wait_ge(sems["s_r"], r0 + 2 * G + 1 + hx)
                v.wait_ge(sems["s_pe"], pe0 + 2 * G + 1 + hx)
                if G >= 1:
                    v.wait_ge(sems["ld"], ld0 + 32 * G)
                v.tensor_copy(rbs[:, :], qk[0:64, 0:QC])
                v.scalar_tensor_tensor(aost[:, hx * 2 * QC:hx * 2 * QC + QC],
                                       pva[0:64, :], 1.0, rbs[:, :],
                                       ALU.mult, ALU.mult).then_inc(sems["s_ao"], 1)
                v.tensor_copy(rbs[:, :], qk[0:64, QC:2 * QC])
                v.scalar_tensor_tensor(aost[:, (hx * 2 + 1) * QC:(hx * 2 + 1) * QC + QC],
                                       pvb[0:64, :], 1.0, rbs[:, :],
                                       ALU.mult, ALU.mult).then_inc(sems["s_ao"], 1)
        V["s_st"] = st0 + len([1 for e in devents if (e[0] * 4 + e[1]) <= G])
        V["s_ao"] += 4
        V["s_lnl"] += 2

        V["s_ex"] += 2 * NST

        @blk.sync
        def _(sy, pr=pr, qcg=qcg, G=G):
            half = qcg // 2
            colo = (qcg % 2) * 1024
            for hx in range(2):
                l = 2 * pr + hx
                snd = snd2a if l < 4 else snd2b
                row = half * 256 + (l % 4) * 64
                sy.wait_ge(sems["s_ao"], ao0 + 4 * G + 2 * hx + 2)
                i1 = sy.dma_start(
                    snd.ap()[row:row + 64, colo:colo + 1024],
                    aost[:, hx * 2 * QC:hx * 2 * QC + 2 * QC])
                inc(i1, "ld", 16)

        if G == 7:
            @blk.gpsimd
            def _(g):
                if b4_parts >= 8:
                    g.wait_ge(sems["ld"], V["ld"])
                    i1 = g.collective_compute(
                        "AllGather", ALU.bypass,
                        replica_groups=[[0, 1], [2, 3], [4, 5], [6, 7]],
                        ins=[snd2a.ap().opt()], outs=[rcv2a.ap().opt()])
                    inc(i1, "cc2")

    @blk.scalar
    def _(s):
        for pr, qcg in _pq:
            G = pr * 4 + qcg
            L0 = G * NST
            heads = (2 * pr, 2 * pr + 1)
            for st in range(NST):
                L = L0 + st
                c0 = _chunk_class(st, qcg * 2)
                c1 = _chunk_class(st, qcg * 2 + 1)
                if c0 == "B" and c1 == "B":
                    s.wait_ge(sems["s_qk"], qk0 + 2 * L + 2)
                    if L >= 2:
                        s.wait_ge(sems["s_pv"], pv0 + 2 * (L - 2) + 2)
                    pb = (st % 2) * 2048
                    s.activation(pt_sb[:, pb:pb + 2048], P01[:, :], AF.Exp,
                                 bias=negc_sb[:, 0:1], scale=1.0).then_inc(sems["s_ex"], 2)
                    continue
                for hx in range(2):
                    l = heads[hx]
                    qk = (P0, P1)[hx]
                    s.wait_ge(sems["s_qk"], qk0 + 2 * L + 1 + hx)
                    if L >= 2:
                        s.wait_ge(sems["s_pv"], pv0 + 2 * (L - 2) + 1 + hx)
                    pbase = (st % 2) * 2048 + hx * 1024
                    pdst = pt_sb[:, pbase:pbase + 1024]
                    if c0 == c1 and c0 != "D":
                        if c0 == "A":
                            bias = biasc_sb[:, l * NST + st:l * NST + st + 1]
                        else:
                            bias = negc_sb[:, 0:1]
                        i1 = s.activation(pdst, qk[:, :], AF.Exp, bias=bias, scale=1.0)
                    else:
                        for j in range(2):
                            qc = qcg * 2 + j
                            cls = _chunk_class(st, qc)
                            dd = pt_sb[:, pbase + j * QC:pbase + j * QC + QC]
                            if cls == "D":
                                e = dix[(pr, qcg, st, hx, j)]
                                s.wait_ge(sems["s_st"], st0 + e + 1)
                                i1 = s.activation(dd, dsc[:, (e % 2) * QC:(e % 2) * QC + QC],
                                                  AF.Exp, bias=negc_sb[:, 0:1], scale=1.0)
                            elif cls == "A":
                                i1 = s.activation(dd, qk[:, j * QC:j * QC + QC], AF.Exp,
                                                  bias=biasc_sb[:, l * NST + st:l * NST + st + 1],
                                                  scale=1.0)
                            else:
                                i1 = s.activation(dd, qk[:, j * QC:j * QC + QC], AF.Exp,
                                                  bias=negc_sb[:, 0:1], scale=1.0)
                    i1.then_inc(sems["s_ex"], 1)
            for hx in range(2):
                s.wait_ge(sems["s_lnl"], lnl0 + 2 * G + 1 + hx)
                s.wait_ge(sems["s_pe"], pe0 + 2 * G + hx)
                s.activation(lnt_att[:, :], lnl_sb[:, :], AF.Ln, bias=0.0, scale=1.0)
                s.activation(rr_sb[:, :], lnt_att[:, :], AF.Exp,
                             bias=0.0, scale=-1.0).then_inc(sems["s_r"], 1)

    V["s_r"] += 2 * len(_pq)

    @blk.gpsimd
    def _(g):
        if b4_parts < 8:
            return
        g.wait_ge(sems["ld"], V["ld"])
        i1 = g.collective_compute(
            "AllGather", ALU.bypass,
            replica_groups=[[0, 1], [2, 3], [4, 5], [6, 7]],
            ins=[snd2b.ap().opt()], outs=[rcv2b.ap().opt()])
        inc(i1, "cc2")

    close_block()
    if nblocks <= 5:
        return _finish_early(nc, sems, V, sem_cms)
    blk = open_block()

    @blk.gpsimd
    def _(g):
        i1 = g.dma_start(wpj_sb[:, :].rearrange("p (t c) -> p t c", c=K1),
                         wproj_in.ap().rearrange("(t p) c -> p t c", p=128))
        inc(i1, "lg", 16)

    _ptorder = [0, 1, 4, 5, 2, 3, 6, 7]

    @blk.sync
    def _(sy):
        for i, pt in enumerate(_ptorder):
            if i == 0:
                sy.wait_ge(sems["cc2"], V["cc2"] - 1)
            if i == 4:
                sy.wait_ge(sems["cc2"], V["cc2"])
            rcv = rcv2a if pt in (0, 1, 4, 5) else rcv2b
            lp = pt % 2 if pt < 4 else (pt - 4) % 2
            r0 = (0 if pt < 4 else 512) + (pt % 4 % 2) * 0 + lp * 128
            r0 = (512 if pt >= 4 else 0) + lp * 128
            r1 = r0 + 256
            sy.dma_start(aoc0[:, :], rcv.ap()[r0:r0 + 128, :]).then_inc(sems["ld"], 16)
            V["ld"] += 16
            sy.dma_start(aoc1[:, :], rcv.ap()[r1:r1 + 128, :]).then_inc(sems["ld"], 16)
            V["ld"] += 16
            sy.wait_ge(sems["s_dv"], V["s_dv"] + i + 1)

    @blk.vector
    def _(v):
        base_ld = V["ld"] - 32 * NPT
        for i, pt in enumerate(_ptorder):
            v.wait_ge(sems["ld"], base_ld + (i + 1) * 32)
            v.tensor_scalar_mul(aot2[:, :], aoc0[:, :], sel_sb[:, 0:1])
            i1 = v.scalar_tensor_tensor(ao_sb[:, pt * TL:pt * TL + TL],
                                        aoc1[:, :], sel_sb[:, 1:2], aot2[:, :],
                                        ALU.mult, ALU.add)
            inc(i1, "s_dv")

    ao_ready_dv = V["s_dv"]
    wpj_ready = V["lg"]

    for nch in range(TL // QC):
        for mt in range(NPT):
            pe_t = V["s_pe"]
            step = nch * NPT + mt

            @blk.tensor
            def _(t, mt=mt, nch=nch, step=step):
                if step == 0:
                    t.wait_ge(sems["s_dv"], ao_ready_dv)
                    t.wait_ge(sems["lg"], wpj_ready)
                slot = P0[:, (step % 2) * QC:(step % 2) * QC + QC]
                if step >= 2:
                    t.wait_ge(sems["s_dv"], V["s_dv"] - 1)
                for kc in range(NPT):
                    i1 = t.matmul(slot,
                                  wpj_sb[:, kc * K1 + mt * 128:kc * K1 + mt * 128 + 128],
                                  ao_sb[:, kc * TL + nch * QC:kc * TL + nch * QC + QC],
                                  start=(kc == 0), stop=(kc == NPT - 1))
                inc(i1, "s_pe")

            @blk.vector
            def _(v, mt=mt, nch=nch, pe_t=pe_t, step=step):
                v.wait_ge(sems["s_pe"], pe_t + 1)
                if step == 0:
                    v.wait_ge(sems["ld"], resid_back)
                slot = P0[:, (step % 2) * QC:(step % 2) * QC + QC]
                dstr = resid[:, mt * TL + nch * QC:mt * TL + nch * QC + QC]
                i1 = v.scalar_tensor_tensor(dstr, slot, bproj_sb[:, mt:mt + 1],
                                            dstr, ALU.add, ALU.add)
                inc(i1, "s_dv")
        emit_ln_stats_chunk(blk, nch)
    emit_ln_stats_fin(blk)

    close_block()
    if nblocks <= 6:
        return _finish_early(nc, sems, V, sem_cms)
    blk = open_block()

    @blk.gpsimd
    def _(g):
        for i in range(3):
            i1 = g.dma_start(wsb[i][:, :].rearrange("p (t c) -> p t c", c=K1),
                             w_ffn[i].ap().rearrange("(t p) c -> p t c", p=128))
            inc(i1, "lg", 16)
    w_ready = V["lg"]

    emit_ln_bcast(blk, 0)
    emit_ln_apply(blk, 0, xlnc, lambda pt: pt * QC)

    @blk.tensor
    def _(t):
        t.wait_ge(sems["lg"], w_ready)
    for nch in range(TL // QC):
        if nch + 1 < TL // QC:
            emit_ln_bcast(blk, nch + 1)
        emit_ffn_chunk(blk, nch, xlnc, (nch % 2) * NPT * QC)
        if nch + 1 < TL // QC:
            p = (nch + 1) % 2
            emit_ln_apply(blk, nch + 1, xlnc, lambda pt, p=p: p * NPT * QC + pt * QC)

        @blk.sync
        def _(sy, nch=nch):
            sy.wait_ge(sems["s_dv"], V["s_dv"])
            i1 = sy.dma_start(
                y_out.ap().rearrange("(t p) c -> p t c", p=128)[:, :, nch * QC:nch * QC + QC],
                resid[:, :].rearrange("p (t c) -> p t c", c=TL)[:, :, nch * QC:nch * QC + QC])
            inc(i1, "ld", 16)

    @blk.sync
    def _(sy):
        sy.wait_ge(sems["ld"], V["ld"])

    close_block()
    if nblocks <= 7:
        return _finish_early(nc, sems, V, sem_cms)
    for cm in reversed(sem_cms):
        cm.__exit__(None, None, None)
    return nc


def _core_tables(parity):
    start = 2.0 ** (-2.0 ** (-(np.log2(16) - 3.0)))
    slopes = start ** np.arange(1, 17, dtype=np.float64)
    gh = np.arange(parity * 8, parity * 8 + 8)
    sl = slopes[gh].astype(np.float32)

    augw = np.zeros((5, NHL * 128), np.float32)
    for l in range(NHL):
        s = float(sl[l])
        s_hi = float(BF(s))
        s_lo = s - s_hi
        m = np.arange(128, dtype=np.float64)
        mv = (s * m / 8.0).astype(np.float32)
        m_hi = BF(mv)
        m_lo = (mv - m_hi).astype(np.float32)
        augw[0, l * 128:(l + 1) * 128] = s_hi
        augw[1, l * 128:(l + 1) * 128] = s_hi
        augw[2, l * 128:(l + 1) * 128] = s_lo
        augw[3, l * 128:(l + 1) * 128] = m_hi
        augw[4, l * 128:(l + 1) * 128] = m_lo

    biasc = np.zeros((128, NHL * NST), np.float32)
    for l in range(NHL):
        for st in range(NST):
            biasc[:, l * NST + st] = sl[l] * (st * 128) / 8.0 - CSHIFT

    slop = np.tile(sl[None, :], (128, 1)).astype(np.float32)
    return augw, biasc, slop


def _shared_tables():
    q = np.arange(TG, dtype=np.float64) / 8.0
    qh = BF(-q)
    ql = (-q - qh).astype(np.float32)
    assert np.all(BF(ql) == ql), "q_lo must be bf16-exact"
    augq = np.zeros((5, TG), np.float32)
    augq[0] = qh
    augq[1] = ql
    augq[2] = qh
    augq[3] = 1.0
    augq[4] = 1.0

    dtab = np.zeros((128, 4 * QC), np.float32)
    for jb in range(4):
        d0 = jb * 128
        p = np.arange(128)[:, None]
        f = np.arange(QC)[None, :]
        dd = d0 + p - f
        dtab[:, jb * QC:(jb + 1) * QC] = np.where(dd <= 0, dd / 8.0, 0.0)
    return augq, dtab


_NC_CACHE = {}


def kernel(L0_attn_x, L0_attn_Wq, L0_attn_Wkv, L0_attn_Wproj, L0_attn_bproj,
           L0_ffn_weight, L0_ffn_aux_weight1, L0_ffn_aux_weight2, L0_ffn_bias):
    if "nc" not in _NC_CACHE:
        _NC_CACHE["nc"] = build_program()
    nc = _NC_CACHE["nc"]

    x = np.asarray(L0_attn_x, np.float32)
    Wq = np.asarray(L0_attn_Wq, np.float32)
    Wkv = np.asarray(L0_attn_Wkv, np.float32)
    Wproj = np.asarray(L0_attn_Wproj, np.float32)
    bproj = np.asarray(L0_attn_bproj, np.float32)
    Wf = np.asarray(L0_ffn_weight, np.float32)
    A1 = np.asarray(L0_ffn_aux_weight1, np.float32)
    A2 = np.asarray(L0_ffn_aux_weight2, np.float32)
    fb = np.asarray(L0_ffn_bias, np.float32)

    wq_t = np.ascontiguousarray(Wq.T)
    wkv_t = np.ascontiguousarray(Wkv.T)
    wproj_t = np.ascontiguousarray(Wproj.T)
    bproj_c = np.ascontiguousarray(bproj.reshape(NPT, 128).T)
    fb_c = np.ascontiguousarray(fb.reshape(NPT, 128).T)
    onek = np.full((128, 1), 1.0 / 1024.0, np.float32)
    negc = np.full((128, 1), -CSHIFT, np.float32)
    sels = [np.tile(np.array([[1.0, 0.0]], np.float32), (128, 1)),
            np.tile(np.array([[0.0, 1.0]], np.float32), (128, 1))]
    ones1 = np.ones((1, 128), np.float32)
    id64 = np.eye(128, dtype=np.float32)
    augq, dtab = _shared_tables()
    tabs = [_core_tables(0), _core_tables(1)]

    in_maps = []
    for c in range(8):
        b, half = c // 2, c % 2
        t0 = half * TL
        augw, biasc, slop = tabs[half]
        wq_c = np.ascontiguousarray(wq_t[:, half * 512:half * 512 + 512])
        in_maps.append({
            "x": np.ascontiguousarray(x[b, t0:t0 + TL, :].T),
            "wffn0": Wf, "wffn1": A1, "wffn2": A2,
            "wq": wq_c, "wkv": wkv_t, "wproj": wproj_t,
            "bproj": bproj_c, "fbias": fb_c,
            "augw": augw, "augq": augq, "biasc": biasc, "dtab": dtab,
            "slop": slop, "onek": onek, "ones1": ones1, "id64": id64,
            "negc": negc, "sel": sels[half],
        })

    res = run_bass_kernel_spmd(nc, in_maps, core_ids=list(range(8)))
    out = np.empty((4, TG, D), np.float32)
    for c in range(8):
        b, half = c // 2, c % 2
        t0 = half * TL
        out[b, t0:t0 + TL, :] = res.results[c]["y"].T
    return out


# version 21
# speedup vs baseline: 1.0191x; 1.0191x over previous
import numpy as np
import ml_dtypes
import concourse.bass as bass
import concourse.mybir as mybir
from concourse.bass_utils import run_bass_kernel_spmd

F32 = mybir.dt.float32
BF16 = mybir.dt.bfloat16
AF = mybir.ActivationFunctionType
ALU = mybir.AluOpType

D = 1024
NPT = 8
TL = 2048
TG = 4096
HD = 64
NHL = 8
KVH = 4
QC = 512
NQC = TG // QC
NST = TG // 128
CSHIFT = 15.0
EPS = 1e-5

BF = lambda a: np.asarray(a, np.float32).astype(ml_dtypes.bfloat16).astype(np.float32)


def _chunk_class(st, qc):
    d0 = st * 128 - qc * QC
    if d0 <= -128:
        return "A"
    if d0 >= QC:
        return "B"
    return "D"



def _finish_early(nc, sems, V, sem_cms):
    b = nc.Block()
    blk = b.__enter__()

    @blk.sync
    def _(sy):
        sy.wait_ge(sems["ld"], V["ld"])

    @blk.gpsimd
    def _(g):
        g.wait_ge(sems["lg"], V["lg"])
        g.wait_ge(sems["cc1"], V["cc1"])
        g.wait_ge(sems["cc2"], V["cc2"])

    @blk.vector
    def _(v):
        v.wait_ge(sems["s_dv"], V["s_dv"])

    @blk.scalar
    def _(s):
        s.wait_ge(sems["s_ac"], V["s_ac"])

    @blk.tensor
    def _(t):
        t.wait_ge(sems["s_pe"], V["s_pe"])
    b.__exit__(None, None, None)
    for cm in reversed(sem_cms):
        cm.__exit__(None, None, None)
    return nc


def build_program(nblocks=99, b3_parts=9, b4_parts=9):
    nc = bass.Bass("TRN2", target_bir_lowering=False, detect_race_conditions=False)

    x_in = nc.dram_tensor("x", [D, TL], F32, kind="ExternalInput")
    w_ffn = [nc.dram_tensor(n, [D, D], F32, kind="ExternalInput")
             for n in ("wffn0", "wffn1", "wffn2")]
    wq_in = nc.dram_tensor("wq", [D, NHL * HD], F32, kind="ExternalInput")
    wkv_in = nc.dram_tensor("wkv", [D, 512], F32, kind="ExternalInput")
    wproj_in = nc.dram_tensor("wproj", [D, D], F32, kind="ExternalInput")
    bproj_in = nc.dram_tensor("bproj", [128, NPT], F32, kind="ExternalInput")
    fbias_in = nc.dram_tensor("fbias", [128, NPT], F32, kind="ExternalInput")
    augw_in = nc.dram_tensor("augw", [5, NHL * 128], F32, kind="ExternalInput")
    augq_in = nc.dram_tensor("augq", [5, TG], F32, kind="ExternalInput")
    biasc_in = nc.dram_tensor("biasc", [128, NHL * NST], F32, kind="ExternalInput")
    dtab_in = nc.dram_tensor("dtab", [128, 4 * QC], F32, kind="ExternalInput")
    slop_in = nc.dram_tensor("slop", [128, NHL], F32, kind="ExternalInput")
    onek_in = nc.dram_tensor("onek", [128, 1], F32, kind="ExternalInput")
    ones1_in = nc.dram_tensor("ones1", [1, 128], F32, kind="ExternalInput")
    id64_in = nc.dram_tensor("id64", [128, 128], F32, kind="ExternalInput")
    negc_in = nc.dram_tensor("negc", [128, 1], F32, kind="ExternalInput")
    sel_in = nc.dram_tensor("sel", [128, 2], F32, kind="ExternalInput")

    y_out = nc.dram_tensor("y", [D, TL], F32, kind="ExternalOutput")

    rs_spill = nc.dram_tensor("rs_spill", [128, NPT * TL], F32)
    cin1a = nc.dram_tensor("cin1a", [D, TL // 2], BF16)
    cin1b = nc.dram_tensor("cin1b", [D, TL // 2], BF16)
    cout1a = nc.dram_tensor("cout1a", [2 * D, TL // 2], BF16)
    cout1b = nc.dram_tensor("cout1b", [2 * D, TL // 2], BF16)
    snd2a = nc.dram_tensor("snd2a", [512, TL], BF16)
    snd2b = nc.dram_tensor("snd2b", [512, TL], BF16)
    rcv2a = nc.dram_tensor("rcv2a", [1024, TL], BF16)
    rcv2b = nc.dram_tensor("rcv2b", [1024, TL], BF16)

    SB = {}

    SBUF_BASE = 17 * 1024

    def sb(name, off, shape, dtype):
        off = off + SBUF_BASE
        t = nc.alloc_sbuf_tensor_at(name, list(shape), dtype, offset=off)
        SB[name] = (off, shape, dtype)
        return t

    K1 = 1024
    resid = sb("resid", 0, [128, NPT * TL], F32)
    xlnfull = sb("xlnfull", 0, [128, NPT * TG], BF16)

    wsb = [sb(f"wsb{i}", 64 * K1 + i * 16 * K1, [128, NPT * K1], BF16)
           for i in range(3)]
    q_sb = sb("q_sb", 64 * K1, [128, 4 * TG], BF16)
    kf_sb = sb("kf_sb", 96 * K1, [128, 2 * TG], BF16)
    ao_sb = sb("ao_sb", 64 * K1, [128, NPT * TL], BF16)

    h1c = sb("h1c", 112 * K1, [128, NPT * QC], BF16)
    h2c = sb("h2c", 120 * K1, [128, NPT * QC], BF16)
    xb = sb("xb", 128 * K1, [128, 2 * QC], BF16)
    xsq = sb("xsq", 130 * K1, [128, 2 * QC], BF16)
    VPS = 68
    vp_sb = sb("vp_sb", 112 * K1, [128, KVH * NST * VPS], BF16)
    wpj_sb = sb("wpj_sb", 112 * K1, [128, NPT * K1], BF16)

    wq_sb = sb("wq_sb", 132 * K1, [128, NPT * 512], BF16)
    wkv_sb = sb("wkv_sb", 140 * K1, [128, NPT * 512], BF16)
    xlnc = sb("xlnc", 132 * K1, [128, 2 * NPT * QC], BF16)
    pt_sb = sb("pt_sb", 132 * K1, [128, 4 * 1024], BF16)
    dsc = sb("dsc", 140 * K1, [128, 2 * QC], F32)
    rr_sb = sb("rr_sb", 144 * K1, [1, 2 * QC], F32)
    aost = sb("aost", 150 * K1, [64, 4 * QC], BF16)
    lnl_sb = sb("lnl_sb", 160 * K1, [1, 2 * QC], F32)
    lnt_att = sb("lnt_att", 168 * K1, [1, 2 * QC], F32)
    rbs = sb("rbs", 154 * K1, [64, QC], F32)

    lnstage = sb("lnstage", 148 * K1, [128, NPT * QC], BF16)
    vstage = sb("vstage", 148 * K1, [128, QC], BF16)
    tmpf = sb("tmpf", 156 * K1, [128, 2 * QC], F32)
    mrbs = sb("mrbs", 128 * K1, [128, QC], F32)
    th_sb = sb("th_sb", 148 * K1, [128, QC], BF16)
    tab_off = 185 * K1

    stA = sb("stA", 160 * K1, [1, 4 * QC], F32)
    stB = sb("stB", 168 * K1, [1, 4 * QC], F32)
    stC = sb("stC", 176 * K1, [1, 4 * QC], F32)
    pvt = sb("pvt", 176 * K1, [65, 4 * QC], F32)
    lnstage2 = sb("lnstage2", 132 * K1, [128, NPT * QC], BF16)

    augw_sb = sb("augw_sb", tab_off, [69, NHL * 128], BF16)
    augq_sb = sb("augq_sb", tab_off + 2 * K1, [69, TG], BF16)
    biasc_sb = sb("biasc_sb", tab_off + 10 * K1, [128, NHL * NST], F32)
    dtab_sb = sb("dtab_sb", tab_off + 11 * K1, [128, 4 * QC], F32)
    slop_sb = sb("slop_sb", tab_off + 19 * K1, [128, NHL], F32)
    onek_sb = sb("onek_sb", tab_off + 19 * K1 + 64, [128, 1], BF16)
    ones1_sb = sb("ones1_sb", tab_off + 19 * K1 + 128, [1, 128], F32)
    id64_sb = sb("id64_sb", tab_off + 20 * K1, [128, 128], BF16)
    bproj_sb = sb("bproj_sb", tab_off + 20 * K1 + 256, [128, NPT], F32)
    fbias_sb = sb("fbias_sb", tab_off + 20 * K1 + 512, [128, NPT], F32)
    negc_sb = sb("negc_sb", tab_off + 20 * K1 + 768, [128, 1], F32)
    sel_sb = sb("sel_sb", tab_off + 20 * K1 + 832, [128, 2], F32)
    aoc0 = sb("aoc0", 132 * K1, [128, TL], BF16)
    aoc1 = sb("aoc1", 136 * K1, [128, TL], BF16)
    aot2 = sb("aot2", 140 * K1, [128, TL], BF16)
    assert tab_off + 21 * K1 + 17 * K1 <= 229344 // 1024 * K1

    P01 = nc.alloc_psum_tensor("P01", [128, 2048], F32)
    P0 = P01[:, 0:1024]
    P1 = P01[:, 1024:2048]
    P2 = nc.alloc_psum_tensor("P2", [65, QC], F32)
    P3 = nc.alloc_psum_tensor("P3", [65, QC], F32)
    PT_guard = nc.psum_tensor("PT", [128, 512], BF16)
    PT = PT_guard.__enter__()

    semnames = ["ld", "lg", "cc1", "cc2", "s_pe", "s_ac", "s_dv", "s_gp",
                "s_qk", "s_ex", "s_pv", "s_st", "s_r", "s_ao", "s_lnl", "s_fin"]
    sems = {}
    sem_cms = []
    for n in semnames:
        cm = nc.semaphore(n)
        sems[n] = cm.__enter__()
        sem_cms.append(cm)
    V = {n: 0 for n in semnames}

    def inc(inst, name, n=1):
        inst.then_inc(sems[name], n)
        V[name] += n

    blocks = []

    def open_block():
        b = nc.Block()
        bb = b.__enter__()
        blocks.append(b)
        return bb

    def close_block(b=None):
        blocks.pop().__exit__(None, None, None)

    blk = open_block()

    x_ld = {}

    @blk.sync
    def _(sy):
        for t_in, t_sb in ((onek_in, None), (ones1_in, ones1_sb)):
            pass
        for nch in range(TL // QC):
            sy.dma_start(
                resid[:, :].rearrange("p (t c) -> p t c", c=TL)[:, :, nch * QC:nch * QC + QC],
                x_in.ap().rearrange("(t p) c -> p t c", p=128)[:, :, nch * QC:nch * QC + QC],
            ).then_inc(sems["ld"], 16)
            V["ld"] += 16
            x_ld[nch] = V["ld"]
        for t_in, t_sb in ((bproj_in, bproj_sb), (fbias_in, fbias_sb),
                           (biasc_in, biasc_sb), (dtab_in, dtab_sb),
                           (slop_in, slop_sb), (ones1_in, ones1_sb),
                           (negc_in, negc_sb), (sel_in, sel_sb)):
            sy.dma_start(t_sb[:, :], t_in.ap()).then_inc(sems["ld"], 16)
            V["ld"] += 16

    @blk.gpsimd
    def _(g):
        for i in range(3):
            g.dma_start(wsb[i][:, :].rearrange("p (t c) -> p t c", c=K1),
                        w_ffn[i].ap().rearrange("(t p) c -> p t c", p=128)).then_inc(sems["lg"], 16)
            V["lg"] += 16
        for base in (0, 64):
            g.dma_start(augw_sb[base:base + 5, :], augw_in.ap()).then_inc(sems["lg"], 16)
            V["lg"] += 16
            g.dma_start(augq_sb[base:base + 5, :], augq_in.ap()).then_inc(sems["lg"], 16)
            V["lg"] += 16
        g.dma_start(onek_sb[:, :], onek_in.ap()).then_inc(sems["lg"], 16)
        V["lg"] += 16
        g.dma_start(id64_sb[:, :], id64_in.ap()).then_inc(sems["lg"], 16)
        V["lg"] += 16
    b0_ld = V["ld"]
    b0_lg = V["lg"]

    close_block()
    if nblocks <= 1:
        return _finish_early(nc, sems, V, sem_cms)

    def emit_ln_stats_all(blk):
        for nch in range(TL // QC):
            @blk.vector
            def _(v, nch=nch):
                if nch in x_ld:
                    v.wait_ge(sems["ld"], x_ld[nch])
                for pt in range(NPT):
                    sl = xb[:, (pt % 2) * QC:(pt % 2) * QC + QC]
                    sq = xsq[:, (pt % 2) * QC:(pt % 2) * QC + QC]
                    k = nch * NPT + pt
                    if k >= 2:
                        v.wait_ge(sems["s_pe"], V["s_pe"] + (k - 2) * 2 + 2 - nch * 2 * NPT)
                    v.tensor_copy(sl, resid[:, pt * TL + nch * QC:
                                            pt * TL + nch * QC + QC])
                    i1 = v.tensor_mul(sq, sl, sl)
                    inc(i1, "s_dv")

            @blk.tensor
            def _(t, nch=nch):
                if nch == 0:
                    t.wait_ge(sems["lg"], b0_lg)
                for pt in range(NPT):
                    t.wait_ge(sems["s_dv"], V["s_dv"] - NPT + pt + 1)
                    if pt == 0 and nch > 0:
                        t.wait_ge(sems["s_dv"], V["s_dv"] - NPT)
                    sl = xb[:, (pt % 2) * QC:(pt % 2) * QC + QC]
                    sq = xsq[:, (pt % 2) * QC:(pt % 2) * QC + QC]
                    i1 = t.matmul(P2[0:1, :], onek_sb[:, :], sl,
                                  start=(pt == 0), stop=(pt == NPT - 1))
                    inc(i1, "s_pe")
                    i2 = t.matmul(P3[0:1, :], onek_sb[:, :], sq,
                                  start=(pt == 0), stop=(pt == NPT - 1))
                    inc(i2, "s_pe")

            @blk.vector
            def _(v, nch=nch):
                v.wait_ge(sems["s_pe"], V["s_pe"])
                c = nch * QC
                v.tensor_copy(stA[:, c:c + QC], P2[0:1, :])
                v.tensor_copy(stB[:, c:c + QC], P3[0:1, :])
                v.tensor_mul(stC[:, c:c + QC], stA[:, c:c + QC], stA[:, c:c + QC])
                v.scalar_tensor_tensor(stB[:, c:c + QC], stB[:, c:c + QC], 1.0,
                                       stC[:, c:c + QC], ALU.mult, ALU.subtract)
                i1 = v.tensor_scalar_add(stB[:, c:c + QC], stB[:, c:c + QC], EPS)
                inc(i1, "s_dv")

        @blk.scalar
        def _(s):
            s.wait_ge(sems["s_dv"], V["s_dv"])
            s.activation(stC[:, :], stB[:, :], AF.Ln, bias=0.0, scale=1.0)
            i1 = s.activation(stB[:, :], stC[:, :], AF.Exp, bias=0.0, scale=-0.5)
            inc(i1, "s_ac")

        @blk.vector
        def _(v):
            v.wait_ge(sems["s_ac"], V["s_ac"])
            i1 = v.tensor_mul(stC[:, :], stA[:, :], stB[:, :])
            inc(i1, "s_dv")

    def emit_ln_bcast(blk, nch, extra_dv_wait=0):
        @blk.tensor
        def _(t):
            t.wait_ge(sems["s_dv"], V["s_dv"] + extra_dv_wait)
            c = nch * QC
            t.matmul(P1[:, 0:QC], ones1_sb[0:1, :], stB[:, c:c + QC],
                     start=True, stop=True)
            i1 = t.matmul(P1[:, QC:2 * QC], ones1_sb[0:1, :], stC[:, c:c + QC],
                          start=True, stop=True)
            inc(i1, "s_pe")

    def emit_ln_apply(blk, nch, dst, dst_col, ld_wait=None):
        pe_target = V["s_pe"]
        gp_base = V["s_gp"]

        @blk.vector
        def _(v):
            v.wait_ge(sems["s_pe"], pe_target)
            v.tensor_copy(mrbs[:, :], P1[:, QC:2 * QC])
            for pt in range(NPT):
                if pt >= 2:
                    v.wait_ge(sems["s_gp"], gp_base + pt - 1)
                src = resid[:, pt * TL + nch * QC:pt * TL + nch * QC + QC]
                i1 = v.tensor_mul(tmpf[:, (pt % 2) * QC:(pt % 2) * QC + QC],
                                  src, P1[:, 0:QC])
                inc(i1, "s_dv")

        @blk.gpsimd
        def _(g):
            g.wait_ge(sems["s_pe"], pe_target)
            if ld_wait is not None:
                g.wait_ge(sems["ld"], ld_wait)
            for pt in range(NPT):
                g.wait_ge(sems["s_dv"], V["s_dv"] - NPT + pt + 1)
                i1 = g.tensor_sub(dst[:, dst_col(pt):dst_col(pt) + QC],
                                  tmpf[:, (pt % 2) * QC:(pt % 2) * QC + QC],
                                  mrbs[:, :])
                inc(i1, "s_gp")

    def emit_ffn_chunk(blk, nch, src_t, src_c0):
        layers = [(wsb[0], src_t, src_c0, h1c), (wsb[1], h1c, 0, h2c),
                  (wsb[2], h2c, 0, None)]
        for li, (w, src, sc0, dst) in enumerate(layers):
            pe_start = V["s_pe"]

            @blk.tensor
            def _(t, w=w, src=src, sc0=sc0, li=li):
                for mt in range(NPT):
                    slot = P0[:, (mt % 2) * QC:(mt % 2) * QC + QC]
                    if mt == 0:
                        t.wait_ge(sems["s_dv"], V["s_dv"])
                        if li == 0:
                            t.wait_ge(sems["s_gp"], V["s_gp"])
                    if mt >= 2:
                        if li == 2:
                            t.wait_ge(sems["s_ac"], V["s_ac"] + mt - 1)
                        else:
                            t.wait_ge(sems["s_dv"], V["s_dv"] + mt - 1)
                    for kc in range(NPT):
                        i1 = t.matmul(slot, w[:, kc * K1 + mt * 128:kc * K1 + mt * 128 + 128],
                                      src[:, sc0 + kc * QC:sc0 + kc * QC + QC],
                                      start=(kc == 0), stop=(kc == NPT - 1))
                    inc(i1, "s_pe")

            if li < 2:
                @blk.vector
                def _(v, dst=dst, pe_start=pe_start):
                    for mt in range(NPT):
                        v.wait_ge(sems["s_pe"], pe_start + mt + 1)
                        slot = P0[:, (mt % 2) * QC:(mt % 2) * QC + QC]
                        i1 = v.tensor_copy(dst[:, mt * QC:mt * QC + QC], slot)
                        inc(i1, "s_dv")
            else:
                @blk.scalar
                def _(s, pe_start=pe_start):
                    for mt in range(NPT):
                        s.wait_ge(sems["s_pe"], pe_start + mt + 1)
                        s.wait_ge(sems["s_dv"], V["s_dv"] + mt)
                        slot = P0[:, (mt % 2) * QC:(mt % 2) * QC + QC]
                        i1 = s.activation(th_sb[:, :], slot, AF.Tanh,
                                          bias=fbias_sb[:, mt:mt + 1], scale=1.0)
                        inc(i1, "s_ac")

                @blk.vector
                def _(v, nch=nch):
                    for mt in range(NPT):
                        v.wait_ge(sems["s_ac"], V["s_ac"] - NPT + mt + 1)
                        dstr = resid[:, mt * TL + nch * QC:mt * TL + nch * QC + QC]
                        i1 = v.scalar_tensor_tensor(dstr, th_sb[:, :], 0.5, dstr,
                                                    ALU.mult, ALU.add)
                        inc(i1, "s_dv")

    blk = open_block()
    emit_ln_stats_all(blk)
    emit_ln_bcast(blk, 0)
    emit_ln_apply(blk, 0, xlnc, lambda pt: pt * QC)
    for nch in range(TL // QC):
        if nch + 1 < TL // QC:
            emit_ln_bcast(blk, nch + 1)
        emit_ffn_chunk(blk, nch, xlnc, (nch % 2) * NPT * QC)
        if nch + 1 < TL // QC:
            p = (nch + 1) % 2
            emit_ln_apply(blk, nch + 1, xlnc, lambda pt, p=p: p * NPT * QC + pt * QC)
    close_block()
    if nblocks <= 2:
        return _finish_early(nc, sems, V, sem_cms)
    blk = open_block()
    emit_ln_stats_all(blk)
    emit_ln_bcast(blk, 0)
    dma_ld_after = {}
    for nch in range(TL // QC):
        buf = lnstage if nch % 2 == 0 else lnstage2
        emit_ln_apply(blk, nch, buf, lambda pt: pt * QC,
                      ld_wait=(dma_ld_after[nch - 2] if nch >= 2 else None))
        if nch + 1 < TL // QC:
            emit_ln_bcast(blk, nch + 1)

        @blk.sync
        def _(sy, nch=nch, buf=buf):
            sy.wait_ge(sems["s_gp"], V["s_gp"])
            ct = cin1a if nch < 2 else cin1b
            i1 = sy.dma_start(
                ct.ap().rearrange("(t p) c -> p t c", p=128)[
                    :, :, (nch % 2) * QC:(nch % 2) * QC + QC],
                buf[:, :].rearrange("p (t c) -> p t c", c=QC))
            inc(i1, "ld", 16)
        dma_ld_after[nch] = V["ld"]

        if nch == 1:
            cc1a_ld = V["ld"]

            @blk.gpsimd
            def _(g, cc1a_ld=cc1a_ld):
                g.wait_ge(sems["ld"], cc1a_ld)
                i1 = g.collective_compute(
                    "AllGather", ALU.bypass,
                    replica_groups=[[0, 1], [2, 3], [4, 5], [6, 7]],
                    ins=[cin1a.ap().opt()], outs=[cout1a.ap().opt()])
                inc(i1, "cc1")
    cc1_ld = V["ld"]

    @blk.sync
    def _(sy):
        sy.wait_ge(sems["s_dv"], V["s_dv"])
        i1 = sy.dma_start(rs_spill.ap(), resid[:, :])
        inc(i1, "ld", 16)

    @blk.gpsimd
    def _(g):
        g.wait_ge(sems["ld"], cc1_ld)
        i1 = g.collective_compute(
            "AllGather", ALU.bypass,
            replica_groups=[[0, 1], [2, 3], [4, 5], [6, 7]],
            ins=[cin1b.ap().opt()], outs=[cout1b.ap().opt()])
        inc(i1, "cc1")

    close_block()
    if nblocks <= 3:
        return _finish_early(nc, sems, V, sem_cms)
    blk = open_block()

    @blk.gpsimd
    def _(g):
        g.dma_start(wq_sb[:, :].rearrange("p (t c) -> p t c", c=512),
                    wq_in.ap().rearrange("(t p) c -> p t c", p=128)).then_inc(sems["lg"], 16)
        V["lg"] += 16
        g.dma_start(wkv_sb[:, :].rearrange("p (t c) -> p t c", c=512),
                    wkv_in.ap().rearrange("(t p) c -> p t c", p=128)).then_inc(sems["lg"], 16)
        V["lg"] += 16

    @blk.sync
    def _(sy):
        sy.wait_ge(sems["cc1"], V["cc1"] - 1)
        for r in range(2):
            i1 = sy.dma_start(
                xlnfull[:, :].rearrange("p (t c) -> p t c", c=TG)[
                    :, :, r * TL:r * TL + TL // 2],
                cout1a.ap()[r * D:(r + 1) * D, :].rearrange("(t p) c -> p t c", p=128))
            inc(i1, "ld", 16)
    xlna_ready = V["ld"]

    @blk.sync
    def _(sy):
        sy.wait_ge(sems["cc1"], V["cc1"])
        for r in range(2):
            i1 = sy.dma_start(
                xlnfull[:, :].rearrange("p (t c) -> p t c", c=TG)[
                    :, :, r * TL + TL // 2:r * TL + TL],
                cout1b.ap()[r * D:(r + 1) * D, :].rearrange("(t p) c -> p t c", p=128))
            inc(i1, "ld", 16)

    @blk.vector
    def _(v):
        i1 = v.memset(vp_sb[:, :], 1.0)
        inc(i1, "s_dv")

    xln_ready = V["ld"]
    vpones = V["s_dv"]

    _kv_mts = [] if b3_parts < 2 else ([0, 1] if b3_parts < 3 else [0, 1, 2, 3])
    evac_dv = {}
    _nch_order = [0, 1, 4, 5, 2, 3, 6, 7]
    for mt in _kv_mts:
        for ni in range(NQC):
            nch = _nch_order[ni]
            pe_t = V["s_pe"]
            step = mt * NQC + ni

            @blk.tensor
            def _(t, mt=mt, nch=nch, ni=ni, step=step):
                if mt == 0 and ni == 0:
                    t.wait_ge(sems["ld"], xlna_ready)
                    t.wait_ge(sems["lg"], V["lg"])
                if mt == 0 and ni == 4:
                    t.wait_ge(sems["ld"], xln_ready)
                slot = P0[:, (nch % 2) * QC:(nch % 2) * QC + QC]
                if step >= 2:
                    t.wait_ge(sems["s_dv"], evac_dv[step - 2])
                for kc in range(NPT):
                    i1 = t.matmul(slot,
                                  wkv_sb[:, kc * 512 + mt * 128:kc * 512 + mt * 128 + 128],
                                  xlnfull[:, kc * TG + nch * QC:kc * TG + nch * QC + QC],
                                  start=(kc == 0), stop=(kc == NPT - 1))
                inc(i1, "s_pe")

            if mt < 2:
                @blk.vector
                def _(v, mt=mt, nch=nch, pe_t=pe_t):
                    v.wait_ge(sems["s_pe"], pe_t + 1)
                    slot = P0[:, (nch % 2) * QC:(nch % 2) * QC + QC]
                    i1 = v.tensor_copy(kf_sb[:, mt * TG + nch * QC:mt * TG + nch * QC + QC],
                                       slot)
                    inc(i1, "s_dv")
                evac_dv[step] = V["s_dv"]
            else:
                @blk.vector
                def _(v, mt=mt, nch=nch, pe_t=pe_t):
                    v.wait_ge(sems["s_pe"], pe_t + 1)
                    slot = P0[:, (nch % 2) * QC:(nch % 2) * QC + QC]
                    i1 = v.tensor_copy(vstage[:, :], slot)
                    inc(i1, "s_dv")
                evac_dv[step] = V["s_dv"]
                dv_t = V["s_dv"]

                @blk.tensor
                def _(t, mt=mt, nch=nch, dv_t=dv_t):
                    t.wait_ge(sems["s_dv"], dv_t)
                    if mt == 2 and nch == 0:
                        t.wait_ge(sems["s_dv"], vpones)
                    for sst in range(4):
                        i1 = t.transpose(PT[:, sst * 128:sst * 128 + 128],
                                         vstage[:, sst * 128:sst * 128 + 128],
                                         id64_sb[:, :])
                    inc(i1, "s_pe")

                @blk.vector
                def _(v, mt=mt, nch=nch):
                    v.wait_ge(sems["s_pe"], V["s_pe"])
                    for half in range(2):
                        g = (mt - 2) * 2 + half
                        st0 = nch * 4
                        i1 = v.tensor_copy(
                            vp_sb[:, :].rearrange("p (a b) -> p a b", b=VPS)[
                                :, g * NST + st0:g * NST + st0 + 4, 0:64],
                            PT[:, :].rearrange("p (a b) -> p a b", b=128)[
                                :, :, half * 64:half * 64 + 64])
                        inc(i1, "s_dv")

    qevac_dv = {}
    for mt in (range(4) if b3_parts >= 4 else []):
        for ni in range(NQC):
            nch = _nch_order[ni]
            pe_t = V["s_pe"]
            step = mt * NQC + ni

            @blk.tensor
            def _(t, mt=mt, nch=nch, step=step):
                slot = P1[:, (nch % 2) * QC:(nch % 2) * QC + QC]
                if step >= 2:
                    t.wait_ge(sems["s_dv"], qevac_dv[step - 2])
                for kc in range(NPT):
                    i1 = t.matmul(slot,
                                  wq_sb[:, kc * 512 + mt * 128:kc * 512 + mt * 128 + 128],
                                  xlnfull[:, kc * TG + nch * QC:kc * TG + nch * QC + QC],
                                  start=(kc == 0), stop=(kc == NPT - 1))
                inc(i1, "s_pe")

            @blk.vector
            def _(v, mt=mt, nch=nch, pe_t=pe_t):
                v.wait_ge(sems["s_pe"], pe_t + 1)
                slot = P1[:, (nch % 2) * QC:(nch % 2) * QC + QC]
                i1 = v.tensor_scalar_mul(q_sb[:, mt * TG + nch * QC:mt * TG + nch * QC + QC],
                                         slot, 0.125)
                inc(i1, "s_dv")
            qevac_dv[step] = V["s_dv"]

    close_block()
    PT_guard.__exit__(None, None, None)
    P4 = nc.alloc_psum_tensor("P4", [65, QC], F32)
    P5 = nc.alloc_psum_tensor("P5", [65, QC], F32)
    if nblocks <= 4:
        return _finish_early(nc, sems, V, sem_cms)
    blk = open_block()

    @blk.sync
    def _(sy):
        i1 = sy.dma_start(resid[:, :], rs_spill.ap())
        inc(i1, "ld", 16)
    resid_back = V["ld"]

    NGRP = NST

    def lin(pr, qcg, st):
        return (pr * 4 + qcg) * NST + st

    devents = []
    dix = {}
    if b4_parts <= 1:
        _pq = []
    elif b4_parts == 2:
        _pq = [(0, 0)]
    elif b4_parts == 3:
        _pq = [(0, 0), (0, 1)]
    elif b4_parts == 4:
        _pq = [(0, q) for q in range(4)]
    else:
        _pq = [(pr, q) for pr in range(4) for q in range(4)]
    for pr, qcg in _pq:
        for st in range(NST):
            for hx in range(2):
                for j in range(2):
                    if _chunk_class(st, qcg * 2 + j) == "D":
                        dix[(pr, qcg, st, hx)] = dix.get((pr, qcg, st, hx))
                        dix[(pr, qcg, st, hx, j)] = len(devents)
                        devents.append((pr, qcg, st, hx, j))

    qk0 = V["s_qk"]
    ex0 = V["s_ex"]
    pv0 = V["s_pv"]
    st0 = V["s_st"]
    r0 = V["s_r"]
    ao0 = V["s_ao"]
    ld0 = V["ld"]
    pe0 = V["s_pe"]
    lnl0 = V["s_lnl"]

    def _exp_of_event(e):
        pe_, pq_, ps_, ph_, pj_ = devents[e]
        return ex0 + 2 * lin(pe_, pq_, ps_) + 1 + ph_

    for pr, qcg in _pq:
        G = pr * 4 + qcg
        L0 = G * NST
        heads = (2 * pr, 2 * pr + 1)
        QKP = (P0, P1)
        PVP = ((P2, P3), (P4, P5))

        @blk.tensor
        def _(t, pr=pr, qcg=qcg, G=G, L0=L0, heads=heads):
            for st in range(NST):
                L = L0 + st
                acls = [_chunk_class(st, qcg * 2 + j) for j in range(2)]
                if L >= 1:
                    t.wait_ge(sems["s_ex"], ex0 + 2 * (L - 1) + 2)
                last = [None, None]
                for j in range(2):
                    for hx in range(2):
                        l = heads[hx]
                        band = (l % 2) * 64
                        g = l % 4
                        lhs = kf_sb[(g % 2) * 64:(g % 2) * 64 + 64,
                                    (g // 2) * TG + st * 128:(g // 2) * TG + st * 128 + 128]
                        rhs = q_sb[band:band + 64,
                                   (l // 2) * TG + (qcg * 2 + j) * QC:
                                   (l // 2) * TG + (qcg * 2 + j) * QC + QC]
                        last[hx] = t.matmul(QKP[hx][:, j * QC:j * QC + QC], lhs, rhs,
                                            start=True, stop=(acls[j] != "A"),
                                            tile_position=(band, 0))
                for j in range(2):
                    if acls[j] != "A":
                        continue
                    qc = qcg * 2 + j
                    for hx in range(2):
                        l = heads[hx]
                        aband = 64 - (l % 2) * 64
                        last[hx] = t.matmul(
                            QKP[hx][:, j * QC:j * QC + QC],
                            augw_sb[aband:aband + 5, l * 128:l * 128 + 128],
                            augq_sb[aband:aband + 5, qc * QC:qc * QC + QC],
                            start=False, stop=True, tile_position=(aband, 0))
                last[0].then_inc(sems["s_qk"], 1)
                last[1].then_inc(sems["s_qk"], 1)
                if st >= 1:
                    for hx in range(2):
                        l = heads[hx]
                        g = l % 4
                        vcol = (g * NST + (st - 1)) * VPS
                        for j in range(2):
                            pvp = PVP[hx][j]
                            if st == 1 and G >= 1:
                                t.wait_ge(sems["s_ao"],
                                          ao0 + 4 * (G - 1) + 2 * hx + j + 1)
                            i1 = t.matmul(
                                pvp[:, :], vp_sb[:, vcol:vcol + 65],
                                pt_sb[:, ((st - 1) % 2) * 2048 + hx * 1024 + j * QC:
                                      ((st - 1) % 2) * 2048 + hx * 1024 + j * QC + QC],
                                start=(st - 1 == 0), stop=False)
                        i1.then_inc(sems["s_pv"], 1)
            for hx in range(2):
                l = heads[hx]
                g = l % 4
                t.wait_ge(sems["s_ex"], ex0 + 2 * (L0 + NST - 1) + 1 + hx)
                vcol = (g * NST + NST - 1) * VPS
                for j in range(2):
                    pvp = PVP[hx][j]
                    i1 = t.matmul(
                        pvp[:, :], vp_sb[:, vcol:vcol + 65],
                        pt_sb[:, ((NST - 1) % 2) * 2048 + hx * 1024 + j * QC:
                              ((NST - 1) % 2) * 2048 + hx * 1024 + j * QC + QC],
                        start=False, stop=True)
                i1.then_inc(sems["s_pv"], 1)
            for hx in range(2):
                t.wait_ge(sems["s_r"], r0 + 2 * G + 1 + hx)
                pva, pvb = PVP[hx]
                t.matmul(pva[0:64, :], ones1_sb[0:1, 0:64], rr_sb[0:1, 0:QC],
                         start=True, stop=True)
                i1 = t.matmul(pvb[0:64, :], ones1_sb[0:1, 0:64],
                              rr_sb[0:1, QC:2 * QC], start=True, stop=True)
                i1.then_inc(sems["s_pe"], 1)
        V["s_qk"] += 2 * NST
        V["s_pv"] += 2 * NST
        V["s_pe"] += 2

        @blk.vector
        def _(v, pr=pr, qcg=qcg, G=G, L0=L0, heads=heads):
            for st in range(NST):
                L = L0 + st
                for hx in range(2):
                    l = heads[hx]
                    qk = QKP[hx]
                    for j in range(2):
                        qc = qcg * 2 + j
                        if _chunk_class(st, qc) != "D":
                            continue
                        e = dix[(pr, qcg, st, hx, j)]
                        v.wait_ge(sems["s_qk"], qk0 + 2 * L + 1 + hx)
                        if e >= 2:
                            v.wait_ge(sems["s_ex"], _exp_of_event(e - 2))
                        d0 = st * 128 - qc * QC
                        i1 = v.scalar_tensor_tensor(
                            dsc[:, (e % 2) * QC:(e % 2) * QC + QC],
                            dtab_sb[:, (d0 // 128) * QC:(d0 // 128) * QC + QC],
                            slop_sb[:, l:l + 1],
                            qk[:, j * QC:j * QC + QC],
                            ALU.mult, ALU.add)
                        inc(i1, "s_st")
            for hx in range(2):
                pva, pvb = PVP[hx]
                v.wait_ge(sems["s_pv"], pv0 + 2 * (L0 + NST - 1) + 1 + hx)
                if G >= 1:
                    v.wait_ge(sems["s_r"], r0 + 2 * (G - 1) + 1 + hx)
                v.tensor_copy(pvt[:, hx * 2 * QC:hx * 2 * QC + QC], pva[:, :])
                i1 = v.tensor_copy(pvt[:, (2 * hx + 1) * QC:(2 * hx + 1) * QC + QC],
                                   pvb[:, :])
                inc(i1, "s_lnl")
                v.wait_ge(sems["s_r"], r0 + 2 * G + 1 + hx)
                v.wait_ge(sems["s_pe"], pe0 + 2 * G + 1 + hx)
                if G >= 1:
                    v.wait_ge(sems["ld"], ld0 + 32 * G)
                v.tensor_copy(rbs[:, :], pva[0:64, :])
                v.scalar_tensor_tensor(aost[:, hx * 2 * QC:hx * 2 * QC + QC],
                                       pvt[0:64, hx * 2 * QC:hx * 2 * QC + QC],
                                       1.0, rbs[:, :],
                                       ALU.mult, ALU.mult).then_inc(sems["s_ao"], 1)
                v.tensor_copy(rbs[:, :], pvb[0:64, :])
                v.scalar_tensor_tensor(aost[:, (hx * 2 + 1) * QC:(hx * 2 + 1) * QC + QC],
                                       pvt[0:64, (2 * hx + 1) * QC:(2 * hx + 1) * QC + QC],
                                       1.0, rbs[:, :],
                                       ALU.mult, ALU.mult).then_inc(sems["s_ao"], 1)
        V["s_st"] = st0 + len([1 for e in devents if (e[0] * 4 + e[1]) <= G])
        V["s_ao"] += 4
        V["s_lnl"] += 2

        V["s_ex"] += 2 * NST

        @blk.sync
        def _(sy, pr=pr, qcg=qcg, G=G):
            half = qcg // 2
            colo = (qcg % 2) * 1024
            for hx in range(2):
                l = 2 * pr + hx
                snd = snd2a if l < 4 else snd2b
                row = half * 256 + (l % 4) * 64
                sy.wait_ge(sems["s_ao"], ao0 + 4 * G + 2 * hx + 2)
                i1 = sy.dma_start(
                    snd.ap()[row:row + 64, colo:colo + 1024],
                    aost[:, hx * 2 * QC:hx * 2 * QC + 2 * QC])
                inc(i1, "ld", 16)

        if G == 7:
            @blk.gpsimd
            def _(g):
                if b4_parts >= 8:
                    g.wait_ge(sems["ld"], V["ld"])
                    i1 = g.collective_compute(
                        "AllGather", ALU.bypass,
                        replica_groups=[[0, 1], [2, 3], [4, 5], [6, 7]],
                        ins=[snd2a.ap().opt()], outs=[rcv2a.ap().opt()])
                    inc(i1, "cc2")

    @blk.scalar
    def _(s):
        for pr, qcg in _pq:
            G = pr * 4 + qcg
            L0 = G * NST
            heads = (2 * pr, 2 * pr + 1)
            for st in range(NST):
                L = L0 + st
                c0 = _chunk_class(st, qcg * 2)
                c1 = _chunk_class(st, qcg * 2 + 1)
                if c0 == "B" and c1 == "B":
                    s.wait_ge(sems["s_qk"], qk0 + 2 * L + 2)
                    if L >= 2:
                        s.wait_ge(sems["s_pv"], pv0 + 2 * (L - 2) + 2)
                    pb = (st % 2) * 2048
                    s.activation(pt_sb[:, pb:pb + 2048], P01[:, :], AF.Exp,
                                 bias=negc_sb[:, 0:1], scale=1.0).then_inc(sems["s_ex"], 2)
                    continue
                for hx in range(2):
                    l = heads[hx]
                    qk = (P0, P1)[hx]
                    s.wait_ge(sems["s_qk"], qk0 + 2 * L + 1 + hx)
                    if L >= 2:
                        s.wait_ge(sems["s_pv"], pv0 + 2 * (L - 2) + 1 + hx)
                    pbase = (st % 2) * 2048 + hx * 1024
                    pdst = pt_sb[:, pbase:pbase + 1024]
                    if c0 == c1 and c0 != "D":
                        if c0 == "A":
                            bias = biasc_sb[:, l * NST + st:l * NST + st + 1]
                        else:
                            bias = negc_sb[:, 0:1]
                        i1 = s.activation(pdst, qk[:, :], AF.Exp, bias=bias, scale=1.0)
                    else:
                        for j in range(2):
                            qc = qcg * 2 + j
                            cls = _chunk_class(st, qc)
                            dd = pt_sb[:, pbase + j * QC:pbase + j * QC + QC]
                            if cls == "D":
                                e = dix[(pr, qcg, st, hx, j)]
                                s.wait_ge(sems["s_st"], st0 + e + 1)
                                i1 = s.activation(dd, dsc[:, (e % 2) * QC:(e % 2) * QC + QC],
                                                  AF.Exp, bias=negc_sb[:, 0:1], scale=1.0)
                            elif cls == "A":
                                i1 = s.activation(dd, qk[:, j * QC:j * QC + QC], AF.Exp,
                                                  bias=biasc_sb[:, l * NST + st:l * NST + st + 1],
                                                  scale=1.0)
                            else:
                                i1 = s.activation(dd, qk[:, j * QC:j * QC + QC], AF.Exp,
                                                  bias=negc_sb[:, 0:1], scale=1.0)
                    i1.then_inc(sems["s_ex"], 1)
            for hx in range(2):
                s.wait_ge(sems["s_lnl"], lnl0 + 2 * G + 1 + hx)
                s.wait_ge(sems["s_pe"], pe0 + 2 * G + hx)
                s.activation(lnt_att[:, :], pvt[64:65, hx * 2 * QC:hx * 2 * QC + 2 * QC],
                             AF.Ln, bias=0.0, scale=1.0)
                s.activation(rr_sb[:, :], lnt_att[:, :], AF.Exp,
                             bias=0.0, scale=-1.0).then_inc(sems["s_r"], 1)

    V["s_r"] += 2 * len(_pq)

    @blk.gpsimd
    def _(g):
        if b4_parts < 8:
            return
        g.wait_ge(sems["ld"], V["ld"])
        i1 = g.collective_compute(
            "AllGather", ALU.bypass,
            replica_groups=[[0, 1], [2, 3], [4, 5], [6, 7]],
            ins=[snd2b.ap().opt()], outs=[rcv2b.ap().opt()])
        inc(i1, "cc2")

    close_block()
    if nblocks <= 5:
        return _finish_early(nc, sems, V, sem_cms)
    blk = open_block()

    @blk.gpsimd
    def _(g):
        i1 = g.dma_start(wpj_sb[:, :].rearrange("p (t c) -> p t c", c=K1),
                         wproj_in.ap().rearrange("(t p) c -> p t c", p=128))
        inc(i1, "lg", 16)

    _ptorder = [0, 1, 4, 5, 2, 3, 6, 7]

    @blk.sync
    def _(sy):
        for i, pt in enumerate(_ptorder):
            if i == 0:
                sy.wait_ge(sems["cc2"], V["cc2"] - 1)
            if i == 4:
                sy.wait_ge(sems["cc2"], V["cc2"])
            rcv = rcv2a if pt in (0, 1, 4, 5) else rcv2b
            lp = pt % 2 if pt < 4 else (pt - 4) % 2
            r0 = (0 if pt < 4 else 512) + (pt % 4 % 2) * 0 + lp * 128
            r0 = (512 if pt >= 4 else 0) + lp * 128
            r1 = r0 + 256
            sy.dma_start(aoc0[:, :], rcv.ap()[r0:r0 + 128, :]).then_inc(sems["ld"], 16)
            V["ld"] += 16
            sy.dma_start(aoc1[:, :], rcv.ap()[r1:r1 + 128, :]).then_inc(sems["ld"], 16)
            V["ld"] += 16
            sy.wait_ge(sems["s_dv"], V["s_dv"] + i + 1)

    @blk.vector
    def _(v):
        base_ld = V["ld"] - 32 * NPT
        for i, pt in enumerate(_ptorder):
            v.wait_ge(sems["ld"], base_ld + (i + 1) * 32)
            v.tensor_scalar_mul(aot2[:, :], aoc0[:, :], sel_sb[:, 0:1])
            i1 = v.scalar_tensor_tensor(ao_sb[:, pt * TL:pt * TL + TL],
                                        aoc1[:, :], sel_sb[:, 1:2], aot2[:, :],
                                        ALU.mult, ALU.add)
            inc(i1, "s_dv")

    ao_ready_dv = V["s_dv"]
    wpj_ready = V["lg"]

    for mt in range(NPT):
        for nch in range(TL // QC):
            pe_t = V["s_pe"]

            @blk.tensor
            def _(t, mt=mt, nch=nch):
                if mt == 0 and nch == 0:
                    t.wait_ge(sems["s_dv"], ao_ready_dv)
                    t.wait_ge(sems["lg"], wpj_ready)
                slot = P0[:, (nch % 2) * QC:(nch % 2) * QC + QC]
                if mt * (TL // QC) + nch >= 2:
                    t.wait_ge(sems["s_dv"], V["s_dv"] - 1)
                for kc in range(NPT):
                    i1 = t.matmul(slot,
                                  wpj_sb[:, kc * K1 + mt * 128:kc * K1 + mt * 128 + 128],
                                  ao_sb[:, kc * TL + nch * QC:kc * TL + nch * QC + QC],
                                  start=(kc == 0), stop=(kc == NPT - 1))
                inc(i1, "s_pe")

            @blk.vector
            def _(v, mt=mt, nch=nch, pe_t=pe_t):
                v.wait_ge(sems["s_pe"], pe_t + 1)
                if mt == 0 and nch == 0:
                    v.wait_ge(sems["ld"], resid_back)
                slot = P0[:, (nch % 2) * QC:(nch % 2) * QC + QC]
                dstr = resid[:, mt * TL + nch * QC:mt * TL + nch * QC + QC]
                i1 = v.scalar_tensor_tensor(dstr, slot, bproj_sb[:, mt:mt + 1],
                                            dstr, ALU.add, ALU.add)
                inc(i1, "s_dv")

    close_block()
    if nblocks <= 6:
        return _finish_early(nc, sems, V, sem_cms)
    blk = open_block()

    @blk.gpsimd
    def _(g):
        for i in range(3):
            i1 = g.dma_start(wsb[i][:, :].rearrange("p (t c) -> p t c", c=K1),
                             w_ffn[i].ap().rearrange("(t p) c -> p t c", p=128))
            inc(i1, "lg", 16)
    w_ready = V["lg"]

    emit_ln_stats_all(blk)
    emit_ln_bcast(blk, 0)
    emit_ln_apply(blk, 0, xlnc, lambda pt: pt * QC)

    @blk.tensor
    def _(t):
        t.wait_ge(sems["lg"], w_ready)
    for nch in range(TL // QC):
        if nch + 1 < TL // QC:
            emit_ln_bcast(blk, nch + 1)
        emit_ffn_chunk(blk, nch, xlnc, (nch % 2) * NPT * QC)
        if nch + 1 < TL // QC:
            p = (nch + 1) % 2
            emit_ln_apply(blk, nch + 1, xlnc, lambda pt, p=p: p * NPT * QC + pt * QC)

        @blk.sync
        def _(sy, nch=nch):
            sy.wait_ge(sems["s_dv"], V["s_dv"])
            i1 = sy.dma_start(
                y_out.ap().rearrange("(t p) c -> p t c", p=128)[:, :, nch * QC:nch * QC + QC],
                resid[:, :].rearrange("p (t c) -> p t c", c=TL)[:, :, nch * QC:nch * QC + QC])
            inc(i1, "ld", 16)

    @blk.sync
    def _(sy):
        sy.wait_ge(sems["ld"], V["ld"])

    close_block()
    if nblocks <= 7:
        return _finish_early(nc, sems, V, sem_cms)
    for cm in reversed(sem_cms):
        cm.__exit__(None, None, None)
    return nc


def _core_tables(parity):
    start = 2.0 ** (-2.0 ** (-(np.log2(16) - 3.0)))
    slopes = start ** np.arange(1, 17, dtype=np.float64)
    gh = np.arange(parity * 8, parity * 8 + 8)
    sl = slopes[gh].astype(np.float32)

    augw = np.zeros((5, NHL * 128), np.float32)
    for l in range(NHL):
        s = float(sl[l])
        s_hi = float(BF(s))
        s_lo = s - s_hi
        m = np.arange(128, dtype=np.float64)
        mv = (s * m / 8.0).astype(np.float32)
        m_hi = BF(mv)
        m_lo = (mv - m_hi).astype(np.float32)
        augw[0, l * 128:(l + 1) * 128] = s_hi
        augw[1, l * 128:(l + 1) * 128] = s_hi
        augw[2, l * 128:(l + 1) * 128] = s_lo
        augw[3, l * 128:(l + 1) * 128] = m_hi
        augw[4, l * 128:(l + 1) * 128] = m_lo

    biasc = np.zeros((128, NHL * NST), np.float32)
    for l in range(NHL):
        for st in range(NST):
            biasc[:, l * NST + st] = sl[l] * (st * 128) / 8.0 - CSHIFT

    slop = np.tile(sl[None, :], (128, 1)).astype(np.float32)
    return augw, biasc, slop


def _shared_tables():
    q = np.arange(TG, dtype=np.float64) / 8.0
    qh = BF(-q)
    ql = (-q - qh).astype(np.float32)
    assert np.all(BF(ql) == ql), "q_lo must be bf16-exact"
    augq = np.zeros((5, TG), np.float32)
    augq[0] = qh
    augq[1] = ql
    augq[2] = qh
    augq[3] = 1.0
    augq[4] = 1.0

    dtab = np.zeros((128, 4 * QC), np.float32)
    for jb in range(4):
        d0 = jb * 128
        p = np.arange(128)[:, None]
        f = np.arange(QC)[None, :]
        dd = d0 + p - f
        dtab[:, jb * QC:(jb + 1) * QC] = np.where(dd <= 0, dd / 8.0, 0.0)
    return augq, dtab


_NC_CACHE = {}


def kernel(L0_attn_x, L0_attn_Wq, L0_attn_Wkv, L0_attn_Wproj, L0_attn_bproj,
           L0_ffn_weight, L0_ffn_aux_weight1, L0_ffn_aux_weight2, L0_ffn_bias):
    if "nc" not in _NC_CACHE:
        _NC_CACHE["nc"] = build_program()
    nc = _NC_CACHE["nc"]

    x = np.asarray(L0_attn_x, np.float32)
    Wq = np.asarray(L0_attn_Wq, np.float32)
    Wkv = np.asarray(L0_attn_Wkv, np.float32)
    Wproj = np.asarray(L0_attn_Wproj, np.float32)
    bproj = np.asarray(L0_attn_bproj, np.float32)
    Wf = np.asarray(L0_ffn_weight, np.float32)
    A1 = np.asarray(L0_ffn_aux_weight1, np.float32)
    A2 = np.asarray(L0_ffn_aux_weight2, np.float32)
    fb = np.asarray(L0_ffn_bias, np.float32)

    wq_t = np.ascontiguousarray(Wq.T)
    wkv_t = np.ascontiguousarray(Wkv.T)
    wproj_t = np.ascontiguousarray(Wproj.T)
    bproj_c = np.ascontiguousarray(bproj.reshape(NPT, 128).T)
    fb_c = np.ascontiguousarray(fb.reshape(NPT, 128).T)
    onek = np.full((128, 1), 1.0 / 1024.0, np.float32)
    negc = np.full((128, 1), -CSHIFT, np.float32)
    sels = [np.tile(np.array([[1.0, 0.0]], np.float32), (128, 1)),
            np.tile(np.array([[0.0, 1.0]], np.float32), (128, 1))]
    ones1 = np.ones((1, 128), np.float32)
    id64 = np.eye(128, dtype=np.float32)
    augq, dtab = _shared_tables()
    tabs = [_core_tables(0), _core_tables(1)]

    in_maps = []
    for c in range(8):
        b, half = c // 2, c % 2
        t0 = half * TL
        augw, biasc, slop = tabs[half]
        wq_c = np.ascontiguousarray(wq_t[:, half * 512:half * 512 + 512])
        in_maps.append({
            "x": np.ascontiguousarray(x[b, t0:t0 + TL, :].T),
            "wffn0": Wf, "wffn1": A1, "wffn2": A2,
            "wq": wq_c, "wkv": wkv_t, "wproj": wproj_t,
            "bproj": bproj_c, "fbias": fb_c,
            "augw": augw, "augq": augq, "biasc": biasc, "dtab": dtab,
            "slop": slop, "onek": onek, "ones1": ones1, "id64": id64,
            "negc": negc, "sel": sels[half],
        })

    res = run_bass_kernel_spmd(nc, in_maps, core_ids=list(range(8)))
    out = np.empty((4, TG, D), np.float32)
    for c in range(8):
        b, half = c // 2, c % 2
        t0 = half * TL
        out[b, t0:t0 + TL, :] = res.results[c]["y"].T
    return out


# version 22 (final)
# speedup vs baseline: 1.0216x; 1.0024x over previous
import numpy as np
import ml_dtypes
import concourse.bass as bass
import concourse.mybir as mybir
from concourse.bass_utils import run_bass_kernel_spmd

F32 = mybir.dt.float32
BF16 = mybir.dt.bfloat16
AF = mybir.ActivationFunctionType
ALU = mybir.AluOpType

D = 1024
NPT = 8
TL = 2048
TG = 4096
HD = 64
NHL = 8
KVH = 4
QC = 512
NQC = TG // QC
NST = TG // 128
CSHIFT = 15.0
EPS = 1e-5

BF = lambda a: np.asarray(a, np.float32).astype(ml_dtypes.bfloat16).astype(np.float32)


def _chunk_class(st, qc):
    d0 = st * 128 - qc * QC
    if d0 <= -128:
        return "A"
    if d0 >= QC:
        return "B"
    return "D"



def _finish_early(nc, sems, V, sem_cms):
    b = nc.Block()
    blk = b.__enter__()

    @blk.sync
    def _(sy):
        sy.wait_ge(sems["ld"], V["ld"])

    @blk.gpsimd
    def _(g):
        g.wait_ge(sems["lg"], V["lg"])
        g.wait_ge(sems["cc1"], V["cc1"])
        g.wait_ge(sems["cc2"], V["cc2"])

    @blk.vector
    def _(v):
        v.wait_ge(sems["s_dv"], V["s_dv"])

    @blk.scalar
    def _(s):
        s.wait_ge(sems["s_ac"], V["s_ac"])

    @blk.tensor
    def _(t):
        t.wait_ge(sems["s_pe"], V["s_pe"])
    b.__exit__(None, None, None)
    for cm in reversed(sem_cms):
        cm.__exit__(None, None, None)
    return nc


def build_program(nblocks=99, b3_parts=9, b4_parts=9):
    nc = bass.Bass("TRN2", target_bir_lowering=False, detect_race_conditions=False)

    x_in = nc.dram_tensor("x", [D, TL], F32, kind="ExternalInput")
    w_ffn = [nc.dram_tensor(n, [D, D], F32, kind="ExternalInput")
             for n in ("wffn0", "wffn1", "wffn2")]
    wq_in = nc.dram_tensor("wq", [D, NHL * HD], F32, kind="ExternalInput")
    wkv_in = nc.dram_tensor("wkv", [D, 512], F32, kind="ExternalInput")
    wproj_in = nc.dram_tensor("wproj", [D, D], F32, kind="ExternalInput")
    bproj_in = nc.dram_tensor("bproj", [128, NPT], F32, kind="ExternalInput")
    fbias_in = nc.dram_tensor("fbias", [128, NPT], F32, kind="ExternalInput")
    augw_in = nc.dram_tensor("augw", [5, NHL * 128], F32, kind="ExternalInput")
    augq_in = nc.dram_tensor("augq", [5, TG], F32, kind="ExternalInput")
    biasc_in = nc.dram_tensor("biasc", [128, NHL * NST], F32, kind="ExternalInput")
    dtab_in = nc.dram_tensor("dtab", [128, 4 * QC], F32, kind="ExternalInput")
    slop_in = nc.dram_tensor("slop", [128, NHL], F32, kind="ExternalInput")
    onek_in = nc.dram_tensor("onek", [128, 1], F32, kind="ExternalInput")
    ones1_in = nc.dram_tensor("ones1", [1, 128], F32, kind="ExternalInput")
    id64_in = nc.dram_tensor("id64", [128, 128], F32, kind="ExternalInput")
    negc_in = nc.dram_tensor("negc", [128, 1], F32, kind="ExternalInput")
    sel_in = nc.dram_tensor("sel", [128, 2], F32, kind="ExternalInput")

    y_out = nc.dram_tensor("y", [D, TL], F32, kind="ExternalOutput")

    rs_spill = nc.dram_tensor("rs_spill", [128, NPT * TL], F32)
    cin1a = nc.dram_tensor("cin1a", [D, TL // 2], BF16)
    cin1b = nc.dram_tensor("cin1b", [D, TL // 2], BF16)
    cout1a = nc.dram_tensor("cout1a", [2 * D, TL // 2], BF16)
    cout1b = nc.dram_tensor("cout1b", [2 * D, TL // 2], BF16)
    snd2a = nc.dram_tensor("snd2a", [512, TL], BF16)
    snd2b = nc.dram_tensor("snd2b", [256, TL], BF16)
    snd2c = nc.dram_tensor("snd2c", [256, TL], BF16)
    rcv2a = nc.dram_tensor("rcv2a", [1024, TL], BF16)
    rcv2b = nc.dram_tensor("rcv2b", [512, TL], BF16)
    rcv2c = nc.dram_tensor("rcv2c", [512, TL], BF16)

    SB = {}

    SBUF_BASE = 17 * 1024

    def sb(name, off, shape, dtype):
        off = off + SBUF_BASE
        t = nc.alloc_sbuf_tensor_at(name, list(shape), dtype, offset=off)
        SB[name] = (off, shape, dtype)
        return t

    K1 = 1024
    resid = sb("resid", 0, [128, NPT * TL], F32)
    xlnfull = sb("xlnfull", 0, [128, NPT * TG], BF16)

    wsb = [sb(f"wsb{i}", 64 * K1 + i * 16 * K1, [128, NPT * K1], BF16)
           for i in range(3)]
    q_sb = sb("q_sb", 64 * K1, [128, 4 * TG], BF16)
    kf_sb = sb("kf_sb", 96 * K1, [128, 2 * TG], BF16)
    ao_sb = sb("ao_sb", 64 * K1, [128, NPT * TL], BF16)

    h1c = sb("h1c", 112 * K1, [128, NPT * QC], BF16)
    h2c = sb("h2c", 120 * K1, [128, NPT * QC], BF16)
    xb = sb("xb", 128 * K1, [128, 2 * QC], BF16)
    xsq = sb("xsq", 130 * K1, [128, 2 * QC], BF16)
    VPS = 68
    vp_sb = sb("vp_sb", 112 * K1, [128, KVH * NST * VPS], BF16)
    wpj_sb = sb("wpj_sb", 112 * K1, [128, NPT * K1], BF16)

    wq_sb = sb("wq_sb", 132 * K1, [128, NPT * 512], BF16)
    wkv_sb = sb("wkv_sb", 140 * K1, [128, NPT * 512], BF16)
    xlnc = sb("xlnc", 132 * K1, [128, 2 * NPT * QC], BF16)
    pt_sb = sb("pt_sb", 132 * K1, [128, 4 * 1024], BF16)
    dsc = sb("dsc", 140 * K1, [128, 2 * QC], F32)
    rr_sb = sb("rr_sb", 144 * K1, [1, 2 * QC], F32)
    aost = sb("aost", 150 * K1, [64, 4 * QC], BF16)
    lnl_sb = sb("lnl_sb", 160 * K1, [1, 2 * QC], F32)
    lnt_att = sb("lnt_att", 168 * K1, [1, 2 * QC], F32)
    rbs = sb("rbs", 154 * K1, [64, QC], F32)

    lnstage = sb("lnstage", 148 * K1, [128, NPT * QC], BF16)
    vstage = sb("vstage", 148 * K1, [128, QC], BF16)
    tmpf = sb("tmpf", 156 * K1, [128, 2 * QC], F32)
    mrbs = sb("mrbs", 128 * K1, [128, QC], F32)
    th_sb = sb("th_sb", 148 * K1, [128, QC], BF16)
    tab_off = 185 * K1

    stA = sb("stA", 160 * K1, [1, 4 * QC], F32)
    stB = sb("stB", 168 * K1, [1, 4 * QC], F32)
    stC = sb("stC", 176 * K1, [1, 4 * QC], F32)
    pvt = sb("pvt", 176 * K1, [65, 4 * QC], F32)
    lnstage2 = sb("lnstage2", 132 * K1, [128, NPT * QC], BF16)

    augw_sb = sb("augw_sb", tab_off, [69, NHL * 128], BF16)
    augq_sb = sb("augq_sb", tab_off + 2 * K1, [69, TG], BF16)
    biasc_sb = sb("biasc_sb", tab_off + 10 * K1, [128, NHL * NST], F32)
    dtab_sb = sb("dtab_sb", tab_off + 11 * K1, [128, 4 * QC], F32)
    slop_sb = sb("slop_sb", tab_off + 19 * K1, [128, NHL], F32)
    onek_sb = sb("onek_sb", tab_off + 19 * K1 + 64, [128, 1], BF16)
    ones1_sb = sb("ones1_sb", tab_off + 19 * K1 + 128, [1, 128], F32)
    id64_sb = sb("id64_sb", tab_off + 20 * K1, [128, 128], BF16)
    bproj_sb = sb("bproj_sb", tab_off + 20 * K1 + 256, [128, NPT], F32)
    fbias_sb = sb("fbias_sb", tab_off + 20 * K1 + 512, [128, NPT], F32)
    negc_sb = sb("negc_sb", tab_off + 20 * K1 + 768, [128, 1], F32)
    sel_sb = sb("sel_sb", tab_off + 20 * K1 + 832, [128, 2], F32)
    aoc0 = sb("aoc0", 132 * K1, [128, TL], BF16)
    aoc1 = sb("aoc1", 136 * K1, [128, TL], BF16)
    aot2 = sb("aot2", 140 * K1, [128, TL], BF16)
    assert tab_off + 21 * K1 + 17 * K1 <= 229344 // 1024 * K1

    P01 = nc.alloc_psum_tensor("P01", [128, 2048], F32)
    P0 = P01[:, 0:1024]
    P1 = P01[:, 1024:2048]
    P2 = nc.alloc_psum_tensor("P2", [65, QC], F32)
    P3 = nc.alloc_psum_tensor("P3", [65, QC], F32)
    PT_guard = nc.psum_tensor("PT", [128, 512], BF16)
    PT = PT_guard.__enter__()

    semnames = ["ld", "lg", "cc1", "cc2", "s_pe", "s_ac", "s_dv", "s_gp",
                "s_qk", "s_ex", "s_pv", "s_st", "s_r", "s_ao", "s_lnl", "s_fin"]
    sems = {}
    sem_cms = []
    for n in semnames:
        cm = nc.semaphore(n)
        sems[n] = cm.__enter__()
        sem_cms.append(cm)
    V = {n: 0 for n in semnames}

    def inc(inst, name, n=1):
        inst.then_inc(sems[name], n)
        V[name] += n

    blocks = []

    def open_block():
        b = nc.Block()
        bb = b.__enter__()
        blocks.append(b)
        return bb

    def close_block(b=None):
        blocks.pop().__exit__(None, None, None)

    blk = open_block()

    x_ld = {}

    @blk.sync
    def _(sy):
        for t_in, t_sb in ((onek_in, None), (ones1_in, ones1_sb)):
            pass
        for nch in range(TL // QC):
            sy.dma_start(
                resid[:, :].rearrange("p (t c) -> p t c", c=TL)[:, :, nch * QC:nch * QC + QC],
                x_in.ap().rearrange("(t p) c -> p t c", p=128)[:, :, nch * QC:nch * QC + QC],
            ).then_inc(sems["ld"], 16)
            V["ld"] += 16
            x_ld[nch] = V["ld"]
        for t_in, t_sb in ((bproj_in, bproj_sb), (fbias_in, fbias_sb),
                           (biasc_in, biasc_sb), (dtab_in, dtab_sb),
                           (slop_in, slop_sb), (ones1_in, ones1_sb),
                           (negc_in, negc_sb), (sel_in, sel_sb)):
            sy.dma_start(t_sb[:, :], t_in.ap()).then_inc(sems["ld"], 16)
            V["ld"] += 16

    @blk.gpsimd
    def _(g):
        for i in range(3):
            g.dma_start(wsb[i][:, :].rearrange("p (t c) -> p t c", c=K1),
                        w_ffn[i].ap().rearrange("(t p) c -> p t c", p=128)).then_inc(sems["lg"], 16)
            V["lg"] += 16
        for base in (0, 64):
            g.dma_start(augw_sb[base:base + 5, :], augw_in.ap()).then_inc(sems["lg"], 16)
            V["lg"] += 16
            g.dma_start(augq_sb[base:base + 5, :], augq_in.ap()).then_inc(sems["lg"], 16)
            V["lg"] += 16
        g.dma_start(onek_sb[:, :], onek_in.ap()).then_inc(sems["lg"], 16)
        V["lg"] += 16
        g.dma_start(id64_sb[:, :], id64_in.ap()).then_inc(sems["lg"], 16)
        V["lg"] += 16
    b0_ld = V["ld"]
    b0_lg = V["lg"]

    close_block()
    if nblocks <= 1:
        return _finish_early(nc, sems, V, sem_cms)

    def emit_ln_stats_all(blk):
        for nch in range(TL // QC):
            @blk.vector
            def _(v, nch=nch):
                if nch in x_ld:
                    v.wait_ge(sems["ld"], x_ld[nch])
                for pt in range(NPT):
                    sl = xb[:, (pt % 2) * QC:(pt % 2) * QC + QC]
                    sq = xsq[:, (pt % 2) * QC:(pt % 2) * QC + QC]
                    k = nch * NPT + pt
                    if k >= 2:
                        v.wait_ge(sems["s_pe"], V["s_pe"] + (k - 2) * 2 + 2 - nch * 2 * NPT)
                    v.tensor_copy(sl, resid[:, pt * TL + nch * QC:
                                            pt * TL + nch * QC + QC])
                    i1 = v.tensor_mul(sq, sl, sl)
                    inc(i1, "s_dv")

            @blk.tensor
            def _(t, nch=nch):
                if nch == 0:
                    t.wait_ge(sems["lg"], b0_lg)
                for pt in range(NPT):
                    t.wait_ge(sems["s_dv"], V["s_dv"] - NPT + pt + 1)
                    if pt == 0 and nch > 0:
                        t.wait_ge(sems["s_dv"], V["s_dv"] - NPT)
                    sl = xb[:, (pt % 2) * QC:(pt % 2) * QC + QC]
                    sq = xsq[:, (pt % 2) * QC:(pt % 2) * QC + QC]
                    i1 = t.matmul(P2[0:1, :], onek_sb[:, :], sl,
                                  start=(pt == 0), stop=(pt == NPT - 1))
                    inc(i1, "s_pe")
                    i2 = t.matmul(P3[0:1, :], onek_sb[:, :], sq,
                                  start=(pt == 0), stop=(pt == NPT - 1))
                    inc(i2, "s_pe")

            @blk.vector
            def _(v, nch=nch):
                v.wait_ge(sems["s_pe"], V["s_pe"])
                c = nch * QC
                v.tensor_copy(stA[:, c:c + QC], P2[0:1, :])
                v.tensor_copy(stB[:, c:c + QC], P3[0:1, :])
                v.tensor_mul(stC[:, c:c + QC], stA[:, c:c + QC], stA[:, c:c + QC])
                v.scalar_tensor_tensor(stB[:, c:c + QC], stB[:, c:c + QC], 1.0,
                                       stC[:, c:c + QC], ALU.mult, ALU.subtract)
                i1 = v.tensor_scalar_add(stB[:, c:c + QC], stB[:, c:c + QC], EPS)
                inc(i1, "s_dv")

        @blk.scalar
        def _(s):
            s.wait_ge(sems["s_dv"], V["s_dv"])
            s.activation(stC[:, :], stB[:, :], AF.Ln, bias=0.0, scale=1.0)
            i1 = s.activation(stB[:, :], stC[:, :], AF.Exp, bias=0.0, scale=-0.5)
            inc(i1, "s_ac")

        @blk.vector
        def _(v):
            v.wait_ge(sems["s_ac"], V["s_ac"])
            i1 = v.tensor_mul(stC[:, :], stA[:, :], stB[:, :])
            inc(i1, "s_dv")

    def emit_ln_bcast(blk, nch, extra_dv_wait=0):
        @blk.tensor
        def _(t):
            t.wait_ge(sems["s_dv"], V["s_dv"] + extra_dv_wait)
            c = nch * QC
            t.matmul(P1[:, 0:QC], ones1_sb[0:1, :], stB[:, c:c + QC],
                     start=True, stop=True)
            i1 = t.matmul(P1[:, QC:2 * QC], ones1_sb[0:1, :], stC[:, c:c + QC],
                          start=True, stop=True)
            inc(i1, "s_pe")

    def emit_ln_apply(blk, nch, dst, dst_col, ld_wait=None):
        pe_target = V["s_pe"]
        gp_base = V["s_gp"]

        @blk.vector
        def _(v):
            v.wait_ge(sems["s_pe"], pe_target)
            v.tensor_copy(mrbs[:, :], P1[:, QC:2 * QC])
            for pt in range(NPT):
                if pt >= 2:
                    v.wait_ge(sems["s_gp"], gp_base + pt - 1)
                src = resid[:, pt * TL + nch * QC:pt * TL + nch * QC + QC]
                i1 = v.tensor_mul(tmpf[:, (pt % 2) * QC:(pt % 2) * QC + QC],
                                  src, P1[:, 0:QC])
                inc(i1, "s_dv")

        @blk.gpsimd
        def _(g):
            g.wait_ge(sems["s_pe"], pe_target)
            if ld_wait is not None:
                g.wait_ge(sems["ld"], ld_wait)
            for pt in range(NPT):
                g.wait_ge(sems["s_dv"], V["s_dv"] - NPT + pt + 1)
                i1 = g.tensor_sub(dst[:, dst_col(pt):dst_col(pt) + QC],
                                  tmpf[:, (pt % 2) * QC:(pt % 2) * QC + QC],
                                  mrbs[:, :])
                inc(i1, "s_gp")

    def emit_ffn_chunk(blk, nch, src_t, src_c0):
        layers = [(wsb[0], src_t, src_c0, h1c), (wsb[1], h1c, 0, h2c),
                  (wsb[2], h2c, 0, None)]
        for li, (w, src, sc0, dst) in enumerate(layers):
            pe_start = V["s_pe"]

            @blk.tensor
            def _(t, w=w, src=src, sc0=sc0, li=li):
                for mt in range(NPT):
                    slot = P0[:, (mt % 2) * QC:(mt % 2) * QC + QC]
                    if mt == 0:
                        t.wait_ge(sems["s_dv"], V["s_dv"])
                        if li == 0:
                            t.wait_ge(sems["s_gp"], V["s_gp"])
                    if mt >= 2:
                        if li == 2:
                            t.wait_ge(sems["s_ac"], V["s_ac"] + mt - 1)
                        else:
                            t.wait_ge(sems["s_dv"], V["s_dv"] + mt - 1)
                    for kc in range(NPT):
                        i1 = t.matmul(slot, w[:, kc * K1 + mt * 128:kc * K1 + mt * 128 + 128],
                                      src[:, sc0 + kc * QC:sc0 + kc * QC + QC],
                                      start=(kc == 0), stop=(kc == NPT - 1))
                    inc(i1, "s_pe")

            if li < 2:
                @blk.vector
                def _(v, dst=dst, pe_start=pe_start):
                    for mt in range(NPT):
                        v.wait_ge(sems["s_pe"], pe_start + mt + 1)
                        slot = P0[:, (mt % 2) * QC:(mt % 2) * QC + QC]
                        i1 = v.tensor_copy(dst[:, mt * QC:mt * QC + QC], slot)
                        inc(i1, "s_dv")
            else:
                @blk.scalar
                def _(s, pe_start=pe_start):
                    for mt in range(NPT):
                        s.wait_ge(sems["s_pe"], pe_start + mt + 1)
                        s.wait_ge(sems["s_dv"], V["s_dv"] + mt)
                        slot = P0[:, (mt % 2) * QC:(mt % 2) * QC + QC]
                        i1 = s.activation(th_sb[:, :], slot, AF.Tanh,
                                          bias=fbias_sb[:, mt:mt + 1], scale=1.0)
                        inc(i1, "s_ac")

                @blk.vector
                def _(v, nch=nch):
                    for mt in range(NPT):
                        v.wait_ge(sems["s_ac"], V["s_ac"] - NPT + mt + 1)
                        dstr = resid[:, mt * TL + nch * QC:mt * TL + nch * QC + QC]
                        i1 = v.scalar_tensor_tensor(dstr, th_sb[:, :], 0.5, dstr,
                                                    ALU.mult, ALU.add)
                        inc(i1, "s_dv")

    blk = open_block()
    emit_ln_stats_all(blk)
    emit_ln_bcast(blk, 0)
    emit_ln_apply(blk, 0, xlnc, lambda pt: pt * QC)
    for nch in range(TL // QC):
        if nch + 1 < TL // QC:
            emit_ln_bcast(blk, nch + 1)
        emit_ffn_chunk(blk, nch, xlnc, (nch % 2) * NPT * QC)
        if nch + 1 < TL // QC:
            p = (nch + 1) % 2
            emit_ln_apply(blk, nch + 1, xlnc, lambda pt, p=p: p * NPT * QC + pt * QC)
    close_block()
    if nblocks <= 2:
        return _finish_early(nc, sems, V, sem_cms)
    blk = open_block()
    emit_ln_stats_all(blk)
    emit_ln_bcast(blk, 0)
    dma_ld_after = {}
    for nch in range(TL // QC):
        buf = lnstage if nch % 2 == 0 else lnstage2
        emit_ln_apply(blk, nch, buf, lambda pt: pt * QC,
                      ld_wait=(dma_ld_after[nch - 2] if nch >= 2 else None))
        if nch + 1 < TL // QC:
            emit_ln_bcast(blk, nch + 1)

        @blk.sync
        def _(sy, nch=nch, buf=buf):
            sy.wait_ge(sems["s_gp"], V["s_gp"])
            ct = cin1a if nch < 2 else cin1b
            i1 = sy.dma_start(
                ct.ap().rearrange("(t p) c -> p t c", p=128)[
                    :, :, (nch % 2) * QC:(nch % 2) * QC + QC],
                buf[:, :].rearrange("p (t c) -> p t c", c=QC))
            inc(i1, "ld", 16)
        dma_ld_after[nch] = V["ld"]

        if nch == 1:
            cc1a_ld = V["ld"]

            @blk.gpsimd
            def _(g, cc1a_ld=cc1a_ld):
                g.wait_ge(sems["ld"], cc1a_ld)
                i1 = g.collective_compute(
                    "AllGather", ALU.bypass,
                    replica_groups=[[0, 1], [2, 3], [4, 5], [6, 7]],
                    ins=[cin1a.ap().opt()], outs=[cout1a.ap().opt()])
                inc(i1, "cc1")
    cc1_ld = V["ld"]

    @blk.sync
    def _(sy):
        sy.wait_ge(sems["s_dv"], V["s_dv"])
        i1 = sy.dma_start(rs_spill.ap(), resid[:, :])
        inc(i1, "ld", 16)

    @blk.gpsimd
    def _(g):
        g.wait_ge(sems["ld"], cc1_ld)
        i1 = g.collective_compute(
            "AllGather", ALU.bypass,
            replica_groups=[[0, 1], [2, 3], [4, 5], [6, 7]],
            ins=[cin1b.ap().opt()], outs=[cout1b.ap().opt()])
        inc(i1, "cc1")

    close_block()
    if nblocks <= 3:
        return _finish_early(nc, sems, V, sem_cms)
    blk = open_block()

    @blk.gpsimd
    def _(g):
        g.dma_start(wq_sb[:, :].rearrange("p (t c) -> p t c", c=512),
                    wq_in.ap().rearrange("(t p) c -> p t c", p=128)).then_inc(sems["lg"], 16)
        V["lg"] += 16
        g.dma_start(wkv_sb[:, :].rearrange("p (t c) -> p t c", c=512),
                    wkv_in.ap().rearrange("(t p) c -> p t c", p=128)).then_inc(sems["lg"], 16)
        V["lg"] += 16

    @blk.sync
    def _(sy):
        sy.wait_ge(sems["cc1"], V["cc1"] - 1)
        for r in range(2):
            i1 = sy.dma_start(
                xlnfull[:, :].rearrange("p (t c) -> p t c", c=TG)[
                    :, :, r * TL:r * TL + TL // 2],
                cout1a.ap()[r * D:(r + 1) * D, :].rearrange("(t p) c -> p t c", p=128))
            inc(i1, "ld", 16)
    xlna_ready = V["ld"]

    @blk.sync
    def _(sy):
        sy.wait_ge(sems["cc1"], V["cc1"])
        for r in range(2):
            i1 = sy.dma_start(
                xlnfull[:, :].rearrange("p (t c) -> p t c", c=TG)[
                    :, :, r * TL + TL // 2:r * TL + TL],
                cout1b.ap()[r * D:(r + 1) * D, :].rearrange("(t p) c -> p t c", p=128))
            inc(i1, "ld", 16)

    @blk.vector
    def _(v):
        i1 = v.memset(vp_sb[:, :], 1.0)
        inc(i1, "s_dv")

    xln_ready = V["ld"]
    vpones = V["s_dv"]

    _kv_mts = [] if b3_parts < 2 else ([0, 1] if b3_parts < 3 else [0, 1, 2, 3])
    evac_dv = {}
    _nch_order = [0, 1, 4, 5, 2, 3, 6, 7]
    for mt in _kv_mts:
        for ni in range(NQC):
            nch = _nch_order[ni]
            pe_t = V["s_pe"]
            step = mt * NQC + ni

            @blk.tensor
            def _(t, mt=mt, nch=nch, ni=ni, step=step):
                if mt == 0 and ni == 0:
                    t.wait_ge(sems["ld"], xlna_ready)
                    t.wait_ge(sems["lg"], V["lg"])
                if mt == 0 and ni == 4:
                    t.wait_ge(sems["ld"], xln_ready)
                slot = P0[:, (nch % 2) * QC:(nch % 2) * QC + QC]
                if step >= 2:
                    t.wait_ge(sems["s_dv"], evac_dv[step - 2])
                for kc in range(NPT):
                    i1 = t.matmul(slot,
                                  wkv_sb[:, kc * 512 + mt * 128:kc * 512 + mt * 128 + 128],
                                  xlnfull[:, kc * TG + nch * QC:kc * TG + nch * QC + QC],
                                  start=(kc == 0), stop=(kc == NPT - 1))
                inc(i1, "s_pe")

            if mt < 2:
                @blk.vector
                def _(v, mt=mt, nch=nch, pe_t=pe_t):
                    v.wait_ge(sems["s_pe"], pe_t + 1)
                    slot = P0[:, (nch % 2) * QC:(nch % 2) * QC + QC]
                    i1 = v.tensor_copy(kf_sb[:, mt * TG + nch * QC:mt * TG + nch * QC + QC],
                                       slot)
                    inc(i1, "s_dv")
                evac_dv[step] = V["s_dv"]
            else:
                @blk.vector
                def _(v, mt=mt, nch=nch, pe_t=pe_t):
                    v.wait_ge(sems["s_pe"], pe_t + 1)
                    slot = P0[:, (nch % 2) * QC:(nch % 2) * QC + QC]
                    i1 = v.tensor_copy(vstage[:, :], slot)
                    inc(i1, "s_dv")
                evac_dv[step] = V["s_dv"]
                dv_t = V["s_dv"]

                @blk.tensor
                def _(t, mt=mt, nch=nch, dv_t=dv_t):
                    t.wait_ge(sems["s_dv"], dv_t)
                    if mt == 2 and nch == 0:
                        t.wait_ge(sems["s_dv"], vpones)
                    for sst in range(4):
                        i1 = t.transpose(PT[:, sst * 128:sst * 128 + 128],
                                         vstage[:, sst * 128:sst * 128 + 128],
                                         id64_sb[:, :])
                    inc(i1, "s_pe")

                @blk.vector
                def _(v, mt=mt, nch=nch):
                    v.wait_ge(sems["s_pe"], V["s_pe"])
                    for half in range(2):
                        g = (mt - 2) * 2 + half
                        st0 = nch * 4
                        i1 = v.tensor_copy(
                            vp_sb[:, :].rearrange("p (a b) -> p a b", b=VPS)[
                                :, g * NST + st0:g * NST + st0 + 4, 0:64],
                            PT[:, :].rearrange("p (a b) -> p a b", b=128)[
                                :, :, half * 64:half * 64 + 64])
                        inc(i1, "s_dv")

    qevac_dv = {}
    for mt in (range(4) if b3_parts >= 4 else []):
        for ni in range(NQC):
            nch = _nch_order[ni]
            pe_t = V["s_pe"]
            step = mt * NQC + ni

            @blk.tensor
            def _(t, mt=mt, nch=nch, step=step):
                slot = P1[:, (nch % 2) * QC:(nch % 2) * QC + QC]
                if step >= 2:
                    t.wait_ge(sems["s_dv"], qevac_dv[step - 2])
                for kc in range(NPT):
                    i1 = t.matmul(slot,
                                  wq_sb[:, kc * 512 + mt * 128:kc * 512 + mt * 128 + 128],
                                  xlnfull[:, kc * TG + nch * QC:kc * TG + nch * QC + QC],
                                  start=(kc == 0), stop=(kc == NPT - 1))
                inc(i1, "s_pe")

            @blk.vector
            def _(v, mt=mt, nch=nch, pe_t=pe_t):
                v.wait_ge(sems["s_pe"], pe_t + 1)
                slot = P1[:, (nch % 2) * QC:(nch % 2) * QC + QC]
                i1 = v.tensor_scalar_mul(q_sb[:, mt * TG + nch * QC:mt * TG + nch * QC + QC],
                                         slot, 0.125)
                inc(i1, "s_dv")
            qevac_dv[step] = V["s_dv"]

    close_block()
    PT_guard.__exit__(None, None, None)
    P4 = nc.alloc_psum_tensor("P4", [65, QC], F32)
    P5 = nc.alloc_psum_tensor("P5", [65, QC], F32)
    if nblocks <= 4:
        return _finish_early(nc, sems, V, sem_cms)
    blk = open_block()

    @blk.sync
    def _(sy):
        i1 = sy.dma_start(resid[:, :], rs_spill.ap())
        inc(i1, "ld", 16)
    resid_back = V["ld"]

    NGRP = NST

    def lin(pr, qcg, st):
        return (pr * 4 + qcg) * NST + st

    devents = []
    dix = {}
    if b4_parts <= 1:
        _pq = []
    elif b4_parts == 2:
        _pq = [(0, 0)]
    elif b4_parts == 3:
        _pq = [(0, 0), (0, 1)]
    elif b4_parts == 4:
        _pq = [(0, q) for q in range(4)]
    else:
        _pq = [(pr, q) for pr in range(4) for q in range(4)]
    for pr, qcg in _pq:
        for st in range(NST):
            for hx in range(2):
                for j in range(2):
                    if _chunk_class(st, qcg * 2 + j) == "D":
                        dix[(pr, qcg, st, hx)] = dix.get((pr, qcg, st, hx))
                        dix[(pr, qcg, st, hx, j)] = len(devents)
                        devents.append((pr, qcg, st, hx, j))

    qk0 = V["s_qk"]
    ex0 = V["s_ex"]
    pv0 = V["s_pv"]
    st0 = V["s_st"]
    r0 = V["s_r"]
    ao0 = V["s_ao"]
    ld0 = V["ld"]
    pe0 = V["s_pe"]
    lnl0 = V["s_lnl"]

    def _exp_of_event(e):
        pe_, pq_, ps_, ph_, pj_ = devents[e]
        return ex0 + 2 * lin(pe_, pq_, ps_) + 1 + ph_

    for pr, qcg in _pq:
        G = pr * 4 + qcg
        L0 = G * NST
        heads = (2 * pr, 2 * pr + 1)
        QKP = (P0, P1)
        PVP = ((P2, P3), (P4, P5))

        @blk.tensor
        def _(t, pr=pr, qcg=qcg, G=G, L0=L0, heads=heads):
            for st in range(NST):
                L = L0 + st
                acls = [_chunk_class(st, qcg * 2 + j) for j in range(2)]
                if L >= 1:
                    t.wait_ge(sems["s_ex"], ex0 + 2 * (L - 1) + 2)
                last = [None, None]
                for j in range(2):
                    for hx in range(2):
                        l = heads[hx]
                        band = (l % 2) * 64
                        g = l % 4
                        lhs = kf_sb[(g % 2) * 64:(g % 2) * 64 + 64,
                                    (g // 2) * TG + st * 128:(g // 2) * TG + st * 128 + 128]
                        rhs = q_sb[band:band + 64,
                                   (l // 2) * TG + (qcg * 2 + j) * QC:
                                   (l // 2) * TG + (qcg * 2 + j) * QC + QC]
                        last[hx] = t.matmul(QKP[hx][:, j * QC:j * QC + QC], lhs, rhs,
                                            start=True, stop=(acls[j] != "A"),
                                            tile_position=(band, 0))
                for hx in range(2):
                    l = heads[hx]
                    aband = 64 - (l % 2) * 64
                    for j in range(2):
                        if acls[j] != "A":
                            continue
                        qc = qcg * 2 + j
                        last[hx] = t.matmul(
                            QKP[hx][:, j * QC:j * QC + QC],
                            augw_sb[aband:aband + 5, l * 128:l * 128 + 128],
                            augq_sb[aband:aband + 5, qc * QC:qc * QC + QC],
                            start=False, stop=True, tile_position=(aband, 0))
                last[0].then_inc(sems["s_qk"], 1)
                last[1].then_inc(sems["s_qk"], 1)
                if st >= 1:
                    for hx in range(2):
                        l = heads[hx]
                        g = l % 4
                        vcol = (g * NST + (st - 1)) * VPS
                        for j in range(2):
                            pvp = PVP[hx][j]
                            if st == 1 and G >= 1:
                                t.wait_ge(sems["s_ao"],
                                          ao0 + 4 * (G - 1) + 2 * hx + j + 1)
                            i1 = t.matmul(
                                pvp[:, :], vp_sb[:, vcol:vcol + 65],
                                pt_sb[:, ((st - 1) % 2) * 2048 + hx * 1024 + j * QC:
                                      ((st - 1) % 2) * 2048 + hx * 1024 + j * QC + QC],
                                start=(st - 1 == 0), stop=False)
                        i1.then_inc(sems["s_pv"], 1)
            for hx in range(2):
                l = heads[hx]
                g = l % 4
                t.wait_ge(sems["s_ex"], ex0 + 2 * (L0 + NST - 1) + 1 + hx)
                vcol = (g * NST + NST - 1) * VPS
                for j in range(2):
                    pvp = PVP[hx][j]
                    i1 = t.matmul(
                        pvp[:, :], vp_sb[:, vcol:vcol + 65],
                        pt_sb[:, ((NST - 1) % 2) * 2048 + hx * 1024 + j * QC:
                              ((NST - 1) % 2) * 2048 + hx * 1024 + j * QC + QC],
                        start=False, stop=True)
                i1.then_inc(sems["s_pv"], 1)
            for hx in range(2):
                t.wait_ge(sems["s_r"], r0 + 2 * G + 1 + hx)
                pva, pvb = PVP[hx]
                t.matmul(pva[0:64, :], ones1_sb[0:1, 0:64], rr_sb[0:1, 0:QC],
                         start=True, stop=True)
                i1 = t.matmul(pvb[0:64, :], ones1_sb[0:1, 0:64],
                              rr_sb[0:1, QC:2 * QC], start=True, stop=True)
                i1.then_inc(sems["s_pe"], 1)
        V["s_qk"] += 2 * NST
        V["s_pv"] += 2 * NST
        V["s_pe"] += 2

        @blk.vector
        def _(v, pr=pr, qcg=qcg, G=G, L0=L0, heads=heads):
            for st in range(NST):
                L = L0 + st
                for hx in range(2):
                    l = heads[hx]
                    qk = QKP[hx]
                    for j in range(2):
                        qc = qcg * 2 + j
                        if _chunk_class(st, qc) != "D":
                            continue
                        e = dix[(pr, qcg, st, hx, j)]
                        v.wait_ge(sems["s_qk"], qk0 + 2 * L + 1 + hx)
                        if e >= 2:
                            v.wait_ge(sems["s_ex"], _exp_of_event(e - 2))
                        d0 = st * 128 - qc * QC
                        i1 = v.scalar_tensor_tensor(
                            dsc[:, (e % 2) * QC:(e % 2) * QC + QC],
                            dtab_sb[:, (d0 // 128) * QC:(d0 // 128) * QC + QC],
                            slop_sb[:, l:l + 1],
                            qk[:, j * QC:j * QC + QC],
                            ALU.mult, ALU.add)
                        inc(i1, "s_st")
            for hx in range(2):
                pva, pvb = PVP[hx]
                v.wait_ge(sems["s_pv"], pv0 + 2 * (L0 + NST - 1) + 1 + hx)
                if G >= 1:
                    v.wait_ge(sems["s_r"], r0 + 2 * (G - 1) + 1 + hx)
                v.tensor_copy(pvt[:, hx * 2 * QC:hx * 2 * QC + QC], pva[:, :])
                i1 = v.tensor_copy(pvt[:, (2 * hx + 1) * QC:(2 * hx + 1) * QC + QC],
                                   pvb[:, :])
                inc(i1, "s_lnl")
                v.wait_ge(sems["s_r"], r0 + 2 * G + 1 + hx)
                v.wait_ge(sems["s_pe"], pe0 + 2 * G + 1 + hx)
                if G >= 1:
                    v.wait_ge(sems["ld"], ld0 + 32 * G)
                v.tensor_copy(rbs[:, :], pva[0:64, :])
                v.scalar_tensor_tensor(aost[:, hx * 2 * QC:hx * 2 * QC + QC],
                                       pvt[0:64, hx * 2 * QC:hx * 2 * QC + QC],
                                       1.0, rbs[:, :],
                                       ALU.mult, ALU.mult).then_inc(sems["s_ao"], 1)
                v.tensor_copy(rbs[:, :], pvb[0:64, :])
                v.scalar_tensor_tensor(aost[:, (hx * 2 + 1) * QC:(hx * 2 + 1) * QC + QC],
                                       pvt[0:64, (2 * hx + 1) * QC:(2 * hx + 1) * QC + QC],
                                       1.0, rbs[:, :],
                                       ALU.mult, ALU.mult).then_inc(sems["s_ao"], 1)
        V["s_st"] = st0 + len([1 for e in devents if (e[0] * 4 + e[1]) <= G])
        V["s_ao"] += 4
        V["s_lnl"] += 2

        V["s_ex"] += 2 * NST

        @blk.sync
        def _(sy, pr=pr, qcg=qcg, G=G):
            half = qcg // 2
            colo = (qcg % 2) * 1024
            for hx in range(2):
                l = 2 * pr + hx
                if l < 4:
                    snd, row = snd2a, half * 256 + l * 64
                elif l < 6:
                    snd, row = snd2b, half * 128 + (l - 4) * 64
                else:
                    snd, row = snd2c, half * 128 + (l - 6) * 64
                sy.wait_ge(sems["s_ao"], ao0 + 4 * G + 2 * hx + 2)
                i1 = sy.dma_start(
                    snd.ap()[row:row + 64, colo:colo + 1024],
                    aost[:, hx * 2 * QC:hx * 2 * QC + 2 * QC])
                inc(i1, "ld", 16)

        if G == 7:
            @blk.gpsimd
            def _(g):
                if b4_parts >= 8:
                    g.wait_ge(sems["ld"], V["ld"])
                    i1 = g.collective_compute(
                        "AllGather", ALU.bypass,
                        replica_groups=[[0, 1], [2, 3], [4, 5], [6, 7]],
                        ins=[snd2a.ap().opt()], outs=[rcv2a.ap().opt()])
                    inc(i1, "cc2")
        if G == 11:
            @blk.gpsimd
            def _(g):
                if b4_parts >= 8:
                    g.wait_ge(sems["ld"], V["ld"])
                    i1 = g.collective_compute(
                        "AllGather", ALU.bypass,
                        replica_groups=[[0, 1], [2, 3], [4, 5], [6, 7]],
                        ins=[snd2b.ap().opt()], outs=[rcv2b.ap().opt()])
                    inc(i1, "cc2")

    @blk.scalar
    def _(s):
        for pr, qcg in _pq:
            G = pr * 4 + qcg
            L0 = G * NST
            heads = (2 * pr, 2 * pr + 1)
            for st in range(NST):
                L = L0 + st
                c0 = _chunk_class(st, qcg * 2)
                c1 = _chunk_class(st, qcg * 2 + 1)
                if c0 == "B" and c1 == "B":
                    s.wait_ge(sems["s_qk"], qk0 + 2 * L + 2)
                    if L >= 2:
                        s.wait_ge(sems["s_pv"], pv0 + 2 * (L - 2) + 2)
                    pb = (st % 2) * 2048
                    s.activation(pt_sb[:, pb:pb + 2048], P01[:, :], AF.Exp,
                                 bias=negc_sb[:, 0:1], scale=1.0).then_inc(sems["s_ex"], 2)
                    continue
                for hx in range(2):
                    l = heads[hx]
                    qk = (P0, P1)[hx]
                    s.wait_ge(sems["s_qk"], qk0 + 2 * L + 1 + hx)
                    if L >= 2:
                        s.wait_ge(sems["s_pv"], pv0 + 2 * (L - 2) + 1 + hx)
                    pbase = (st % 2) * 2048 + hx * 1024
                    pdst = pt_sb[:, pbase:pbase + 1024]
                    if c0 == c1 and c0 != "D":
                        if c0 == "A":
                            bias = biasc_sb[:, l * NST + st:l * NST + st + 1]
                        else:
                            bias = negc_sb[:, 0:1]
                        i1 = s.activation(pdst, qk[:, :], AF.Exp, bias=bias, scale=1.0)
                    else:
                        for j in range(2):
                            qc = qcg * 2 + j
                            cls = _chunk_class(st, qc)
                            dd = pt_sb[:, pbase + j * QC:pbase + j * QC + QC]
                            if cls == "D":
                                e = dix[(pr, qcg, st, hx, j)]
                                s.wait_ge(sems["s_st"], st0 + e + 1)
                                i1 = s.activation(dd, dsc[:, (e % 2) * QC:(e % 2) * QC + QC],
                                                  AF.Exp, bias=negc_sb[:, 0:1], scale=1.0)
                            elif cls == "A":
                                i1 = s.activation(dd, qk[:, j * QC:j * QC + QC], AF.Exp,
                                                  bias=biasc_sb[:, l * NST + st:l * NST + st + 1],
                                                  scale=1.0)
                            else:
                                i1 = s.activation(dd, qk[:, j * QC:j * QC + QC], AF.Exp,
                                                  bias=negc_sb[:, 0:1], scale=1.0)
                    i1.then_inc(sems["s_ex"], 1)
            for hx in range(2):
                s.wait_ge(sems["s_lnl"], lnl0 + 2 * G + 1 + hx)
                s.wait_ge(sems["s_pe"], pe0 + 2 * G + hx)
                s.activation(lnt_att[:, :], pvt[64:65, hx * 2 * QC:hx * 2 * QC + 2 * QC],
                             AF.Ln, bias=0.0, scale=1.0)
                s.activation(rr_sb[:, :], lnt_att[:, :], AF.Exp,
                             bias=0.0, scale=-1.0).then_inc(sems["s_r"], 1)

    V["s_r"] += 2 * len(_pq)

    @blk.gpsimd
    def _(g):
        if b4_parts < 8:
            return
        g.wait_ge(sems["ld"], V["ld"])
        i1 = g.collective_compute(
            "AllGather", ALU.bypass,
            replica_groups=[[0, 1], [2, 3], [4, 5], [6, 7]],
            ins=[snd2c.ap().opt()], outs=[rcv2c.ap().opt()])
        inc(i1, "cc2")

    close_block()
    if nblocks <= 5:
        return _finish_early(nc, sems, V, sem_cms)
    blk = open_block()

    @blk.gpsimd
    def _(g):
        i1 = g.dma_start(wpj_sb[:, :].rearrange("p (t c) -> p t c", c=K1),
                         wproj_in.ap().rearrange("(t p) c -> p t c", p=128))
        inc(i1, "lg", 16)

    _pmap = {0: (rcv2a, 0, 256), 1: (rcv2a, 128, 384),
             4: (rcv2a, 512, 768), 5: (rcv2a, 640, 896),
             2: (rcv2b, 0, 128), 6: (rcv2b, 256, 384),
             3: (rcv2c, 0, 128), 7: (rcv2c, 256, 384)}
    _ptorder = [0, 1, 4, 5, 2, 6, 3, 7]

    @blk.sync
    def _(sy):
        for i, pt in enumerate(_ptorder):
            if i == 0:
                sy.wait_ge(sems["cc2"], V["cc2"] - 2)
            if i == 4:
                sy.wait_ge(sems["cc2"], V["cc2"] - 1)
            if i == 6:
                sy.wait_ge(sems["cc2"], V["cc2"])
            rcv, r0, r1 = _pmap[pt]
            sy.dma_start(aoc0[:, :], rcv.ap()[r0:r0 + 128, :]).then_inc(sems["ld"], 16)
            V["ld"] += 16
            sy.dma_start(aoc1[:, :], rcv.ap()[r1:r1 + 128, :]).then_inc(sems["ld"], 16)
            V["ld"] += 16
            sy.wait_ge(sems["s_dv"], V["s_dv"] + i + 1)

    @blk.vector
    def _(v):
        base_ld = V["ld"] - 32 * NPT
        for i, pt in enumerate(_ptorder):
            v.wait_ge(sems["ld"], base_ld + (i + 1) * 32)
            v.tensor_scalar_mul(aot2[:, :], aoc0[:, :], sel_sb[:, 0:1])
            i1 = v.scalar_tensor_tensor(ao_sb[:, pt * TL:pt * TL + TL],
                                        aoc1[:, :], sel_sb[:, 1:2], aot2[:, :],
                                        ALU.mult, ALU.add)
            inc(i1, "s_dv")

    ao_ready_dv = V["s_dv"]
    wpj_ready = V["lg"]

    for mt in range(NPT):
        for nch in range(TL // QC):
            pe_t = V["s_pe"]

            @blk.tensor
            def _(t, mt=mt, nch=nch):
                if mt == 0 and nch == 0:
                    t.wait_ge(sems["s_dv"], ao_ready_dv)
                    t.wait_ge(sems["lg"], wpj_ready)
                slot = P0[:, (nch % 2) * QC:(nch % 2) * QC + QC]
                if mt * (TL // QC) + nch >= 2:
                    t.wait_ge(sems["s_dv"], V["s_dv"] - 1)
                for kc in range(NPT):
                    i1 = t.matmul(slot,
                                  wpj_sb[:, kc * K1 + mt * 128:kc * K1 + mt * 128 + 128],
                                  ao_sb[:, kc * TL + nch * QC:kc * TL + nch * QC + QC],
                                  start=(kc == 0), stop=(kc == NPT - 1))
                inc(i1, "s_pe")

            @blk.vector
            def _(v, mt=mt, nch=nch, pe_t=pe_t):
                v.wait_ge(sems["s_pe"], pe_t + 1)
                if mt == 0 and nch == 0:
                    v.wait_ge(sems["ld"], resid_back)
                slot = P0[:, (nch % 2) * QC:(nch % 2) * QC + QC]
                dstr = resid[:, mt * TL + nch * QC:mt * TL + nch * QC + QC]
                i1 = v.scalar_tensor_tensor(dstr, slot, bproj_sb[:, mt:mt + 1],
                                            dstr, ALU.add, ALU.add)
                inc(i1, "s_dv")

    close_block()
    if nblocks <= 6:
        return _finish_early(nc, sems, V, sem_cms)
    blk = open_block()

    @blk.gpsimd
    def _(g):
        for i in range(3):
            i1 = g.dma_start(wsb[i][:, :].rearrange("p (t c) -> p t c", c=K1),
                             w_ffn[i].ap().rearrange("(t p) c -> p t c", p=128))
            inc(i1, "lg", 16)
    w_ready = V["lg"]

    emit_ln_stats_all(blk)
    emit_ln_bcast(blk, 0)
    emit_ln_apply(blk, 0, xlnc, lambda pt: pt * QC)

    @blk.tensor
    def _(t):
        t.wait_ge(sems["lg"], w_ready)
    for nch in range(TL // QC):
        if nch + 1 < TL // QC:
            emit_ln_bcast(blk, nch + 1)
        emit_ffn_chunk(blk, nch, xlnc, (nch % 2) * NPT * QC)
        if nch + 1 < TL // QC:
            p = (nch + 1) % 2
            emit_ln_apply(blk, nch + 1, xlnc, lambda pt, p=p: p * NPT * QC + pt * QC)

        @blk.sync
        def _(sy, nch=nch):
            sy.wait_ge(sems["s_dv"], V["s_dv"])
            i1 = sy.dma_start(
                y_out.ap().rearrange("(t p) c -> p t c", p=128)[:, :, nch * QC:nch * QC + QC],
                resid[:, :].rearrange("p (t c) -> p t c", c=TL)[:, :, nch * QC:nch * QC + QC])
            inc(i1, "ld", 16)

    @blk.sync
    def _(sy):
        sy.wait_ge(sems["ld"], V["ld"])

    close_block()
    if nblocks <= 7:
        return _finish_early(nc, sems, V, sem_cms)
    for cm in reversed(sem_cms):
        cm.__exit__(None, None, None)
    return nc


def _core_tables(parity):
    start = 2.0 ** (-2.0 ** (-(np.log2(16) - 3.0)))
    slopes = start ** np.arange(1, 17, dtype=np.float64)
    gh = np.arange(parity * 8, parity * 8 + 8)
    sl = slopes[gh].astype(np.float32)

    augw = np.zeros((5, NHL * 128), np.float32)
    for l in range(NHL):
        s = float(sl[l])
        s_hi = float(BF(s))
        s_lo = s - s_hi
        m = np.arange(128, dtype=np.float64)
        mv = (s * m / 8.0).astype(np.float32)
        m_hi = BF(mv)
        m_lo = (mv - m_hi).astype(np.float32)
        augw[0, l * 128:(l + 1) * 128] = s_hi
        augw[1, l * 128:(l + 1) * 128] = s_hi
        augw[2, l * 128:(l + 1) * 128] = s_lo
        augw[3, l * 128:(l + 1) * 128] = m_hi
        augw[4, l * 128:(l + 1) * 128] = m_lo

    biasc = np.zeros((128, NHL * NST), np.float32)
    for l in range(NHL):
        for st in range(NST):
            biasc[:, l * NST + st] = sl[l] * (st * 128) / 8.0 - CSHIFT

    slop = np.tile(sl[None, :], (128, 1)).astype(np.float32)
    return augw, biasc, slop


def _shared_tables():
    q = np.arange(TG, dtype=np.float64) / 8.0
    qh = BF(-q)
    ql = (-q - qh).astype(np.float32)
    assert np.all(BF(ql) == ql), "q_lo must be bf16-exact"
    augq = np.zeros((5, TG), np.float32)
    augq[0] = qh
    augq[1] = ql
    augq[2] = qh
    augq[3] = 1.0
    augq[4] = 1.0

    dtab = np.zeros((128, 4 * QC), np.float32)
    for jb in range(4):
        d0 = jb * 128
        p = np.arange(128)[:, None]
        f = np.arange(QC)[None, :]
        dd = d0 + p - f
        dtab[:, jb * QC:(jb + 1) * QC] = np.where(dd <= 0, dd / 8.0, 0.0)
    return augq, dtab


_NC_CACHE = {}


def kernel(L0_attn_x, L0_attn_Wq, L0_attn_Wkv, L0_attn_Wproj, L0_attn_bproj,
           L0_ffn_weight, L0_ffn_aux_weight1, L0_ffn_aux_weight2, L0_ffn_bias):
    if "nc" not in _NC_CACHE:
        _NC_CACHE["nc"] = build_program()
    nc = _NC_CACHE["nc"]

    x = np.asarray(L0_attn_x, np.float32)
    Wq = np.asarray(L0_attn_Wq, np.float32)
    Wkv = np.asarray(L0_attn_Wkv, np.float32)
    Wproj = np.asarray(L0_attn_Wproj, np.float32)
    bproj = np.asarray(L0_attn_bproj, np.float32)
    Wf = np.asarray(L0_ffn_weight, np.float32)
    A1 = np.asarray(L0_ffn_aux_weight1, np.float32)
    A2 = np.asarray(L0_ffn_aux_weight2, np.float32)
    fb = np.asarray(L0_ffn_bias, np.float32)

    wq_t = np.ascontiguousarray(Wq.T)
    wkv_t = np.ascontiguousarray(Wkv.T)
    wproj_t = np.ascontiguousarray(Wproj.T)
    bproj_c = np.ascontiguousarray(bproj.reshape(NPT, 128).T)
    fb_c = np.ascontiguousarray(fb.reshape(NPT, 128).T)
    onek = np.full((128, 1), 1.0 / 1024.0, np.float32)
    negc = np.full((128, 1), -CSHIFT, np.float32)
    sels = [np.tile(np.array([[1.0, 0.0]], np.float32), (128, 1)),
            np.tile(np.array([[0.0, 1.0]], np.float32), (128, 1))]
    ones1 = np.ones((1, 128), np.float32)
    id64 = np.eye(128, dtype=np.float32)
    augq, dtab = _shared_tables()
    tabs = [_core_tables(0), _core_tables(1)]

    in_maps = []
    for c in range(8):
        b, half = c // 2, c % 2
        t0 = half * TL
        augw, biasc, slop = tabs[half]
        wq_c = np.ascontiguousarray(wq_t[:, half * 512:half * 512 + 512])
        in_maps.append({
            "x": np.ascontiguousarray(x[b, t0:t0 + TL, :].T),
            "wffn0": Wf, "wffn1": A1, "wffn2": A2,
            "wq": wq_c, "wkv": wkv_t, "wproj": wproj_t,
            "bproj": bproj_c, "fbias": fb_c,
            "augw": augw, "augq": augq, "biasc": biasc, "dtab": dtab,
            "slop": slop, "onek": onek, "ones1": ones1, "id64": id64,
            "negc": negc, "sel": sels[half],
        })

    res = run_bass_kernel_spmd(nc, in_maps, core_ids=list(range(8)))
    out = np.empty((4, TG, D), np.float32)
    for c in range(8):
        b, half = c // 2, c % 2
        t0 = half * TL
        out[b, t0:t0 + TL, :] = res.results[c]["y"].T
    return out


# version 23
# speedup vs baseline: 1.0233x; 1.0017x over previous
import numpy as np
import ml_dtypes
import concourse.bass as bass
import concourse.mybir as mybir
from concourse.bass_utils import run_bass_kernel_spmd

F32 = mybir.dt.float32
BF16 = mybir.dt.bfloat16
AF = mybir.ActivationFunctionType
ALU = mybir.AluOpType

D = 1024
NPT = 8
TL = 2048
TG = 4096
HD = 64
NHL = 8
KVH = 4
QC = 512
NQC = TG // QC
NST = TG // 128
CSHIFT = 15.0
EPS = 1e-5

BF = lambda a: np.asarray(a, np.float32).astype(ml_dtypes.bfloat16).astype(np.float32)


def _chunk_class(st, qc):
    d0 = st * 128 - qc * QC
    if d0 <= -128:
        return "A"
    if d0 >= QC:
        return "B"
    return "D"



def _finish_early(nc, sems, V, sem_cms):
    b = nc.Block()
    blk = b.__enter__()

    @blk.sync
    def _(sy):
        sy.wait_ge(sems["ld"], V["ld"])

    @blk.gpsimd
    def _(g):
        g.wait_ge(sems["lg"], V["lg"])
        g.wait_ge(sems["cc1"], V["cc1"])
        g.wait_ge(sems["cc2"], V["cc2"])

    @blk.vector
    def _(v):
        v.wait_ge(sems["s_dv"], V["s_dv"])

    @blk.scalar
    def _(s):
        s.wait_ge(sems["s_ac"], V["s_ac"])

    @blk.tensor
    def _(t):
        t.wait_ge(sems["s_pe"], V["s_pe"])
    b.__exit__(None, None, None)
    for cm in reversed(sem_cms):
        cm.__exit__(None, None, None)
    return nc


def build_program(nblocks=99, b3_parts=9, b4_parts=9):
    nc = bass.Bass("TRN2", target_bir_lowering=False, detect_race_conditions=False)

    x_in = nc.dram_tensor("x", [D, TL], F32, kind="ExternalInput")
    w_ffn = [nc.dram_tensor(n, [D, D], F32, kind="ExternalInput")
             for n in ("wffn0", "wffn1", "wffn2")]
    wq_in = nc.dram_tensor("wq", [D, NHL * HD], F32, kind="ExternalInput")
    wkv_in = nc.dram_tensor("wkv", [D, 512], F32, kind="ExternalInput")
    wproj_in = nc.dram_tensor("wproj", [D, D], F32, kind="ExternalInput")
    bproj_in = nc.dram_tensor("bproj", [128, NPT], F32, kind="ExternalInput")
    fbias_in = nc.dram_tensor("fbias", [128, NPT], F32, kind="ExternalInput")
    augw_in = nc.dram_tensor("augw", [5, NHL * 128], F32, kind="ExternalInput")
    augq_in = nc.dram_tensor("augq", [5, TG], F32, kind="ExternalInput")
    biasc_in = nc.dram_tensor("biasc", [128, NHL * NST], F32, kind="ExternalInput")
    dtab_in = nc.dram_tensor("dtab", [128, 4 * QC], F32, kind="ExternalInput")
    slop_in = nc.dram_tensor("slop", [128, NHL], F32, kind="ExternalInput")
    onek_in = nc.dram_tensor("onek", [128, 1], F32, kind="ExternalInput")
    ones1_in = nc.dram_tensor("ones1", [1, 128], F32, kind="ExternalInput")
    id64_in = nc.dram_tensor("id64", [128, 128], F32, kind="ExternalInput")
    negc_in = nc.dram_tensor("negc", [128, 1], F32, kind="ExternalInput")
    sel_in = nc.dram_tensor("sel", [128, 2], F32, kind="ExternalInput")

    y_out = nc.dram_tensor("y", [D, TL], F32, kind="ExternalOutput")

    rs_spill = nc.dram_tensor("rs_spill", [128, NPT * TL], F32)
    cin1 = [nc.dram_tensor(f"cin1_{k}", [D, QC], BF16) for k in range(4)]
    cout1 = [nc.dram_tensor(f"cout1_{k}", [2 * D, QC], BF16) for k in range(4)]
    snd2a = nc.dram_tensor("snd2a", [512, TL], BF16)
    snd2b = nc.dram_tensor("snd2b", [256, TL], BF16)
    snd2c = nc.dram_tensor("snd2c", [256, TL], BF16)
    rcv2a = nc.dram_tensor("rcv2a", [1024, TL], BF16)
    rcv2b = nc.dram_tensor("rcv2b", [512, TL], BF16)
    rcv2c = nc.dram_tensor("rcv2c", [512, TL], BF16)

    SB = {}

    SBUF_BASE = 17 * 1024

    def sb(name, off, shape, dtype):
        off = off + SBUF_BASE
        t = nc.alloc_sbuf_tensor_at(name, list(shape), dtype, offset=off)
        SB[name] = (off, shape, dtype)
        return t

    K1 = 1024
    resid = sb("resid", 0, [128, NPT * TL], F32)
    xlnfull = sb("xlnfull", 0, [128, NPT * TG], BF16)

    wsb = [sb(f"wsb{i}", 64 * K1 + i * 16 * K1, [128, NPT * K1], BF16)
           for i in range(3)]
    q_sb = sb("q_sb", 64 * K1, [128, 4 * TG], BF16)
    kf_sb = sb("kf_sb", 96 * K1, [128, 2 * TG], BF16)
    ao_sb = sb("ao_sb", 64 * K1, [128, NPT * TL], BF16)

    h1c = sb("h1c", 112 * K1, [128, NPT * QC], BF16)
    h2c = sb("h2c", 120 * K1, [128, NPT * QC], BF16)
    xb = sb("xb", 128 * K1, [128, 2 * QC], BF16)
    xsq = sb("xsq", 130 * K1, [128, 2 * QC], BF16)
    VPS = 68
    vp_sb = sb("vp_sb", 112 * K1, [128, KVH * NST * VPS], BF16)
    wpj_sb = sb("wpj_sb", 112 * K1, [128, NPT * K1], BF16)

    wq_sb = sb("wq_sb", 132 * K1, [128, NPT * 512], BF16)
    wkv_sb = sb("wkv_sb", 140 * K1, [128, NPT * 512], BF16)
    xlnc = sb("xlnc", 132 * K1, [128, 2 * NPT * QC], BF16)
    pt_sb = sb("pt_sb", 132 * K1, [128, 4 * 1024], BF16)
    dsc = sb("dsc", 140 * K1, [128, 2 * QC], F32)
    rr_sb = sb("rr_sb", 144 * K1, [1, 2 * QC], F32)
    aost = sb("aost", 150 * K1, [64, 4 * QC], BF16)
    lnl_sb = sb("lnl_sb", 160 * K1, [1, 2 * QC], F32)
    lnt_att = sb("lnt_att", 168 * K1, [1, 2 * QC], F32)
    rbs = sb("rbs", 154 * K1, [64, QC], F32)

    lnstage = sb("lnstage", 148 * K1, [128, NPT * QC], BF16)
    vstage = sb("vstage", 148 * K1, [128, QC], BF16)
    tmpf = sb("tmpf", 156 * K1, [128, 2 * QC], F32)
    mrbs = sb("mrbs", 128 * K1, [128, QC], F32)
    th_sb = sb("th_sb", 148 * K1, [128, QC], BF16)
    tab_off = 185 * K1

    stA = sb("stA", 160 * K1, [1, 4 * QC], F32)
    stB = sb("stB", 168 * K1, [1, 4 * QC], F32)
    stC = sb("stC", 176 * K1, [1, 4 * QC], F32)
    pvt = sb("pvt", 176 * K1, [65, 4 * QC], F32)
    lnstage2 = sb("lnstage2", 132 * K1, [128, NPT * QC], BF16)

    augw_sb = sb("augw_sb", tab_off, [69, NHL * 128], BF16)
    augq_sb = sb("augq_sb", tab_off + 2 * K1, [69, TG], BF16)
    biasc_sb = sb("biasc_sb", tab_off + 10 * K1, [128, NHL * NST], F32)
    dtab_sb = sb("dtab_sb", tab_off + 11 * K1, [128, 4 * QC], F32)
    slop_sb = sb("slop_sb", tab_off + 19 * K1, [128, NHL], F32)
    onek_sb = sb("onek_sb", tab_off + 19 * K1 + 64, [128, 1], BF16)
    ones1_sb = sb("ones1_sb", tab_off + 19 * K1 + 128, [1, 128], F32)
    id64_sb = sb("id64_sb", tab_off + 20 * K1, [128, 128], BF16)
    bproj_sb = sb("bproj_sb", tab_off + 20 * K1 + 256, [128, NPT], F32)
    fbias_sb = sb("fbias_sb", tab_off + 20 * K1 + 512, [128, NPT], F32)
    negc_sb = sb("negc_sb", tab_off + 20 * K1 + 768, [128, 1], F32)
    sel_sb = sb("sel_sb", tab_off + 20 * K1 + 832, [128, 2], F32)
    aoc0 = sb("aoc0", 132 * K1, [128, TL], BF16)
    aoc1 = sb("aoc1", 136 * K1, [128, TL], BF16)
    aot2 = sb("aot2", 140 * K1, [128, TL], BF16)
    assert tab_off + 21 * K1 + 17 * K1 <= 229344 // 1024 * K1

    P01 = nc.alloc_psum_tensor("P01", [128, 2048], F32)
    P0 = P01[:, 0:1024]
    P1 = P01[:, 1024:2048]
    P2 = nc.alloc_psum_tensor("P2", [65, QC], F32)
    P3 = nc.alloc_psum_tensor("P3", [65, QC], F32)
    PT_guard = nc.psum_tensor("PT", [128, 512], BF16)
    PT = PT_guard.__enter__()

    semnames = ["ld", "lg", "cc1", "cc2", "s_pe", "s_ac", "s_dv", "s_gp",
                "s_qk", "s_ex", "s_pv", "s_st", "s_r", "s_ao", "s_lnl", "s_fin"]
    sems = {}
    sem_cms = []
    for n in semnames:
        cm = nc.semaphore(n)
        sems[n] = cm.__enter__()
        sem_cms.append(cm)
    V = {n: 0 for n in semnames}

    def inc(inst, name, n=1):
        inst.then_inc(sems[name], n)
        V[name] += n

    blocks = []

    def open_block():
        b = nc.Block()
        bb = b.__enter__()
        blocks.append(b)
        return bb

    def close_block(b=None):
        blocks.pop().__exit__(None, None, None)

    blk = open_block()

    x_ld = {}

    @blk.sync
    def _(sy):
        for t_in, t_sb in ((onek_in, None), (ones1_in, ones1_sb)):
            pass
        for nch in range(TL // QC):
            sy.dma_start(
                resid[:, :].rearrange("p (t c) -> p t c", c=TL)[:, :, nch * QC:nch * QC + QC],
                x_in.ap().rearrange("(t p) c -> p t c", p=128)[:, :, nch * QC:nch * QC + QC],
            ).then_inc(sems["ld"], 16)
            V["ld"] += 16
            x_ld[nch] = V["ld"]
        for t_in, t_sb in ((bproj_in, bproj_sb), (fbias_in, fbias_sb),
                           (biasc_in, biasc_sb), (dtab_in, dtab_sb),
                           (slop_in, slop_sb), (ones1_in, ones1_sb),
                           (negc_in, negc_sb), (sel_in, sel_sb)):
            sy.dma_start(t_sb[:, :], t_in.ap()).then_inc(sems["ld"], 16)
            V["ld"] += 16

    @blk.gpsimd
    def _(g):
        for i in range(3):
            g.dma_start(wsb[i][:, :].rearrange("p (t c) -> p t c", c=K1),
                        w_ffn[i].ap().rearrange("(t p) c -> p t c", p=128)).then_inc(sems["lg"], 16)
            V["lg"] += 16
        for base in (0, 64):
            g.dma_start(augw_sb[base:base + 5, :], augw_in.ap()).then_inc(sems["lg"], 16)
            V["lg"] += 16
            g.dma_start(augq_sb[base:base + 5, :], augq_in.ap()).then_inc(sems["lg"], 16)
            V["lg"] += 16
        g.dma_start(onek_sb[:, :], onek_in.ap()).then_inc(sems["lg"], 16)
        V["lg"] += 16
        g.dma_start(id64_sb[:, :], id64_in.ap()).then_inc(sems["lg"], 16)
        V["lg"] += 16
    b0_ld = V["ld"]
    b0_lg = V["lg"]

    close_block()
    if nblocks <= 1:
        return _finish_early(nc, sems, V, sem_cms)

    def emit_ln_stats_all(blk):
        for nch in range(TL // QC):
            @blk.vector
            def _(v, nch=nch):
                if nch in x_ld:
                    v.wait_ge(sems["ld"], x_ld[nch])
                for pt in range(NPT):
                    sl = xb[:, (pt % 2) * QC:(pt % 2) * QC + QC]
                    sq = xsq[:, (pt % 2) * QC:(pt % 2) * QC + QC]
                    k = nch * NPT + pt
                    if k >= 2:
                        v.wait_ge(sems["s_pe"], V["s_pe"] + (k - 2) * 2 + 2 - nch * 2 * NPT)
                    v.tensor_copy(sl, resid[:, pt * TL + nch * QC:
                                            pt * TL + nch * QC + QC])
                    i1 = v.tensor_mul(sq, sl, sl)
                    inc(i1, "s_dv")

            @blk.tensor
            def _(t, nch=nch):
                if nch == 0:
                    t.wait_ge(sems["lg"], b0_lg)
                for pt in range(NPT):
                    t.wait_ge(sems["s_dv"], V["s_dv"] - NPT + pt + 1)
                    if pt == 0 and nch > 0:
                        t.wait_ge(sems["s_dv"], V["s_dv"] - NPT)
                    sl = xb[:, (pt % 2) * QC:(pt % 2) * QC + QC]
                    sq = xsq[:, (pt % 2) * QC:(pt % 2) * QC + QC]
                    i1 = t.matmul(P2[0:1, :], onek_sb[:, :], sl,
                                  start=(pt == 0), stop=(pt == NPT - 1))
                    inc(i1, "s_pe")
                    i2 = t.matmul(P3[0:1, :], onek_sb[:, :], sq,
                                  start=(pt == 0), stop=(pt == NPT - 1))
                    inc(i2, "s_pe")

            @blk.vector
            def _(v, nch=nch):
                v.wait_ge(sems["s_pe"], V["s_pe"])
                c = nch * QC
                v.tensor_copy(stA[:, c:c + QC], P2[0:1, :])
                v.tensor_copy(stB[:, c:c + QC], P3[0:1, :])
                v.tensor_mul(stC[:, c:c + QC], stA[:, c:c + QC], stA[:, c:c + QC])
                v.scalar_tensor_tensor(stB[:, c:c + QC], stB[:, c:c + QC], 1.0,
                                       stC[:, c:c + QC], ALU.mult, ALU.subtract)
                i1 = v.tensor_scalar_add(stB[:, c:c + QC], stB[:, c:c + QC], EPS)
                inc(i1, "s_dv")

        @blk.scalar
        def _(s):
            s.wait_ge(sems["s_dv"], V["s_dv"])
            s.activation(stC[:, :], stB[:, :], AF.Ln, bias=0.0, scale=1.0)
            i1 = s.activation(stB[:, :], stC[:, :], AF.Exp, bias=0.0, scale=-0.5)
            inc(i1, "s_ac")

        @blk.vector
        def _(v):
            v.wait_ge(sems["s_ac"], V["s_ac"])
            i1 = v.tensor_mul(stC[:, :], stA[:, :], stB[:, :])
            inc(i1, "s_dv")

    def emit_ln_bcast(blk, nch, extra_dv_wait=0):
        @blk.tensor
        def _(t):
            t.wait_ge(sems["s_dv"], V["s_dv"] + extra_dv_wait)
            c = nch * QC
            t.matmul(P1[:, 0:QC], ones1_sb[0:1, :], stB[:, c:c + QC],
                     start=True, stop=True)
            i1 = t.matmul(P1[:, QC:2 * QC], ones1_sb[0:1, :], stC[:, c:c + QC],
                          start=True, stop=True)
            inc(i1, "s_pe")

    def emit_ln_apply(blk, nch, dst, dst_col, ld_wait=None):
        pe_target = V["s_pe"]
        gp_base = V["s_gp"]

        @blk.vector
        def _(v):
            v.wait_ge(sems["s_pe"], pe_target)
            v.tensor_copy(mrbs[:, :], P1[:, QC:2 * QC])
            for pt in range(NPT):
                if pt >= 2:
                    v.wait_ge(sems["s_gp"], gp_base + pt - 1)
                src = resid[:, pt * TL + nch * QC:pt * TL + nch * QC + QC]
                i1 = v.tensor_mul(tmpf[:, (pt % 2) * QC:(pt % 2) * QC + QC],
                                  src, P1[:, 0:QC])
                inc(i1, "s_dv")

        @blk.gpsimd
        def _(g):
            g.wait_ge(sems["s_pe"], pe_target)
            if ld_wait is not None:
                g.wait_ge(sems["ld"], ld_wait)
            for pt in range(NPT):
                g.wait_ge(sems["s_dv"], V["s_dv"] - NPT + pt + 1)
                i1 = g.tensor_sub(dst[:, dst_col(pt):dst_col(pt) + QC],
                                  tmpf[:, (pt % 2) * QC:(pt % 2) * QC + QC],
                                  mrbs[:, :])
                inc(i1, "s_gp")

    def emit_ffn_chunk(blk, nch, src_t, src_c0):
        layers = [(wsb[0], src_t, src_c0, h1c), (wsb[1], h1c, 0, h2c),
                  (wsb[2], h2c, 0, None)]
        for li, (w, src, sc0, dst) in enumerate(layers):
            pe_start = V["s_pe"]

            @blk.tensor
            def _(t, w=w, src=src, sc0=sc0, li=li):
                for mt in range(NPT):
                    slot = P0[:, (mt % 2) * QC:(mt % 2) * QC + QC]
                    if mt == 0:
                        t.wait_ge(sems["s_dv"], V["s_dv"])
                        if li == 0:
                            t.wait_ge(sems["s_gp"], V["s_gp"])
                    if mt >= 2:
                        if li == 2:
                            t.wait_ge(sems["s_ac"], V["s_ac"] + mt - 1)
                        else:
                            t.wait_ge(sems["s_dv"], V["s_dv"] + mt - 1)
                    for kc in range(NPT):
                        i1 = t.matmul(slot, w[:, kc * K1 + mt * 128:kc * K1 + mt * 128 + 128],
                                      src[:, sc0 + kc * QC:sc0 + kc * QC + QC],
                                      start=(kc == 0), stop=(kc == NPT - 1))
                    inc(i1, "s_pe")

            if li < 2:
                @blk.vector
                def _(v, dst=dst, pe_start=pe_start):
                    for mt in range(NPT):
                        v.wait_ge(sems["s_pe"], pe_start + mt + 1)
                        slot = P0[:, (mt % 2) * QC:(mt % 2) * QC + QC]
                        i1 = v.tensor_copy(dst[:, mt * QC:mt * QC + QC], slot)
                        inc(i1, "s_dv")
            else:
                @blk.scalar
                def _(s, pe_start=pe_start):
                    for mt in range(NPT):
                        s.wait_ge(sems["s_pe"], pe_start + mt + 1)
                        s.wait_ge(sems["s_dv"], V["s_dv"] + mt)
                        slot = P0[:, (mt % 2) * QC:(mt % 2) * QC + QC]
                        i1 = s.activation(th_sb[:, :], slot, AF.Tanh,
                                          bias=fbias_sb[:, mt:mt + 1], scale=1.0)
                        inc(i1, "s_ac")

                @blk.vector
                def _(v, nch=nch):
                    for mt in range(NPT):
                        v.wait_ge(sems["s_ac"], V["s_ac"] - NPT + mt + 1)
                        dstr = resid[:, mt * TL + nch * QC:mt * TL + nch * QC + QC]
                        i1 = v.scalar_tensor_tensor(dstr, th_sb[:, :], 0.5, dstr,
                                                    ALU.mult, ALU.add)
                        inc(i1, "s_dv")

    blk = open_block()
    emit_ln_stats_all(blk)
    emit_ln_bcast(blk, 0)
    emit_ln_apply(blk, 0, xlnc, lambda pt: pt * QC)
    for nch in range(TL // QC):
        if nch + 1 < TL // QC:
            emit_ln_bcast(blk, nch + 1)
        emit_ffn_chunk(blk, nch, xlnc, (nch % 2) * NPT * QC)
        if nch + 1 < TL // QC:
            p = (nch + 1) % 2
            emit_ln_apply(blk, nch + 1, xlnc, lambda pt, p=p: p * NPT * QC + pt * QC)
    close_block()
    if nblocks <= 2:
        return _finish_early(nc, sems, V, sem_cms)
    blk = open_block()
    emit_ln_stats_all(blk)
    emit_ln_bcast(blk, 0)
    dma_ld_after = {}
    for nch in range(TL // QC):
        buf = lnstage if nch % 2 == 0 else lnstage2
        emit_ln_apply(blk, nch, buf, lambda pt: pt * QC,
                      ld_wait=(dma_ld_after[nch - 2] if nch >= 2 else None))
        if nch + 1 < TL // QC:
            emit_ln_bcast(blk, nch + 1)

        @blk.sync
        def _(sy, nch=nch, buf=buf):
            sy.wait_ge(sems["s_gp"], V["s_gp"])
            i1 = sy.dma_start(
                cin1[nch].ap().rearrange("(t p) c -> p t c", p=128),
                buf[:, :].rearrange("p (t c) -> p t c", c=QC))
            inc(i1, "ld", 16)
        dma_ld_after[nch] = V["ld"]
        cck_ld = V["ld"]

        @blk.gpsimd
        def _(g, cck_ld=cck_ld, nch=nch):
            g.wait_ge(sems["ld"], cck_ld)
            i1 = g.collective_compute(
                "AllGather", ALU.bypass,
                replica_groups=[[0, 1], [2, 3], [4, 5], [6, 7]],
                ins=[cin1[nch].ap().opt()], outs=[cout1[nch].ap().opt()])
            inc(i1, "cc1")

    @blk.sync
    def _(sy):
        sy.wait_ge(sems["s_dv"], V["s_dv"])
        i1 = sy.dma_start(rs_spill.ap(), resid[:, :])
        inc(i1, "ld", 16)

    close_block()
    if nblocks <= 3:
        return _finish_early(nc, sems, V, sem_cms)
    blk = open_block()

    @blk.gpsimd
    def _(g):
        g.dma_start(wq_sb[:, :].rearrange("p (t c) -> p t c", c=512),
                    wq_in.ap().rearrange("(t p) c -> p t c", p=128)).then_inc(sems["lg"], 16)
        V["lg"] += 16
        g.dma_start(wkv_sb[:, :].rearrange("p (t c) -> p t c", c=512),
                    wkv_in.ap().rearrange("(t p) c -> p t c", p=128)).then_inc(sems["lg"], 16)
        V["lg"] += 16

    xln_r = []
    for k in range(4):
        @blk.sync
        def _(sy, k=k):
            sy.wait_ge(sems["cc1"], V["cc1"] - 3 + k)
            for r in range(2):
                i1 = sy.dma_start(
                    xlnfull[:, :].rearrange("p (t c) -> p t c", c=TG)[
                        :, :, r * TL + k * QC:r * TL + k * QC + QC],
                    cout1[k].ap()[r * D:(r + 1) * D, :].rearrange("(t p) c -> p t c", p=128))
                inc(i1, "ld", 16)
        xln_r.append(V["ld"])

    @blk.vector
    def _(v):
        i1 = v.memset(vp_sb[:, :], 1.0)
        inc(i1, "s_dv")
    vpones = V["s_dv"]

    _kv_mts = [] if b3_parts < 2 else ([0, 1] if b3_parts < 3 else [0, 1, 2, 3])
    evac_dv = {}
    _nch_order = [0, 1, 4, 5, 2, 3, 6, 7]
    for mt in _kv_mts:
        for ni in range(NQC):
            nch = _nch_order[ni]
            pe_t = V["s_pe"]
            step = mt * NQC + ni

            @blk.tensor
            def _(t, mt=mt, nch=nch, ni=ni, step=step):
                if mt == 0 and ni == 0:
                    t.wait_ge(sems["lg"], V["lg"])
                if mt == 0 and ni in (0, 1, 4, 5):
                    t.wait_ge(sems["ld"], xln_r[{0: 0, 1: 1, 4: 2, 5: 3}[ni]])
                slot = P0[:, (nch % 2) * QC:(nch % 2) * QC + QC]
                if step >= 2:
                    t.wait_ge(sems["s_dv"], evac_dv[step - 2])
                for kc in range(NPT):
                    i1 = t.matmul(slot,
                                  wkv_sb[:, kc * 512 + mt * 128:kc * 512 + mt * 128 + 128],
                                  xlnfull[:, kc * TG + nch * QC:kc * TG + nch * QC + QC],
                                  start=(kc == 0), stop=(kc == NPT - 1))
                inc(i1, "s_pe")

            if mt < 2:
                @blk.vector
                def _(v, mt=mt, nch=nch, pe_t=pe_t):
                    v.wait_ge(sems["s_pe"], pe_t + 1)
                    slot = P0[:, (nch % 2) * QC:(nch % 2) * QC + QC]
                    i1 = v.tensor_copy(kf_sb[:, mt * TG + nch * QC:mt * TG + nch * QC + QC],
                                       slot)
                    inc(i1, "s_dv")
                evac_dv[step] = V["s_dv"]
            else:
                @blk.vector
                def _(v, mt=mt, nch=nch, pe_t=pe_t):
                    v.wait_ge(sems["s_pe"], pe_t + 1)
                    slot = P0[:, (nch % 2) * QC:(nch % 2) * QC + QC]
                    i1 = v.tensor_copy(vstage[:, :], slot)
                    inc(i1, "s_dv")
                evac_dv[step] = V["s_dv"]
                dv_t = V["s_dv"]

                @blk.tensor
                def _(t, mt=mt, nch=nch, dv_t=dv_t):
                    t.wait_ge(sems["s_dv"], dv_t)
                    if mt == 2 and nch == 0:
                        t.wait_ge(sems["s_dv"], vpones)
                    for sst in range(4):
                        i1 = t.transpose(PT[:, sst * 128:sst * 128 + 128],
                                         vstage[:, sst * 128:sst * 128 + 128],
                                         id64_sb[:, :])
                    inc(i1, "s_pe")

                @blk.vector
                def _(v, mt=mt, nch=nch):
                    v.wait_ge(sems["s_pe"], V["s_pe"])
                    for half in range(2):
                        g = (mt - 2) * 2 + half
                        st0 = nch * 4
                        i1 = v.tensor_copy(
                            vp_sb[:, :].rearrange("p (a b) -> p a b", b=VPS)[
                                :, g * NST + st0:g * NST + st0 + 4, 0:64],
                            PT[:, :].rearrange("p (a b) -> p a b", b=128)[
                                :, :, half * 64:half * 64 + 64])
                        inc(i1, "s_dv")

    qevac_dv = {}
    for mt in (range(4) if b3_parts >= 4 else []):
        for ni in range(NQC):
            nch = _nch_order[ni]
            pe_t = V["s_pe"]
            step = mt * NQC + ni

            @blk.tensor
            def _(t, mt=mt, nch=nch, step=step):
                slot = P1[:, (nch % 2) * QC:(nch % 2) * QC + QC]
                if step >= 2:
                    t.wait_ge(sems["s_dv"], qevac_dv[step - 2])
                for kc in range(NPT):
                    i1 = t.matmul(slot,
                                  wq_sb[:, kc * 512 + mt * 128:kc * 512 + mt * 128 + 128],
                                  xlnfull[:, kc * TG + nch * QC:kc * TG + nch * QC + QC],
                                  start=(kc == 0), stop=(kc == NPT - 1))
                inc(i1, "s_pe")

            @blk.vector
            def _(v, mt=mt, nch=nch, pe_t=pe_t):
                v.wait_ge(sems["s_pe"], pe_t + 1)
                slot = P1[:, (nch % 2) * QC:(nch % 2) * QC + QC]
                i1 = v.tensor_scalar_mul(q_sb[:, mt * TG + nch * QC:mt * TG + nch * QC + QC],
                                         slot, 0.125)
                inc(i1, "s_dv")
            qevac_dv[step] = V["s_dv"]

    close_block()
    PT_guard.__exit__(None, None, None)
    P4 = nc.alloc_psum_tensor("P4", [65, QC], F32)
    P5 = nc.alloc_psum_tensor("P5", [65, QC], F32)
    if nblocks <= 4:
        return _finish_early(nc, sems, V, sem_cms)
    blk = open_block()

    @blk.sync
    def _(sy):
        i1 = sy.dma_start(resid[:, :], rs_spill.ap())
        inc(i1, "ld", 16)
    resid_back = V["ld"]

    NGRP = NST

    def lin(pr, qcg, st):
        return (pr * 4 + qcg) * NST + st

    devents = []
    dix = {}
    if b4_parts <= 1:
        _pq = []
    elif b4_parts == 2:
        _pq = [(0, 0)]
    elif b4_parts == 3:
        _pq = [(0, 0), (0, 1)]
    elif b4_parts == 4:
        _pq = [(0, q) for q in range(4)]
    else:
        _pq = [(pr, q) for pr in range(4) for q in range(4)]
    for pr, qcg in _pq:
        for st in range(NST):
            for hx in range(2):
                for j in range(2):
                    if _chunk_class(st, qcg * 2 + j) == "D":
                        dix[(pr, qcg, st, hx)] = dix.get((pr, qcg, st, hx))
                        dix[(pr, qcg, st, hx, j)] = len(devents)
                        devents.append((pr, qcg, st, hx, j))

    qk0 = V["s_qk"]
    ex0 = V["s_ex"]
    pv0 = V["s_pv"]
    st0 = V["s_st"]
    r0 = V["s_r"]
    ao0 = V["s_ao"]
    ld0 = V["ld"]
    pe0 = V["s_pe"]
    lnl0 = V["s_lnl"]

    def _exp_of_event(e):
        pe_, pq_, ps_, ph_, pj_ = devents[e]
        return ex0 + 2 * lin(pe_, pq_, ps_) + 1 + ph_

    for pr, qcg in _pq:
        G = pr * 4 + qcg
        L0 = G * NST
        heads = (2 * pr, 2 * pr + 1)
        QKP = (P0, P1)
        PVP = ((P2, P3), (P4, P5))

        @blk.tensor
        def _(t, pr=pr, qcg=qcg, G=G, L0=L0, heads=heads):
            for st in range(NST):
                L = L0 + st
                acls = [_chunk_class(st, qcg * 2 + j) for j in range(2)]
                if L >= 1:
                    t.wait_ge(sems["s_ex"], ex0 + 2 * (L - 1) + 2)
                last = [None, None]
                for j in range(2):
                    for hx in range(2):
                        l = heads[hx]
                        band = (l % 2) * 64
                        g = l % 4
                        lhs = kf_sb[(g % 2) * 64:(g % 2) * 64 + 64,
                                    (g // 2) * TG + st * 128:(g // 2) * TG + st * 128 + 128]
                        rhs = q_sb[band:band + 64,
                                   (l // 2) * TG + (qcg * 2 + j) * QC:
                                   (l // 2) * TG + (qcg * 2 + j) * QC + QC]
                        last[hx] = t.matmul(QKP[hx][:, j * QC:j * QC + QC], lhs, rhs,
                                            start=True, stop=(acls[j] != "A"),
                                            tile_position=(band, 0))
                for hx in range(2):
                    l = heads[hx]
                    aband = 64 - (l % 2) * 64
                    for j in range(2):
                        if acls[j] != "A":
                            continue
                        qc = qcg * 2 + j
                        last[hx] = t.matmul(
                            QKP[hx][:, j * QC:j * QC + QC],
                            augw_sb[aband:aband + 5, l * 128:l * 128 + 128],
                            augq_sb[aband:aband + 5, qc * QC:qc * QC + QC],
                            start=False, stop=True, tile_position=(aband, 0))
                last[0].then_inc(sems["s_qk"], 1)
                last[1].then_inc(sems["s_qk"], 1)
                if st >= 1:
                    for hx in range(2):
                        l = heads[hx]
                        g = l % 4
                        vcol = (g * NST + (st - 1)) * VPS
                        for j in range(2):
                            pvp = PVP[hx][j]
                            if st == 1 and G >= 1:
                                t.wait_ge(sems["s_ao"],
                                          ao0 + 4 * (G - 1) + 2 * hx + j + 1)
                            i1 = t.matmul(
                                pvp[:, :], vp_sb[:, vcol:vcol + 65],
                                pt_sb[:, ((st - 1) % 2) * 2048 + hx * 1024 + j * QC:
                                      ((st - 1) % 2) * 2048 + hx * 1024 + j * QC + QC],
                                start=(st - 1 == 0), stop=False)
                        i1.then_inc(sems["s_pv"], 1)
            for hx in range(2):
                l = heads[hx]
                g = l % 4
                t.wait_ge(sems["s_ex"], ex0 + 2 * (L0 + NST - 1) + 1 + hx)
                vcol = (g * NST + NST - 1) * VPS
                for j in range(2):
                    pvp = PVP[hx][j]
                    i1 = t.matmul(
                        pvp[:, :], vp_sb[:, vcol:vcol + 65],
                        pt_sb[:, ((NST - 1) % 2) * 2048 + hx * 1024 + j * QC:
                              ((NST - 1) % 2) * 2048 + hx * 1024 + j * QC + QC],
                        start=False, stop=True)
                i1.then_inc(sems["s_pv"], 1)
            for hx in range(2):
                t.wait_ge(sems["s_r"], r0 + 2 * G + 1 + hx)
                pva, pvb = PVP[hx]
                t.matmul(pva[0:64, :], ones1_sb[0:1, 0:64], rr_sb[0:1, 0:QC],
                         start=True, stop=True)
                i1 = t.matmul(pvb[0:64, :], ones1_sb[0:1, 0:64],
                              rr_sb[0:1, QC:2 * QC], start=True, stop=True)
                i1.then_inc(sems["s_pe"], 1)
        V["s_qk"] += 2 * NST
        V["s_pv"] += 2 * NST
        V["s_pe"] += 2

        @blk.vector
        def _(v, pr=pr, qcg=qcg, G=G, L0=L0, heads=heads):
            for st in range(NST):
                L = L0 + st
                for hx in range(2):
                    l = heads[hx]
                    qk = QKP[hx]
                    for j in range(2):
                        qc = qcg * 2 + j
                        if _chunk_class(st, qc) != "D":
                            continue
                        e = dix[(pr, qcg, st, hx, j)]
                        v.wait_ge(sems["s_qk"], qk0 + 2 * L + 1 + hx)
                        if e >= 2:
                            v.wait_ge(sems["s_ex"], _exp_of_event(e - 2))
                        d0 = st * 128 - qc * QC
                        i1 = v.scalar_tensor_tensor(
                            dsc[:, (e % 2) * QC:(e % 2) * QC + QC],
                            dtab_sb[:, (d0 // 128) * QC:(d0 // 128) * QC + QC],
                            slop_sb[:, l:l + 1],
                            qk[:, j * QC:j * QC + QC],
                            ALU.mult, ALU.add)
                        inc(i1, "s_st")
            for hx in range(2):
                pva, pvb = PVP[hx]
                v.wait_ge(sems["s_pv"], pv0 + 2 * (L0 + NST - 1) + 1 + hx)
                if G >= 1:
                    v.wait_ge(sems["s_r"], r0 + 2 * (G - 1) + 1 + hx)
                v.tensor_copy(pvt[:, hx * 2 * QC:hx * 2 * QC + QC], pva[:, :])
                i1 = v.tensor_copy(pvt[:, (2 * hx + 1) * QC:(2 * hx + 1) * QC + QC],
                                   pvb[:, :])
                inc(i1, "s_lnl")
                v.wait_ge(sems["s_r"], r0 + 2 * G + 1 + hx)
                v.wait_ge(sems["s_pe"], pe0 + 2 * G + 1 + hx)
                if G >= 1:
                    v.wait_ge(sems["ld"], ld0 + 32 * G)
                v.tensor_copy(rbs[:, :], pva[0:64, :])
                v.scalar_tensor_tensor(aost[:, hx * 2 * QC:hx * 2 * QC + QC],
                                       pvt[0:64, hx * 2 * QC:hx * 2 * QC + QC],
                                       1.0, rbs[:, :],
                                       ALU.mult, ALU.mult).then_inc(sems["s_ao"], 1)
                v.tensor_copy(rbs[:, :], pvb[0:64, :])
                v.scalar_tensor_tensor(aost[:, (hx * 2 + 1) * QC:(hx * 2 + 1) * QC + QC],
                                       pvt[0:64, (2 * hx + 1) * QC:(2 * hx + 1) * QC + QC],
                                       1.0, rbs[:, :],
                                       ALU.mult, ALU.mult).then_inc(sems["s_ao"], 1)
        V["s_st"] = st0 + len([1 for e in devents if (e[0] * 4 + e[1]) <= G])
        V["s_ao"] += 4
        V["s_lnl"] += 2

        V["s_ex"] += 2 * NST

        @blk.sync
        def _(sy, pr=pr, qcg=qcg, G=G):
            half = qcg // 2
            colo = (qcg % 2) * 1024
            for hx in range(2):
                l = 2 * pr + hx
                if l < 4:
                    snd, row = snd2a, half * 256 + l * 64
                elif l < 6:
                    snd, row = snd2b, half * 128 + (l - 4) * 64
                else:
                    snd, row = snd2c, half * 128 + (l - 6) * 64
                sy.wait_ge(sems["s_ao"], ao0 + 4 * G + 2 * hx + 2)
                i1 = sy.dma_start(
                    snd.ap()[row:row + 64, colo:colo + 1024],
                    aost[:, hx * 2 * QC:hx * 2 * QC + 2 * QC])
                inc(i1, "ld", 16)

        if G == 7:
            @blk.gpsimd
            def _(g):
                if b4_parts >= 8:
                    g.wait_ge(sems["ld"], V["ld"])
                    i1 = g.collective_compute(
                        "AllGather", ALU.bypass,
                        replica_groups=[[0, 1], [2, 3], [4, 5], [6, 7]],
                        ins=[snd2a.ap().opt()], outs=[rcv2a.ap().opt()])
                    inc(i1, "cc2")
        if G == 11:
            @blk.gpsimd
            def _(g):
                if b4_parts >= 8:
                    g.wait_ge(sems["ld"], V["ld"])
                    i1 = g.collective_compute(
                        "AllGather", ALU.bypass,
                        replica_groups=[[0, 1], [2, 3], [4, 5], [6, 7]],
                        ins=[snd2b.ap().opt()], outs=[rcv2b.ap().opt()])
                    inc(i1, "cc2")

    @blk.scalar
    def _(s):
        for pr, qcg in _pq:
            G = pr * 4 + qcg
            L0 = G * NST
            heads = (2 * pr, 2 * pr + 1)
            for st in range(NST):
                L = L0 + st
                c0 = _chunk_class(st, qcg * 2)
                c1 = _chunk_class(st, qcg * 2 + 1)
                if c0 == "B" and c1 == "B":
                    s.wait_ge(sems["s_qk"], qk0 + 2 * L + 2)
                    if L >= 2:
                        s.wait_ge(sems["s_pv"], pv0 + 2 * (L - 2) + 2)
                    pb = (st % 2) * 2048
                    s.activation(pt_sb[:, pb:pb + 2048], P01[:, :], AF.Exp,
                                 bias=negc_sb[:, 0:1], scale=1.0).then_inc(sems["s_ex"], 2)
                    continue
                for hx in range(2):
                    l = heads[hx]
                    qk = (P0, P1)[hx]
                    s.wait_ge(sems["s_qk"], qk0 + 2 * L + 1 + hx)
                    if L >= 2:
                        s.wait_ge(sems["s_pv"], pv0 + 2 * (L - 2) + 1 + hx)
                    pbase = (st % 2) * 2048 + hx * 1024
                    pdst = pt_sb[:, pbase:pbase + 1024]
                    if c0 == c1 and c0 != "D":
                        if c0 == "A":
                            bias = biasc_sb[:, l * NST + st:l * NST + st + 1]
                        else:
                            bias = negc_sb[:, 0:1]
                        i1 = s.activation(pdst, qk[:, :], AF.Exp, bias=bias, scale=1.0)
                    else:
                        for j in range(2):
                            qc = qcg * 2 + j
                            cls = _chunk_class(st, qc)
                            dd = pt_sb[:, pbase + j * QC:pbase + j * QC + QC]
                            if cls == "D":
                                e = dix[(pr, qcg, st, hx, j)]
                                s.wait_ge(sems["s_st"], st0 + e + 1)
                                i1 = s.activation(dd, dsc[:, (e % 2) * QC:(e % 2) * QC + QC],
                                                  AF.Exp, bias=negc_sb[:, 0:1], scale=1.0)
                            elif cls == "A":
                                i1 = s.activation(dd, qk[:, j * QC:j * QC + QC], AF.Exp,
                                                  bias=biasc_sb[:, l * NST + st:l * NST + st + 1],
                                                  scale=1.0)
                            else:
                                i1 = s.activation(dd, qk[:, j * QC:j * QC + QC], AF.Exp,
                                                  bias=negc_sb[:, 0:1], scale=1.0)
                    i1.then_inc(sems["s_ex"], 1)
            for hx in range(2):
                s.wait_ge(sems["s_lnl"], lnl0 + 2 * G + 1 + hx)
                s.wait_ge(sems["s_pe"], pe0 + 2 * G + hx)
                s.activation(lnt_att[:, :], pvt[64:65, hx * 2 * QC:hx * 2 * QC + 2 * QC],
                             AF.Ln, bias=0.0, scale=1.0)
                s.activation(rr_sb[:, :], lnt_att[:, :], AF.Exp,
                             bias=0.0, scale=-1.0).then_inc(sems["s_r"], 1)

    V["s_r"] += 2 * len(_pq)

    @blk.gpsimd
    def _(g):
        if b4_parts < 8:
            return
        g.wait_ge(sems["ld"], V["ld"])
        i1 = g.collective_compute(
            "AllGather", ALU.bypass,
            replica_groups=[[0, 1], [2, 3], [4, 5], [6, 7]],
            ins=[snd2c.ap().opt()], outs=[rcv2c.ap().opt()])
        inc(i1, "cc2")

    close_block()
    if nblocks <= 5:
        return _finish_early(nc, sems, V, sem_cms)
    blk = open_block()

    @blk.gpsimd
    def _(g):
        i1 = g.dma_start(wpj_sb[:, :].rearrange("p (t c) -> p t c", c=K1),
                         wproj_in.ap().rearrange("(t p) c -> p t c", p=128))
        inc(i1, "lg", 16)

    _pmap = {0: (rcv2a, 0, 256), 1: (rcv2a, 128, 384),
             4: (rcv2a, 512, 768), 5: (rcv2a, 640, 896),
             2: (rcv2b, 0, 128), 6: (rcv2b, 256, 384),
             3: (rcv2c, 0, 128), 7: (rcv2c, 256, 384)}
    _ptorder = [0, 1, 4, 5, 2, 6, 3, 7]

    @blk.sync
    def _(sy):
        for i, pt in enumerate(_ptorder):
            if i == 0:
                sy.wait_ge(sems["cc2"], V["cc2"] - 2)
            if i == 4:
                sy.wait_ge(sems["cc2"], V["cc2"] - 1)
            if i == 6:
                sy.wait_ge(sems["cc2"], V["cc2"])
            rcv, r0, r1 = _pmap[pt]
            sy.dma_start(aoc0[:, :], rcv.ap()[r0:r0 + 128, :]).then_inc(sems["ld"], 16)
            V["ld"] += 16
            sy.dma_start(aoc1[:, :], rcv.ap()[r1:r1 + 128, :]).then_inc(sems["ld"], 16)
            V["ld"] += 16
            sy.wait_ge(sems["s_dv"], V["s_dv"] + i + 1)

    @blk.vector
    def _(v):
        base_ld = V["ld"] - 32 * NPT
        for i, pt in enumerate(_ptorder):
            v.wait_ge(sems["ld"], base_ld + (i + 1) * 32)
            v.tensor_scalar_mul(aot2[:, :], aoc0[:, :], sel_sb[:, 0:1])
            i1 = v.scalar_tensor_tensor(ao_sb[:, pt * TL:pt * TL + TL],
                                        aoc1[:, :], sel_sb[:, 1:2], aot2[:, :],
                                        ALU.mult, ALU.add)
            inc(i1, "s_dv")

    ao_ready_dv = V["s_dv"]
    wpj_ready = V["lg"]

    for mt in range(NPT):
        for nch in range(TL // QC):
            pe_t = V["s_pe"]

            @blk.tensor
            def _(t, mt=mt, nch=nch):
                if mt == 0 and nch == 0:
                    t.wait_ge(sems["s_dv"], ao_ready_dv)
                    t.wait_ge(sems["lg"], wpj_ready)
                slot = P0[:, (nch % 2) * QC:(nch % 2) * QC + QC]
                if mt * (TL // QC) + nch >= 2:
                    t.wait_ge(sems["s_dv"], V["s_dv"] - 1)
                for kc in range(NPT):
                    i1 = t.matmul(slot,
                                  wpj_sb[:, kc * K1 + mt * 128:kc * K1 + mt * 128 + 128],
                                  ao_sb[:, kc * TL + nch * QC:kc * TL + nch * QC + QC],
                                  start=(kc == 0), stop=(kc == NPT - 1))
                inc(i1, "s_pe")

            @blk.vector
            def _(v, mt=mt, nch=nch, pe_t=pe_t):
                v.wait_ge(sems["s_pe"], pe_t + 1)
                if mt == 0 and nch == 0:
                    v.wait_ge(sems["ld"], resid_back)
                slot = P0[:, (nch % 2) * QC:(nch % 2) * QC + QC]
                dstr = resid[:, mt * TL + nch * QC:mt * TL + nch * QC + QC]
                i1 = v.scalar_tensor_tensor(dstr, slot, bproj_sb[:, mt:mt + 1],
                                            dstr, ALU.add, ALU.add)
                inc(i1, "s_dv")

    close_block()
    if nblocks <= 6:
        return _finish_early(nc, sems, V, sem_cms)
    blk = open_block()

    @blk.gpsimd
    def _(g):
        for i in range(3):
            i1 = g.dma_start(wsb[i][:, :].rearrange("p (t c) -> p t c", c=K1),
                             w_ffn[i].ap().rearrange("(t p) c -> p t c", p=128))
            inc(i1, "lg", 16)
    w_ready = V["lg"]

    emit_ln_stats_all(blk)
    emit_ln_bcast(blk, 0)
    emit_ln_apply(blk, 0, xlnc, lambda pt: pt * QC)

    @blk.tensor
    def _(t):
        t.wait_ge(sems["lg"], w_ready)
    for nch in range(TL // QC):
        if nch + 1 < TL // QC:
            emit_ln_bcast(blk, nch + 1)
        emit_ffn_chunk(blk, nch, xlnc, (nch % 2) * NPT * QC)
        if nch + 1 < TL // QC:
            p = (nch + 1) % 2
            emit_ln_apply(blk, nch + 1, xlnc, lambda pt, p=p: p * NPT * QC + pt * QC)

        @blk.sync
        def _(sy, nch=nch):
            sy.wait_ge(sems["s_dv"], V["s_dv"])
            i1 = sy.dma_start(
                y_out.ap().rearrange("(t p) c -> p t c", p=128)[:, :, nch * QC:nch * QC + QC],
                resid[:, :].rearrange("p (t c) -> p t c", c=TL)[:, :, nch * QC:nch * QC + QC])
            inc(i1, "ld", 16)

    @blk.sync
    def _(sy):
        sy.wait_ge(sems["ld"], V["ld"])

    close_block()
    if nblocks <= 7:
        return _finish_early(nc, sems, V, sem_cms)
    for cm in reversed(sem_cms):
        cm.__exit__(None, None, None)
    return nc


def _core_tables(parity):
    start = 2.0 ** (-2.0 ** (-(np.log2(16) - 3.0)))
    slopes = start ** np.arange(1, 17, dtype=np.float64)
    gh = np.arange(parity * 8, parity * 8 + 8)
    sl = slopes[gh].astype(np.float32)

    augw = np.zeros((5, NHL * 128), np.float32)
    for l in range(NHL):
        s = float(sl[l])
        s_hi = float(BF(s))
        s_lo = s - s_hi
        m = np.arange(128, dtype=np.float64)
        mv = (s * m / 8.0).astype(np.float32)
        m_hi = BF(mv)
        m_lo = (mv - m_hi).astype(np.float32)
        augw[0, l * 128:(l + 1) * 128] = s_hi
        augw[1, l * 128:(l + 1) * 128] = s_hi
        augw[2, l * 128:(l + 1) * 128] = s_lo
        augw[3, l * 128:(l + 1) * 128] = m_hi
        augw[4, l * 128:(l + 1) * 128] = m_lo

    biasc = np.zeros((128, NHL * NST), np.float32)
    for l in range(NHL):
        for st in range(NST):
            biasc[:, l * NST + st] = sl[l] * (st * 128) / 8.0 - CSHIFT

    slop = np.tile(sl[None, :], (128, 1)).astype(np.float32)
    return augw, biasc, slop


def _shared_tables():
    q = np.arange(TG, dtype=np.float64) / 8.0
    qh = BF(-q)
    ql = (-q - qh).astype(np.float32)
    assert np.all(BF(ql) == ql), "q_lo must be bf16-exact"
    augq = np.zeros((5, TG), np.float32)
    augq[0] = qh
    augq[1] = ql
    augq[2] = qh
    augq[3] = 1.0
    augq[4] = 1.0

    dtab = np.zeros((128, 4 * QC), np.float32)
    for jb in range(4):
        d0 = jb * 128
        p = np.arange(128)[:, None]
        f = np.arange(QC)[None, :]
        dd = d0 + p - f
        dtab[:, jb * QC:(jb + 1) * QC] = np.where(dd <= 0, dd / 8.0, 0.0)
    return augq, dtab


_NC_CACHE = {}


def kernel(L0_attn_x, L0_attn_Wq, L0_attn_Wkv, L0_attn_Wproj, L0_attn_bproj,
           L0_ffn_weight, L0_ffn_aux_weight1, L0_ffn_aux_weight2, L0_ffn_bias):
    if "nc" not in _NC_CACHE:
        _NC_CACHE["nc"] = build_program()
    nc = _NC_CACHE["nc"]

    x = np.asarray(L0_attn_x, np.float32)
    Wq = np.asarray(L0_attn_Wq, np.float32)
    Wkv = np.asarray(L0_attn_Wkv, np.float32)
    Wproj = np.asarray(L0_attn_Wproj, np.float32)
    bproj = np.asarray(L0_attn_bproj, np.float32)
    Wf = np.asarray(L0_ffn_weight, np.float32)
    A1 = np.asarray(L0_ffn_aux_weight1, np.float32)
    A2 = np.asarray(L0_ffn_aux_weight2, np.float32)
    fb = np.asarray(L0_ffn_bias, np.float32)

    wq_t = np.ascontiguousarray(Wq.T)
    wkv_t = np.ascontiguousarray(Wkv.T)
    wproj_t = np.ascontiguousarray(Wproj.T)
    bproj_c = np.ascontiguousarray(bproj.reshape(NPT, 128).T)
    fb_c = np.ascontiguousarray(fb.reshape(NPT, 128).T)
    onek = np.full((128, 1), 1.0 / 1024.0, np.float32)
    negc = np.full((128, 1), -CSHIFT, np.float32)
    sels = [np.tile(np.array([[1.0, 0.0]], np.float32), (128, 1)),
            np.tile(np.array([[0.0, 1.0]], np.float32), (128, 1))]
    ones1 = np.ones((1, 128), np.float32)
    id64 = np.eye(128, dtype=np.float32)
    augq, dtab = _shared_tables()
    tabs = [_core_tables(0), _core_tables(1)]

    in_maps = []
    for c in range(8):
        b, half = c // 2, c % 2
        t0 = half * TL
        augw, biasc, slop = tabs[half]
        wq_c = np.ascontiguousarray(wq_t[:, half * 512:half * 512 + 512])
        in_maps.append({
            "x": np.ascontiguousarray(x[b, t0:t0 + TL, :].T),
            "wffn0": Wf, "wffn1": A1, "wffn2": A2,
            "wq": wq_c, "wkv": wkv_t, "wproj": wproj_t,
            "bproj": bproj_c, "fbias": fb_c,
            "augw": augw, "augq": augq, "biasc": biasc, "dtab": dtab,
            "slop": slop, "onek": onek, "ones1": ones1, "id64": id64,
            "negc": negc, "sel": sels[half],
        })

    res = run_bass_kernel_spmd(nc, in_maps, core_ids=list(range(8)))
    out = np.empty((4, TG, D), np.float32)
    for c in range(8):
        b, half = c // 2, c % 2
        t0 = half * TL
        out[b, t0:t0 + TL, :] = res.results[c]["y"].T
    return out
